# version 10
# baseline (speedup 1.0000x reference)
import numpy as np
import ml_dtypes
import concourse.bass as bass
import concourse.mybir as mybir
from concourse.bass_utils import run_bass_kernel_spmd

F32 = mybir.dt.float32
BF16 = mybir.dt.bfloat16
AF = mybir.ActivationFunctionType
ALU = mybir.AluOpType
AX = mybir.AxisListType.X

B, S, D = 2, 8192, 1024
H, DH, PF = 16, 64, 256
HS, DHR = 64, 32
DK = HS + DHR
DFF = 4096
DCH = 4096
TLOC = 2048
NT = 16
NG = 4
GT = 64
ISQ = float(1.0 / np.sqrt(np.float32(DK)))
EPS_LN = 1e-5
EPS_A = 1e-8
EPS_Q = 1e-5

ROW_K = 0
ROW_KR = 16384
ROW_V = 16896
ROW_Q = 33536
XCON_ROWS = 58112


class KB:
    ENGS = ("pe", "act", "dve", "pool", "sp")

    def __init__(self, stage=None):
        self.stage = stage
        nc = bass.Bass("TRN2", target_bir_lowering=False)
        self.nc = nc
        self.cmds = {e: [] for e in self.ENGS}
        sem_names = ["pe", "act", "dve", "pool",
                     "sx", "sx2", "sw1", "sw2a", "sw2b", "sw2c", "sw2d", "satt", "aatt", "srow",
                     "sr0", "sr1", "sog", "sstg", "svst", "skv", "sscr", "sxo",
                     "wp0", "wp1", "wp2", "wp3", "wg", "gm",
                     "cc0", "cc1", "cc2", "cc3"]
        self.cnt = {s: 0 for s in sem_names}
        self.sem_names = sem_names
        self.last_wait = {}
        self.cg = {}
        self._bw = {}
        self._br = {}

        dt = nc.dram_tensor
        self.xin = dt("xin", [TLOC, D], F32, kind="ExternalInput").ap()
        self.xout = dt("xout", [TLOC, D], F32, kind="ExternalOutput").ap()
        w = {}
        for name, shape in [
            ("aWq", [D, D]), ("aWk", [D, D]), ("aWv", [D, D]), ("aproj", [DH, PF]),
            ("aWo", [D, D]), ("abo", [D]),
            ("bWDQ", [D, DCH]), ("bWUQ", [DCH, D]), ("bWDKV", [D, PF]),
            ("bWUK", [PF, D]), ("bWUV", [PF, D]), ("bWQR", [DCH, H * DHR]),
            ("bWKR", [D, DHR]), ("bWO", [D, D]),
            ("fW1", [DFF, D]), ("fb1", [DFF]), ("fW2", [D, DFF]), ("fb2", [D]),
        ]:
            w[name] = dt(name, shape, F32, kind="ExternalInput").ap()
        self.w = w
        self.identi = dt("identi", [128, 128], F32, kind="ExternalInput").ap()
        self.causali = dt("causali", [128, 128], BF16, kind="ExternalInput").ap()
        self.ropei = dt("ropei", [H * DHR, TLOC], BF16, kind="ExternalInput").ap()

        self.wq1s = dt("wq1s", [DFF, D], BF16)
        self.wq2s = dt("wq2s", [D, DFF], BF16)
        self.acon_h = dt("acon", [H * PF, 65], F32)
        self.ag_h = dt("ag", [4 * H * PF, 65], F32)
        self.kvcon_h = dt("kvcon", [33536, 128], BF16)
        self.kvg_h = dt("kvg", [4 * 33536, 128], BF16)
        self.qcon_h = dt("qcon", [24576, 128], BF16)
        self.qg2_h = dt("qg2", [4 * 24576, 128], BF16)
        self.ocon_h = dt("ocon", [4 * GT, 8192], BF16)
        self.og_h = dt("og", [16 * GT, 8192], BF16)
        self.wdq_bf = dt("wdq_bf", [D, DCH], BF16)
        self.wuq_bf = dt("wuq_bf", [DCH, D], BF16)
        self.wqr_bf = dt("wqr_bf", [DCH, H * DHR], BF16)
        self.kscr_h = dt("kscr", [4 * 1024, 128], BF16)
        self.vscr_h = dt("vscr", [4 * 1040, 128], BF16)
        self.qscr_h = dt("qscr", [4 * 1536, 128], BF16)
        self.oscr_h = dt("oscr", [4 * GT, 8192], BF16)

        from contextlib import ExitStack
        self.es = ExitStack()
        sb = lambda n, shp, d: self.es.enter_context(nc.sbuf_tensor(n, shp, d))
        self.x = sb("x_fm", [128, 8 * TLOC], F32)
        self.y = sb("y_fm", [128, 8 * TLOC], BF16)
        self.big1 = sb("big1", [128, 16384], BF16)
        self.big2 = sb("big2", [128, 16640], BF16)
        self.warena = sb("warena", [128, 8192], BF16)
        self.fstage = sb("fstage", [128, 2048], F32)
        self.misc2 = sb("misc2", [128, 1536], F32)
        self.ident = sb("ident", [128, 128], F32)
        self.causal = sb("causal", [128, 128], BF16)
        self.onesb = sb("onesb", [128, 128], BF16)
        self.onesf = sb("onesf", [1, 128], F32)
        self.ones128f = sb("ones128f", [128, 128], F32)
        self.projb = sb("projb", [128, PF], BF16)
        self.ptb = sb("ptb", [128, 2048], BF16)
        self.stg = sb("stg", [128, 1024], BF16)
        self.o2b = sb("o2b", [64, 512], BF16)
        self.kvb = sb("kvb", [128, 2 * H * 65], BF16)
        self.vstg = self.kvb
        self.ropeb = sb("ropeb", [128, 1024], BF16)
        self.constv = sb("constv", [128, 8], F32)
        nc.const_aps.register(F32, EPS_LN, self.constv[:, 0:1])
        nc.const_aps.register(F32, -0.5, self.constv[:, 1:2])
        nc.const_aps.register(F32, 0.5, self.constv[:, 2:3])
        self.pb = [self.es.enter_context(nc.psum_tensor(f"pb{i}", [128, 512], F32))
                   for i in range(4)]
        self.pbs = self.es.enter_context(nc.psum_tensor("pbs", [128, 2048], F32))
        self.pb += [self.pbs[:, k * 512:(k + 1) * 512] for k in range(4)]
        self.sems = {s: self.es.enter_context(nc.semaphore(s)) for s in sem_names}

        self.COL_ROW = 0
        self.COL_KV = 1024
        self.COL_B1 = 1160
        self.COL_B2 = 1192
        self.COL_BO = 1200
        self.COL_AR = 1208
        self.COL_G = 1240
        self.COL_NC = 1246

    def _wait(self, eng, sem, val):
        if val is None or val <= 0:
            return
        key = (eng, sem)
        if self.last_wait.get(key, 0) >= val:
            return
        self.last_wait[key] = val
        self.cmds[eng].append(("w", sem, val))

    def op(self, eng, fn, wait=(), inc=None, n=1):
        for m in wait:
            if m is None:
                continue
            s, v = m
            self._wait(eng, s, v)
        if inc is None:
            self.cmds[eng].append(("i", fn, None, 0))
            return None
        self.cnt[inc] += n
        ms = (inc, self.cnt[inc])
        self.cmds[eng].append(("i", fn, inc, n))
        return ms

    def ms(self, sem):
        return (sem, self.cnt[sem])

    def dma(self, eng, out, in_, sem, wait=(), **kw):
        return self.op(eng, lambda e, o=out, i=in_, k=dict(kw): e.dma_start(out=o, in_=i, **k),
                       wait=wait, inc=sem, n=16)

    def dmat(self, eng, out, in_, sem, wait=()):
        return self.op(eng, lambda e, o=out, i=in_: e.dma_start_transpose(o, i),
                       wait=wait, inc=sem, n=16)

    def mm(self, out, lhsT, rhs, start, stop, wait=(), inc=False):
        return self.op("pe",
                       lambda e, o=out, l=lhsT, r=rhs, st=start, sp=stop:
                       e.matmul(o, l, r, start=st, stop=sp),
                       wait=wait, inc=("pe" if inc else None))

    def bank_written(self, i, ms):
        self._bw[i] = ms

    def bank_read(self, i, ms):
        self._br[i] = ms

    def bfree(self, i):
        return self._br.get(i)

    def bready(self, i):
        return self._bw.get(i)

    def build(self):
        self.emit_all()
        nc = self.nc
        with nc.Block() as block:
            def runner(eng_name):
                def run(e):
                    if eng_name in ("sp", "act", "pool"):
                        cid = e.partition_id()
                        self.cg[eng_name] = cid % 4
                    for cmd in self.cmds[eng_name]:
                        if cmd[0] == "w":
                            e.wait_ge(self.sems[cmd[1]], cmd[2])
                        else:
                            _, fn, inc, n = cmd
                            ins = fn(e)
                            if inc is not None:
                                ins.then_inc(self.sems[inc], n)
                return run
            block.tensor(runner("pe"))
            block.scalar(runner("act"))
            block.vector(runner("dve"))
            block.gpsimd(runner("pool"))
            block.sync(runner("sp"))
        return nc

    def emit_all(self):
        st = self.stage
        if st == "wq1":
            self.p_init()
            self.p_quantize()
            m = self.dma("pool", self.xout, self.wq1s.ap()[0:2048, :], "gm",
                         wait=[self.ms_wq_scr])
            self._wait("pool", m[0], m[1])
            return
        self.p_init()
        self.p_load_x()
        if st == "x0":
            self.p_store(); return
        self.emit_ln()
        if st == "ln1":
            self.p_store_y(); return
        self.p_attn_a()
        if st in ("kv", "aout", "acon", "vtm", "kfm"):
            return
        if st == "x1":
            self.p_store(); return
        self.emit_ln()
        self.p_ffn()
        if st == "x2":
            self.p_store(); return
        self.emit_ln()
        self.p_attn_b()
        if st == "x3":
            self.p_store(); return
        self.emit_ln()
        self.p_ffn()
        self.p_store()

    def p_init(self):
        self.dma("sp", self.ident[:], self.identi, "srow")
        self.dma("sp", self.causal[:], self.causali, "srow")
        self.op("pool", lambda e: e.memset(self.constv[:, 0:1], EPS_LN))
        self.op("pool", lambda e: e.memset(self.constv[:, 1:2], -0.5))
        self.op("pool", lambda e: e.memset(self.constv[:, 2:3], 0.5))
        self.op("pool", lambda e: e.memset(self.onesb[:], 1.0))
        self.op("pool", lambda e: e.memset(self.onesf[:], 1.0))
        self.op("pool", lambda e: e.memset(self.ones128f[:], 1.0), inc="pool")
        self.ms_ones = self.ms("pool")
        self.dma("pool", self.projb[0:64, :], self.w["aproj"], "gm")
        self.dma("pool", self.projb[64:128, :], self.w["aproj"], "gm")
        self.ms_proj = self.ms("gm")

    def p_load_x(self):
        x3 = self.xin.rearrange("(t p) d -> t p d", p=128)
        ms_fs_free = [None, None]
        msd = None
        for ti in range(NT):
            half = ti % 2
            fs = self.fstage[:, half * 1024:(half + 1) * 1024]
            msl = self.dma("sp", fs, x3[ti, :, :], "sx" if half == 0 else "sx2",
                           wait=[ms_fs_free[half]] if ms_fs_free[half] else [])
            for d in range(8):
                bk = 4 + (d % 4)
                fsl = fs[:, d * 128:(d + 1) * 128]
                msmm = self.op("pe", lambda e, fsl=fsl, bk=bk:
                               e.transpose(self.pb[bk][:, 0:128], fsl, self.ident[:]),
                               wait=[msl, ("srow", 32), self.bfree(bk)], inc="pe")
                self.bank_written(bk, msmm)
                dst = self.x[:, d * TLOC + ti * 128: d * TLOC + (ti + 1) * 128]
                msd = self.op("dve", lambda e, dst=dst, bk=bk:
                              e.tensor_copy(dst, self.pb[bk][:, 0:128]),
                              wait=[msmm], inc="dve")
                self.bank_read(bk, msd)
                if d == 7:
                    ms_fs_free[half] = msmm

    def emit_ln(self):
        ms_x = self.ms("dve")
        ms_sx0 = self.ms("sx")
        ms_sx20 = self.ms("sx2")
        for g in range(NG):
            xb = self.big1[:, 0:4096]
            xb2 = self.big1[:, 4096:8192]
            msc = None
            for d in range(8):
                src = self.x[:, d * TLOC + g * 512: d * TLOC + (g + 1) * 512]
                dst = xb[:, d * 512:(d + 1) * 512]
                msc = self.op("act", lambda e, src=src, dst=dst:
                              e.activation(out=dst, in_=src, func=AF.Copy),
                              wait=[ms_x, self.ms("pe")], inc="act")
            mss = self.op("dve", lambda e, xb=xb, xb2=xb2:
                          e.tensor_mul(xb2, xb, xb), wait=[msc], inc="dve")
            m1 = None
            for d in range(8):
                m1 = self.mm(self.pb[4][:, 0:512], self.onesb[:, 0:128],
                             xb[:, d * 512:(d + 1) * 512], start=(d == 0), stop=(d == 7),
                             wait=[msc, self.ms_ones, self.bfree(4)], inc=(d == 7))
            self.bank_written(4, m1)
            m2 = None
            for d in range(8):
                m2 = self.mm(self.pb[5][:, 0:512], self.onesb[:, 0:128],
                             xb2[:, d * 512:(d + 1) * 512], start=(d == 0), stop=(d == 7),
                             wait=[mss, self.bfree(5)], inc=(d == 7))
            self.bank_written(5, m2)
            mb = self.fstage[:, 0:512]
            vr = self.fstage[:, 512:1024]
            ms3 = self.op("dve", lambda e, mb=mb:
                          e.tensor_scalar_mul(mb, self.pb[4][:, 0:512], 1.0 / D),
                          wait=[m1, ms_sx0, ms_sx20], inc="dve")
            self.bank_read(4, ms3)
            ms4 = self.op("dve", lambda e, mb=mb, vr=vr: e.tensor_mul(vr, mb, mb),
                          wait=[ms3], inc="dve")
            ms5 = self.op("dve", lambda e, vr=vr:
                          e.scalar_tensor_tensor(vr, self.pb[5][:, 0:512], 1.0 / D, vr,
                                                 ALU.mult, ALU.subtract),
                          wait=[m2, ms4], inc="dve")
            self.bank_read(5, ms5)
            ms6 = self.op("act", lambda e, vr=vr:
                          e.activation(out=vr, in_=vr, func=AF.Sqrt, bias=EPS_LN),
                          wait=[ms5, self.ms_ones], inc="act")
            ms7 = self.op("dve", lambda e, vr=vr: e.reciprocal(vr, vr),
                          wait=[ms6], inc="dve")
            for d in range(8):
                xs = self.x[:, d * TLOC + g * 512: d * TLOC + (g + 1) * 512]
                ys = self.y[:, d * TLOC + g * 512: d * TLOC + (g + 1) * 512]
                tmp = self.fstage[:, 1024 + 512 * (d % 2): 1536 + 512 * (d % 2)]
                msa = self.op("dve", lambda e, xs=xs, tmp=tmp, mb=mb:
                              e.tensor_sub(tmp, xs, mb), wait=[ms3], inc="dve")
                self.op("dve", lambda e, ys=ys, tmp=tmp, vr=vr:
                        e.tensor_mul(ys, tmp, vr), wait=[ms7, msa], inc="dve")
        self.ms_y = self.ms("dve")

    def p_attn_a(self):
        w = self.w
        ms_y = self.ms_y
        self.dma("pool", self.wdq_bf.ap(), w["bWDQ"], "gm")
        self.dma("pool", self.wuq_bf.ap(), w["bWUQ"], "gm")
        self.dma("pool", self.wqr_bf.ap(), w["bWQR"], "gm")
        self.ms_wbf = self.ms("gm")
        msq = self.op("pool", lambda e: e.memset(self.big2[:], 1.0), inc="pool")
        ms_wv = []
        for k in range(8):
            ms_wv.append(self.dma("pool", self.warena[:, k * 1024:(k + 1) * 1024],
                                  w["aWv"][k * 128:(k + 1) * 128, :], "wg"))
        ms_w_last = None
        for ti in range(NT):
            for eh in range(2):
                bk = (ti * 2 + eh) % 4
                pbk = self.pb[bk]
                mmm = None
                for k in range(8):
                    lhsT = self.y[:, k * TLOC + ti * 128: k * TLOC + (ti + 1) * 128]
                    rhs = self.warena[:, k * 1024 + eh * 512: k * 1024 + (eh + 1) * 512]
                    mmm = self.mm(pbk[:, 0:512], lhsT, rhs, start=(k == 0), stop=(k == 7),
                                  wait=[ms_y, ms_wv[7], self.bfree(bk)], inc=(k == 7))
                self.bank_written(bk, mmm)
                ms_w_last = mmm
                msd = None
                for h8 in range(8):
                    h = eh * 8 + h8
                    dst = self.big2[:, ti * 1040 + h * 65: ti * 1040 + h * 65 + 64]
                    msd = self.op("dve", lambda e, dst=dst, pbk=pbk, h8=h8:
                                  e.tensor_copy(dst, pbk[:, h8 * 64:(h8 + 1) * 64]),
                                  wait=[mmm, msq] if h8 == 0 else [], inc="dve")
                self.bank_read(bk, msd)
        ms_v_done = self.ms("dve")
        if self.stage == "vtm":
            xo = self.xout.rearrange("(a p) d -> p a d", p=128)
            m2 = self.dma("pool", xo[:, 0:16, :],
                          self.big2[:, 0:16640].rearrange("p (a d) -> p a d", a=16, d=1040)[:, :, 0:1024],
                          "gm", wait=[ms_v_done])
            self._wait("pool", m2[0], m2[1])
            return
        ms_wk = []
        for k in range(8):
            ms_wk.append(self.dma("pool", self.warena[:, k * 1024:(k + 1) * 1024],
                                  w["aWk"][k * 128:(k + 1) * 128, :], "wg",
                                  wait=[ms_w_last]))
        for m_ in range(8):
            for g in range(NG):
                bk = (m_ * NG + g) % 4
                pbk = self.pb[bk]
                mmm = None
                for k in range(8):
                    lhsT = self.warena[:, k * 1024 + m_ * 128: k * 1024 + (m_ + 1) * 128]
                    rhs = self.y[:, k * TLOC + g * 512: k * TLOC + (g + 1) * 512]
                    mmm = self.mm(pbk[:, 0:512], lhsT, rhs, start=(k == 0), stop=(k == 7),
                                  wait=[ms_wk[7], self.bfree(bk)], inc=(k == 7))
                self.bank_written(bk, mmm)
                ms_w_last = mmm
                dst = self.big1[:, m_ * TLOC + g * 512: m_ * TLOC + (g + 1) * 512]
                msd = self.op("dve", lambda e, dst=dst, pbk=pbk:
                              e.tensor_copy(dst, pbk[:, 0:512]), wait=[mmm], inc="dve")
                self.bank_read(bk, msd)
        ms_k_done = self.ms("dve")
        if self.stage == "kfm":
            xo = self.xout.rearrange("(a p) d -> p a d", p=128)
            m2 = self.dma("pool", xo[:, 0:16, :],
                          self.big1[:, 0:16384].rearrange("p (a d) -> p a d", a=16),
                          "gm", wait=[ms_k_done])
            self._wait("pool", m2[0], m2[1])
            return
        for h in range(H):
            mkv = None
            for ti in range(NT):
                bk = ti % 4
                pbk = self.pb[bk]
                lhsT = self.big1[(h % 2) * 64:(h % 2) * 64 + 64,
                                 (h // 2) * TLOC + ti * 128:(h // 2) * TLOC + (ti + 1) * 128]
                mm1 = self.mm(pbk[:, 0:256], lhsT,
                              self.projb[(h % 2) * 64:(h % 2) * 64 + 64, :],
                              start=True, stop=True,
                              wait=[ms_k_done, self.ms_proj, self.bfree(bk)], inc=True)
                self.bank_written(bk, mm1)
                stag = self.stg[:, (ti % 2) * 256: (ti % 2) * 256 + 256]
                msr = self.op("dve", lambda e, stag=stag, pbk=pbk:
                              e.tensor_relu(stag, pbk[:, 0:256]), wait=[mm1], inc="dve")
                self.bank_read(bk, msr)
                for fc in range(2):
                    kvbk = 6 + fc
                    mkv = self.mm(self.pb[kvbk][:, 0:65],
                                  stag[:, fc * 128:(fc + 1) * 128],
                                  self.big2[:, ti * 1040 + h * 65: ti * 1040 + (h + 1) * 65],
                                  start=(ti == 0), stop=(ti == NT - 1),
                                  wait=([msr, ms_v_done, self.bfree(kvbk)] if ti == 0
                                        else [msr]),
                                  inc=(ti == NT - 1))
                    if ti == NT - 1:
                        self.bank_written(kvbk, mkv)
            for fc in range(2):
                kvbk = 6 + fc
                col = self.COL_KV + fc * 65
                msd = self.op("dve", lambda e, col=col, kvbk=kvbk:
                              e.tensor_copy(self.misc2[:, col:col + 65],
                                            self.pb[kvbk][:, 0:65]),
                              wait=[self.bready(kvbk)], inc="dve")
                self.bank_read(kvbk, msd)
                mda = self.dma("sp", self.acon_h[h * 256 + fc * 128: h * 256 + (fc + 1) * 128, :],
                               self.misc2[:, col:col + 65], "skv", wait=[msd])
                self._wait("dve", mda[0], mda[1])
        ms_acon = self.ms("skv")
        if self.stage == "acon":
            self._wait("pool", "skv", ms_acon[1])
            m2 = self.dma("pool", self.xout[0:2048, 0:65],
                          self.acon_h.ap()[0:2048, :], "gm", wait=[])
            self._wait("pool", m2[0], m2[1])
            return

        ms_wq = []
        for k in range(8):
            ms_wq.append(self.dma("pool", self.warena[:, k * 1024:(k + 1) * 1024],
                                  w["aWq"][k * 128:(k + 1) * 128, :], "wg",
                                  wait=[ms_w_last]))
        ms_k_pe = self.ms("pe")
        self.op("pool", lambda e: e.collective_compute(
            "AllGather", ALU.bypass, replica_groups=[[0, 1, 2, 3], [4, 5, 6, 7]],
            ins=[self.acon_h.ap().opt()], outs=[self.ag_h.ap().opt()]),
            wait=[ms_acon], inc="cc0")
        ms_cc0 = self.ms("cc0")
        self.p_quantize(qbufs=[(self.ropeb[:, 0:1024], self.kvb[:, 0:1024]),
                               (self.kvb[:, 1024:2048], self.stg[:, 0:1024])],
                        pe_guard=self.ms("pe"))
        self._wait("dve", "sscr", self.ms("sscr")[1])
        for m_ in range(8):
            for g in range(NG):
                bk = (m_ * NG + g) % 4
                pbk = self.pb[bk]
                mmm = None
                for k in range(8):
                    lhsT = self.warena[:, k * 1024 + m_ * 128: k * 1024 + (m_ + 1) * 128]
                    rhs = self.y[:, k * TLOC + g * 512: k * TLOC + (g + 1) * 512]
                    wt = [ms_wq[7], self.bfree(bk)]
                    if m_ == 0 and g == 0:
                        wt.append(ms_k_pe)
                    mmm = self.mm(pbk[:, 0:512], lhsT, rhs, start=(k == 0), stop=(k == 7),
                                  wait=wt, inc=(k == 7))
                self.bank_written(bk, mmm)
                ms_w_last = mmm
                dst = self.big1[:, m_ * TLOC + g * 512: m_ * TLOC + (g + 1) * 512]
                msd = self.op("dve", lambda e, dst=dst, pbk=pbk:
                              e.tensor_copy(dst, pbk[:, 0:512]), wait=[mmm], inc="dve")
                self.bank_read(bk, msd)
        ms_q_done = self.ms("dve")

        ag4 = self.ag_h.ap().rearrange("(r hf p) c -> r p hf c", r=4, p=128)
        ms_v_pe = self.ms("pe")
        msl = []
        for r in range(4):
            dst = self.big2[:, r * 2080:(r + 1) * 2080]
            msl.append(self.dma("pool", dst, ag4[r], "gm", wait=[ms_cc0, ms_v_pe]))
        msadd = self.op("dve", lambda e: e.tensor_add(self.kvb[:], self.big2[:, 0:2080],
                                                      self.big2[:, 2080:4160]),
                        wait=[msl[0], msl[1]], inc="dve")
        msadd = self.op("dve", lambda e: e.tensor_add(self.kvb[:], self.kvb[:],
                                                      self.big2[:, 4160:6240]),
                        wait=[msl[2]], inc="dve")
        msadd = self.op("dve", lambda e: e.tensor_add(self.kvb[:], self.kvb[:],
                                                      self.big2[:, 6240:8320]),
                        wait=[msl[3]], inc="dve")
        if self.stage == "kv":
            xo = self.xout.rearrange("(a p) d -> p a d", p=128)
            m2 = self.dma("pool", xo[:, 0:2, :], self.kvb[:, 0:2048], "gm", wait=[msadd])
            self._wait("pool", m2[0], m2[1])
            return

        for h in range(H):
            for g in range(NG):
                msr = None
                for fc in range(2):
                    bk = ((h * NG + g) % 2) * 2 + fc
                    pbk = self.pb[bk]
                    lhsT = self.projb[(h % 2) * 64:(h % 2) * 64 + 64,
                                      fc * 128:(fc + 1) * 128]
                    rhs = self.big1[(h % 2) * 64:(h % 2) * 64 + 64,
                                    (h // 2) * TLOC + g * 512:(h // 2) * TLOC + (g + 1) * 512]
                    mm1 = self.mm(pbk[:, 0:512], lhsT, rhs, start=True, stop=True,
                                  wait=[ms_q_done, self.bfree(bk)], inc=True)
                    self.bank_written(bk, mm1)
                    dst = self.stg[:, fc * 512:(fc + 1) * 512]
                    msr = self.op("dve", lambda e, dst=dst, pbk=pbk:
                                  e.tensor_relu(dst, pbk[:, 0:512]), wait=[mm1], inc="dve")
                    self.bank_read(bk, msr)
                mma = mmn = None
                for fc in range(2):
                    lk = self.kvb[:, (2 * h + fc) * 65:(2 * h + fc) * 65 + 64]
                    ln_ = self.kvb[:, (2 * h + fc) * 65 + 64:(2 * h + fc) * 65 + 65]
                    rhs = self.stg[:, fc * 512:(fc + 1) * 512]
                    wt = [msadd, msr]
                    if fc == 0:
                        wt.append(self.bfree(2))
                    mma = self.mm(self.pb[2][0:64, 0:512], lk, rhs,
                                  start=(fc == 0), stop=(fc == 1), wait=wt, inc=(fc == 1))
                    wt2 = [self.bfree(3)] if fc == 0 else []
                    mmn = self.mm(self.pb[3][0:1, 0:512], ln_, rhs,
                                  start=(fc == 0), stop=(fc == 1), wait=wt2, inc=(fc == 1))
                self.bank_written(2, mma)
                self.bank_written(3, mmn)
                row = self.misc2[0:1, 0:512]
                msn1 = self.op("dve", lambda e, row=row:
                               e.tensor_scalar_add(row, self.pb[3][0:1, 0:512], EPS_A),
                               wait=[mmn, self.ms("pe")], inc="dve")
                self.bank_read(3, msn1)
                msn2 = self.op("dve", lambda e, row=row: e.reciprocal(row, row),
                               wait=[msn1], inc="dve")
                mmb = self.mm(self.pb[3][0:64, 0:512], self.onesf[0:1, 0:64], row,
                              start=True, stop=True, wait=[msn2], inc=True)
                self.bank_written(3, mmb)
                msc = self.op("dve", lambda e:
                              e.tensor_copy(self.o2b[:], self.pb[2][0:64, 0:512]),
                              wait=[mma], inc="dve")
                self.bank_read(2, msc)
                ydst = self.y[(h % 2) * 64:(h % 2) * 64 + 64,
                              (h // 2) * TLOC + g * 512:(h // 2) * TLOC + (g + 1) * 512]
                msm = self.op("dve", lambda e, ydst=ydst:
                              e.tensor_mul(ydst, self.o2b[:], self.pb[3][0:64, 0:512]),
                              wait=[mmb, msc], inc="dve")
                self.bank_read(3, msm)
        ms_attn_out = self.ms("dve")
        if self.stage == "aout":
            self.p_store_y()
            return

        ms_wo = []
        for k in range(8):
            ms_wo.append(self.dma("pool", self.warena[:, k * 1024:(k + 1) * 1024],
                                  w["aWo"][k * 128:(k + 1) * 128, :], "wg",
                                  wait=[ms_w_last]))
        bo_r = self.w["abo"].rearrange("(a d) -> a d", a=1)
        msbo = self.dma("sp", self.fstage[0:1, 0:1024], bo_r, "srow",
                        wait=[self.ms("dve")])
        self._wait("pe", msbo[0], msbo[1])
        self.p_biascol_from_row(self.fstage, 8, self.COL_BO, 5)
        ms_bo = self.ms("dve")
        for m_ in range(8):
            self.op("dve", lambda e, m_=m_:
                    e.tensor_scalar_add(self.x[:, m_ * TLOC:(m_ + 1) * TLOC],
                                        self.x[:, m_ * TLOC:(m_ + 1) * TLOC],
                                        self.misc2[:, self.COL_BO + m_:self.COL_BO + m_ + 1]),
                    wait=[ms_bo], inc="dve")
        for m_ in range(8):
            for g in range(NG):
                bk = (m_ * NG + g) % 4
                pbk = self.pb[bk]
                mmm = None
                for k in range(8):
                    lhsT = self.warena[:, k * 1024 + m_ * 128: k * 1024 + (m_ + 1) * 128]
                    rhs = self.y[:, k * TLOC + g * 512: k * TLOC + (g + 1) * 512]
                    mmm = self.mm(pbk[:, 0:512], lhsT, rhs, start=(k == 0), stop=(k == 7),
                                  wait=[ms_wo[7], ms_attn_out, self.bfree(bk)], inc=(k == 7))
                self.bank_written(bk, mmm)
                xs = self.x[:, m_ * TLOC + g * 512: m_ * TLOC + (g + 1) * 512]
                msd = self.op("dve", lambda e, xs=xs, pbk=pbk:
                              e.tensor_add(xs, xs, pbk[:, 0:512]), wait=[mmm], inc="dve")
                self.bank_read(bk, msd)
        self.ms_warena_last = self.ms("pe")

    def p_biascol_from_row(self, row_t, ncols, dst_col0, bank):
        for m in range(ncols):
            pbk = self.pb[bank]
            msmm = self.mm(pbk[:, 0:1], row_t[0:1, m * 128:(m + 1) * 128],
                           self.onesf[0:1, 0:1], start=True, stop=True,
                           wait=[self.bfree(bank), self.ms_ones], inc=True)
            self.bank_written(bank, msmm)
            msd = self.op("dve", lambda e, m=m, c=dst_col0, pbk=pbk:
                          e.tensor_copy(self.misc2[:, c + m:c + m + 1], pbk[:, 0:1]),
                          wait=[msmm], inc="dve")
            self.bank_read(bank, msd)

    def p_quantize(self, qbufs=None, pe_guard=None):
        if qbufs is None:
            qbufs = [(self.stg[:, 0:1024], self.ptb[:, 0:1024]),
                     (self.big1[:, 0:1024], self.big1[:, 1024:2048])]
        for wi in range(2):
            wname = "fW1" if wi == 0 else "fW2"
            scr = self.wq1s if wi == 0 else self.wq2s
            rcol = self.COL_G if wi == 0 else self.COL_G + 1
            W = self.w[wname]
            rows = DFF if wi == 0 else D
            cols = D if wi == 0 else DFF
            nr, ncb = rows // 128, cols // 1024
            blocks = [(r, c) for r in range(nr) for c in range(ncb)]
            ms_fs_free = [self.ms("dve"), self.ms("dve")]
            msr = None
            for bi, (r, c) in enumerate(blocks):
                half = bi % 2
                fs = self.fstage[:, half * 1024:(half + 1) * 1024]
                msl = self.dma("sp", fs, W[r * 128:(r + 1) * 128, c * 1024:(c + 1) * 1024],
                               "sx" if half == 0 else "sx2",
                               wait=[ms_fs_free[half]] if ms_fs_free[half] else [])
                col = self.COL_AR + bi
                msr = self.op("dve", lambda e, fs=fs, col=col:
                              e.tensor_reduce(self.misc2[:, col:col + 1], fs, AX, ALU.add,
                                              apply_absolute_value=True),
                              wait=[msl], inc="dve")
                ms_fs_free[half] = msr
            mst = self.op("dve", lambda e:
                          e.tensor_reduce(self.misc2[:, self.COL_G + 2:self.COL_G + 3],
                                          self.misc2[:, self.COL_AR:self.COL_AR + 32],
                                          AX, ALU.add), wait=[msr], inc="dve")
            mmg = self.mm(self.pb[5][:, 0:1], self.ones128f[:],
                          self.misc2[:, self.COL_G + 2:self.COL_G + 3],
                          start=True, stop=True,
                          wait=[mst, self.bfree(5), self.ms_ones], inc=True)
            self.bank_written(5, mmg)
            msg1 = self.op("dve", lambda e, rcol=rcol:
                           e.tensor_scalar(self.misc2[:, rcol:rcol + 1], self.pb[5][:, 0:1],
                                           1.0 / (DFF * D), EPS_Q, ALU.mult, ALU.add),
                           wait=[mmg], inc="dve")
            self.bank_read(5, msg1)
            msg2 = self.op("dve", lambda e, rcol=rcol:
                           e.reciprocal(self.misc2[:, rcol:rcol + 1],
                                        self.misc2[:, rcol:rcol + 1]),
                           wait=[msg1], inc="dve")
            ms_fs_free = [self.ms("dve"), self.ms("dve")]
            ms_set_free = [None, None]
            for bi, (r, c) in enumerate(blocks):
                half = bi % 2
                fs = self.fstage[:, half * 1024:(half + 1) * 1024]
                msl = self.dma("sp", fs, W[r * 128:(r + 1) * 128, c * 1024:(c + 1) * 1024],
                               "sx" if half == 0 else "sx2",
                               wait=[ms_fs_free[half]] if ms_fs_free[half] else [])
                s1, s2 = qbufs[half]
                rc = self.misc2[:, rcol:rcol + 1]
                wt = [msl, msg2, self.ms_ones]
                if pe_guard is not None:
                    wt.append(pe_guard)
                if ms_set_free[half]:
                    wt.append(ms_set_free[half])
                ma = self.op("act", lambda e, fs=fs, s1=s1, rc=rc:
                             e.activation(out=s1, in_=fs, func=AF.Sign, scale=rc, bias=-0.5),
                             wait=wt, inc="act")
                mb = self.op("act", lambda e, fs=fs, s2=s2, rc=rc:
                             e.activation(out=s2, in_=fs, func=AF.Sign, scale=rc, bias=0.5),
                             wait=[], inc="act")
                ms_fs_free[half] = mb
                mc = self.op("dve", lambda e, s1=s1, s2=s2:
                             e.tensor_add(s1, s1, s2), wait=[ma, mb], inc="dve")
                if wi == 0:
                    dst = scr.ap()[r * 128:(r + 1) * 128, :]
                else:
                    dst = scr.ap()[r * 128:(r + 1) * 128, c * 1024:(c + 1) * 1024]
                msw = self.dma("sp", dst, s1, "sscr", wait=[mc])
                ms_set_free[half] = msw
        self.ms_wq_scr = self.ms("sscr")

    def p_ffn(self):
        ms_y = self.ms("dve")
        if not getattr(self, "_bias_built", False):
            self._bias_built = True
            b1r = self.w["fb1"].rearrange("(a d) -> a d", a=1)
            m1 = self.dma("sp", self.fstage[0:1, 0:2048], b1r[0:1, 0:2048], "srow",
                          wait=[self.ms("dve")])
            self._wait("pe", m1[0], m1[1])
            self.p_biascol_from_row(self.fstage, 16, self.COL_B1, 5)
            m2 = self.dma("sp", self.fstage[0:1, 0:2048], b1r[0:1, 2048:4096], "srow",
                          wait=[self.ms("pe")])
            self._wait("pe", m2[0], m2[1])
            self.p_biascol_from_row(self.fstage, 16, self.COL_B1 + 16, 5)
            b2r = self.w["fb2"].rearrange("(a d) -> a d", a=1)
            m3 = self.dma("sp", self.fstage[0:1, 0:1024], b2r, "srow",
                          wait=[self.ms("pe")])
            self._wait("pe", m3[0], m3[1])
            self.p_biascol_from_row(self.fstage, 8, self.COL_B2, 5)
        ms_bias = self.ms("dve")
        for m_ in range(8):
            self.op("dve", lambda e, m_=m_:
                    e.tensor_scalar_add(self.x[:, m_ * TLOC:(m_ + 1) * TLOC],
                                        self.x[:, m_ * TLOC:(m_ + 1) * TLOC],
                                        self.misc2[:, self.COL_B2 + m_:self.COL_B2 + m_ + 1]),
                    wait=[ms_bias], inc="dve")
        ms_b2 = self.ms("dve")

        ms_big2_pe = self.ms("pe")
        for g in range(NG):
            for fh in range(2):
                ms_w1 = []
                for d in range(8):
                    src = self.wq1s.ap()[fh * 2048:(fh + 1) * 2048, d * 128:(d + 1) * 128]
                    m = self.dmat("sp", self.big2[:, d * 2048:(d + 1) * 2048], src, "sw1",
                                  wait=[self.ms_wq_scr, ms_big2_pe])
                    ms_w1.append(m)
                for fl in range(16):
                    f = fh * 16 + fl
                    bk = f % 4
                    pbk = self.pb[bk]
                    mmm = None
                    for d in range(8):
                        lhsT = self.big2[:, d * 2048 + fl * 128: d * 2048 + (fl + 1) * 128]
                        rhs = self.y[:, d * TLOC + g * 512: d * TLOC + (g + 1) * 512]
                        mmm = self.mm(pbk[:, 0:512], lhsT, rhs,
                                      start=(d == 0), stop=(d == 7),
                                      wait=[ms_w1[7], ms_y, self.bfree(bk)], inc=(d == 7))
                    self.bank_written(bk, mmm)
                    if fl == 15:
                        ms_big2_pe = mmm
                    dst = self.big1[:, f * 512:(f + 1) * 512]
                    b1c = self.misc2[:, self.COL_B1 + f:self.COL_B1 + f + 1]
                    msh = self.op("act", lambda e, dst=dst, pbk=pbk, b1c=b1c:
                                  e.activation(out=dst, in_=pbk[:, 0:512], func=AF.Relu,
                                               scale=0.5, bias=b1c),
                                  wait=[mmm], inc="act")
                    self.bank_read(bk, msh)
            ms_h = self.ms("act")
            ms_w2use = [self.ms("pe")] * 4
            w2sems = ["sw2a", "sw2b", "sw2c", "sw2d"]
            mlast = None
            for f in range(32):
                slot = f % 4
                wsl = self.warena[:, slot * 1024:(slot + 1) * 1024]
                wt = [self.ms_wq_scr]
                if ms_w2use[slot]:
                    wt.append(ms_w2use[slot])
                mw = self.dmat("sp", wsl, self.wq2s.ap()[:, f * 128:(f + 1) * 128],
                               w2sems[slot], wait=wt)
                for m_ in range(8):
                    wtm = [mw, ms_h]
                    if f == 0:
                        wtm.append(self.bfree(m_))
                    mlast = self.mm(self.pb[m_][:, 0:512],
                                    wsl[:, m_ * 128:(m_ + 1) * 128],
                                    self.big1[:, f * 512:(f + 1) * 512],
                                    start=(f == 0), stop=(f == 31),
                                    wait=wtm, inc=(m_ == 7))
                ms_w2use[slot] = mlast
            for m_ in range(8):
                self.bank_written(m_, mlast)
                xs = self.x[:, m_ * TLOC + g * 512: m_ * TLOC + (g + 1) * 512]
                msd = self.op("dve", lambda e, xs=xs, m_=m_:
                              e.scalar_tensor_tensor(xs, self.pb[m_][:, 0:512], 0.5, xs,
                                                     ALU.mult, ALU.add),
                              wait=[mlast, ms_b2], inc="dve")
                self.bank_read(m_, msd)

    def p_attn_b(self):
        w = self.w
        ms_y = self.ms("dve")
        ms_warena_pe = self.ms("pe")
        ms_wdkv = []
        for k in range(8):
            ms_wdkv.append(self.dma("pool", self.warena[:, k * 256:(k + 1) * 256],
                                    w["bWDKV"][k * 128:(k + 1) * 128, :], "wg",
                                    wait=[ms_warena_pe]))
        for m_ in range(2):
            for g in range(NG):
                bk = (m_ * NG + g) % 4
                pbk = self.pb[bk]
                mmm = None
                for k in range(8):
                    lhsT = self.warena[:, k * 256 + m_ * 128: k * 256 + (m_ + 1) * 128]
                    rhs = self.y[:, k * TLOC + g * 512: k * TLOC + (g + 1) * 512]
                    mmm = self.mm(pbk[:, 0:512], lhsT, rhs, start=(k == 0), stop=(k == 7),
                                  wait=[ms_wdkv[7], ms_y, self.bfree(bk)], inc=(k == 7))
                self.bank_written(bk, mmm)
                dst = self.warena[:, 2048 + m_ * 2048 + g * 512: 2048 + m_ * 2048 + (g + 1) * 512]
                msd = self.op("dve", lambda e, dst=dst, pbk=pbk:
                              e.tensor_copy(dst, pbk[:, 0:512]), wait=[mmm], inc="dve")
                self.bank_read(bk, msd)
        ms_ckv = self.ms("dve")

        ms_wuk = []
        for ck in range(2):
            ms_wuk.append(self.dma("pool", self.warena[:, 6144 + ck * 1024: 6144 + (ck + 1) * 1024],
                                   w["bWUK"][ck * 128:(ck + 1) * 128, :], "wg"))
        ms_kc_pe = None
        for m_ in range(8):
            for g in range(NG):
                bk = (m_ * NG + g) % 4
                pbk = self.pb[bk]
                mmm = None
                for ck in range(2):
                    lhsT = self.warena[:, 6144 + ck * 1024 + m_ * 128: 6144 + ck * 1024 + (m_ + 1) * 128]
                    rhs = self.warena[:, 2048 + ck * 2048 + g * 512: 2048 + ck * 2048 + (g + 1) * 512]
                    mmm = self.mm(pbk[:, 0:512], lhsT, rhs, start=(ck == 0), stop=(ck == 1),
                                  wait=[ms_wuk[1], ms_ckv, self.bfree(bk)], inc=(ck == 1))
                self.bank_written(bk, mmm)
                ms_kc_pe = mmm
                shalf = (m_ * NG + g) % 2
                sdst = self.stg[:, shalf * 512:(shalf + 1) * 512]
                msd = self.op("dve", lambda e, sdst=sdst, pbk=pbk:
                              e.tensor_copy(sdst, pbk[:, 0:512]), wait=[mmm], inc="dve")
                self.bank_read(bk, msd)
                mdk = None
                for hh in range(2):
                    h = 2 * m_ + hh
                    src = sdst[hh * 64:(hh + 1) * 64, :].rearrange("p (l t) -> p l t", l=4)
                    off = (ROW_K + (h * 16 + g * 4) * 64) * 128
                    dst = bass.AP(self.kvcon_h, off,
                                  [[128, 64], [64 * 128, 4], [1, 128]])
                    mdk = self.dma("sp", dst, src, "sstg", wait=[msd])
                self._wait("dve", mdk[0], mdk[1])

        msvset = self.op("pool", lambda e: e.memset(self.vstg[:], 1.0), inc="pool",
                         wait=[ms_warena_pe])
        ms_wuv = []
        for ck in range(2):
            ms_wuv.append(self.dma("pool", self.warena[:, 6144 + ck * 1024: 6144 + (ck + 1) * 1024],
                                   w["bWUV"][ck * 128:(ck + 1) * 128, :], "wg",
                                   wait=[ms_kc_pe]))
        ms_vdma = [None, None]
        for lt in range(NT):
            vhalf = lt % 2
            vsl = self.vstg[:, vhalf * 1040:(vhalf + 1) * 1040]
            for eh in range(2):
                bk = (lt % 2) * 2 + eh
                pbk = self.pb[bk]
                mmm = None
                for ck in range(2):
                    lhsT = self.warena[:, 2048 + ck * 2048 + lt * 128: 2048 + ck * 2048 + (lt + 1) * 128]
                    rhs = self.warena[:, 6144 + ck * 1024 + eh * 512: 6144 + ck * 1024 + (eh + 1) * 512]
                    mmm = self.mm(pbk[:, 0:512], lhsT, rhs, start=(ck == 0), stop=(ck == 1),
                                  wait=[ms_wuv[1], self.bfree(bk)], inc=(ck == 1))
                self.bank_written(bk, mmm)
                msd = None
                for h8 in range(8):
                    h = eh * 8 + h8
                    wt = [mmm, msvset]
                    if h8 == 0 and ms_vdma[vhalf]:
                        wt.append(ms_vdma[vhalf])
                    msd = self.op("dve", lambda e, vsl=vsl, h=h, pbk=pbk, h8=h8:
                                  e.tensor_copy(vsl[:, h * 65: h * 65 + 64],
                                                pbk[:, h8 * 64:(h8 + 1) * 64]),
                                  wait=wt, inc="dve")
                self.bank_read(bk, msd)
            msv = None
            for h in range(H):
                off = (ROW_V + (h * 16 + lt) * 65) * 128
                dst = bass.AP(self.kvcon_h, off, [[65, 128], [1, 65]])
                msv = self.dma("sp", dst, vsl[:, h * 65:(h + 1) * 65], "svst", wait=[msd])
            ms_vdma[vhalf] = msv

        ms_wkr = self.dma("pool", self.warena[:, 0:256],
                          w["bWKR"].rearrange("(k p) m -> p k m", p=128), "wg",
                          wait=[ms_kc_pe])
        for g in range(NG):
            bk = g % 2
            pbk = self.pb[bk]
            mmm = None
            for k in range(8):
                lhsT = self.warena[:, k * 32:(k + 1) * 32]
                rhs = self.y[:, k * TLOC + g * 512: k * TLOC + (g + 1) * 512]
                mmm = self.mm(pbk[0:32, 0:512], lhsT, rhs, start=(k == 0), stop=(k == 7),
                              wait=[ms_wkr, self.bfree(bk)], inc=(k == 7))
            self.bank_written(bk, mmm)
            mrl = self.dma("act", self.ropeb[0:32, 0:512],
                           self.ropei[0:32, g * 512:(g + 1) * 512], "sr0",
                           wait=[self.ms("dve")])
            msd = self.op("dve", lambda e, pbk=pbk:
                          e.tensor_mul(self.stg[0:32, 0:512], pbk[0:32, 0:512],
                                       self.ropeb[0:32, 0:512]),
                          wait=[mmm, mrl], inc="dve")
            self.bank_read(bk, msd)
            src = self.stg[0:32, 0:512].rearrange("p (l t) -> p l t", l=4)
            off = (ROW_KR + g * 4 * 32) * 128
            dst = bass.AP(self.kvcon_h, off, [[128, 32], [32 * 128, 4], [1, 128]])
            msk = self.dma("sp", dst, src, "sstg", wait=[msd])
            self._wait("dve", msk[0], msk[1])

        ms_kvcon = self.ms("sstg")
        ms_kvcon2 = self.ms("svst")
        self.op("pool", lambda e: e.collective_compute(
            "AllGather", ALU.bypass, replica_groups=[[0, 1, 2, 3], [4, 5, 6, 7]],
            ins=[self.kvcon_h.ap().opt()], outs=[self.kvg_h.ap().opt()]),
            wait=[ms_kvcon, ms_kvcon2], inc="cc1")
        ms_cc1 = self.ms("cc1")

        ms_big1_pe = self.ms("pe")
        for g in range(NG):
            ms_wslot = [self.ms("pe")] * 4
            for mc in range(32):
                slot = mc % 4
                wsl = self.warena[:, slot * 1024:(slot + 1) * 1024]
                src = self.wdq_bf.ap()[:, mc * 128:(mc + 1) * 128].rearrange(
                    "(k p) m -> p k m", p=128)
                wt = [ms_wslot[slot], self.ms_wbf]
                mw = self.dma("act", wsl, src, "wp%d" % slot, wait=wt)
                bk = mc % 4
                pbk = self.pb[bk]
                mmm = None
                for k in range(8):
                    lhsT = wsl[:, k * 128:(k + 1) * 128]
                    rhs = self.y[:, k * TLOC + g * 512: k * TLOC + (g + 1) * 512]
                    wtm = [mw, self.bfree(bk)]
                    if g == 0 and mc == 0:
                        wtm.append(ms_big1_pe)
                    mmm = self.mm(pbk[:, 0:512], lhsT, rhs, start=(k == 0), stop=(k == 7),
                                  wait=wtm, inc=(k == 7))
                self.bank_written(bk, mmm)
                ms_wslot[slot] = mmm
                dst = self.big1[:, mc * 512:(mc + 1) * 512]
                msd = self.op("dve", lambda e, dst=dst, pbk=pbk:
                              e.tensor_copy(dst, pbk[:, 0:512]), wait=[mmm], inc="dve")
                self.bank_read(bk, msd)
            ms_cq = self.ms("dve")
            ms_whalf = [None, None]
            for m_ in range(8):
                half = m_ % 2
                wsl = self.warena[:, half * 4096:(half + 1) * 4096]
                src = self.wuq_bf.ap()[:, m_ * 128:(m_ + 1) * 128].rearrange(
                    "(k p) m -> p k m", p=128)
                wt = [ms_wslot[2 * half], ms_wslot[2 * half + 1]] if m_ < 2 else []
                if ms_whalf[half]:
                    wt = [ms_whalf[half]]
                mw = self.dma("act", wsl, src, "wp%d" % half, wait=wt)
                bk = m_ % 4
                pbk = self.pb[bk]
                mmm = None
                for k in range(32):
                    lhsT = wsl[:, k * 128:(k + 1) * 128]
                    rhs = self.big1[:, k * 512:(k + 1) * 512]
                    mmm = self.mm(pbk[:, 0:512], lhsT, rhs, start=(k == 0), stop=(k == 31),
                                  wait=[mw, ms_cq, self.bfree(bk)], inc=(k == 31))
                self.bank_written(bk, mmm)
                ms_whalf[half] = mmm
                shalf = m_ % 2
                sdst = self.stg[:, shalf * 512:(shalf + 1) * 512]
                msd = self.op("dve", lambda e, sdst=sdst, pbk=pbk:
                              e.tensor_copy(sdst, pbk[:, 0:512]), wait=[mmm], inc="dve")
                self.bank_read(bk, msd)
                for hh in range(2):
                    h = 2 * m_ + hh
                    src2 = sdst[hh * 64:(hh + 1) * 64, :].rearrange("p (l t) -> p l t", l=4)
                    off = ((h * 16 + g * 4) * 96) * 128
                    dst = bass.AP(self.qcon_h, off,
                                  [[128, 64], [96 * 128, 4], [1, 128]])
                    msq = self.dma("sp", dst, src2, "sstg", wait=[msd])
                self._wait("dve", msq[0], msq[1])
            for mr in range(4):
                half = mr % 2
                wsl = self.warena[:, half * 4096:(half + 1) * 4096]
                src = self.wqr_bf.ap()[:, mr * 128:(mr + 1) * 128].rearrange(
                    "(k p) m -> p k m", p=128)
                mw = self.dma("act", wsl, src, "wp%d" % half,
                              wait=[ms_whalf[half]] if ms_whalf[half] else [])
                bk = mr % 4
                pbk = self.pb[bk]
                mmm = None
                for k in range(32):
                    lhsT = wsl[:, k * 128:(k + 1) * 128]
                    rhs = self.big1[:, k * 512:(k + 1) * 512]
                    mmm = self.mm(pbk[:, 0:512], lhsT, rhs, start=(k == 0), stop=(k == 31),
                                  wait=[mw, self.bfree(bk)], inc=(k == 31))
                self.bank_written(bk, mmm)
                ms_whalf[half] = mmm
                rhalf = mr % 2
                rsl = self.ropeb[:, rhalf * 512:(rhalf + 1) * 512]
                mrl = self.dma("act", rsl, self.ropei[mr * 128:(mr + 1) * 128,
                                                     g * 512:(g + 1) * 512],
                               "sr%d" % rhalf, wait=[self.ms("dve")])
                shalf = mr % 2
                sdst = self.stg[:, shalf * 512:(shalf + 1) * 512]
                msd = self.op("dve", lambda e, sdst=sdst, pbk=pbk, rsl=rsl:
                              e.tensor_mul(sdst, pbk[:, 0:512], rsl),
                              wait=[mmm, mrl], inc="dve")
                self.bank_read(bk, msd)
                msq = None
                for hh in range(4):
                    h = 4 * mr + hh
                    src2 = sdst[hh * 32:(hh + 1) * 32, :].rearrange("p (l t) -> p l t", l=4)
                    off = ((h * 16 + g * 4) * 96 + 64) * 128
                    dst = bass.AP(self.qcon_h, off,
                                  [[128, 32], [96 * 128, 4], [1, 128]])
                    msq = self.dma("sp", dst, src2, "sstg", wait=[msd])
                self._wait("dve", msq[0], msq[1])

        ms_qcon = self.ms("sstg")
        self.op("pool", lambda e: e.collective_compute(
            "AllGather", ALU.bypass, replica_groups=[[0, 1, 2, 3], [4, 5, 6, 7]],
            ins=[self.qcon_h.ap().opt()], outs=[self.qg2_h.ap().opt()]),
            wait=[ms_qcon], inc="cc3")
        ms_cc3 = self.ms("cc3")

        xgt = self.kvg_h.ap()
        ms_big2_pe = self.ms("pe")
        for r in range(4):
            reg = xgt[r * 33536 + ROW_KR: r * 33536 + ROW_KR + 512, :]
            src = reg.rearrange("(l f) t -> f l t", l=16)
            self.dma("sp", self.big2[64:96, r * 2048:(r + 1) * 2048], src, "satt",
                     wait=[ms_cc1, ms_big2_pe])
        ms_o_dma = None
        RSTRIDE_KV = 33536 * 128
        RSTRIDE_Q = 24576 * 128
        for hl in range(4):
            q_eng = "sp" if hl % 2 == 0 else "act"
            q_sem = "satt" if hl % 2 == 0 else "aatt"
            ms_prev_pe = self.ms("pe")
            ms_ldq = []

            def mk_d2d(kind, hl=hl, eng=q_eng):
                def fn(e, kind=kind, hl=hl, eng=eng):
                    hsel = self.cg[eng] * 4 + hl
                    if kind == "k":
                        off = ROW_K * 128 + hsel * (16 * 64 * 128)
                        dims = [[RSTRIDE_KV, 4], [128, 1024], [1, 128]]
                        return e.dma_start(out=self.kscr_h.ap(),
                                           in_=bass.AP(self.kvg_h, off, dims))
                    elif kind == "v":
                        off = ROW_V * 128 + hsel * (16 * 65 * 128)
                        dims = [[RSTRIDE_KV, 4], [128, 1040], [1, 128]]
                        return e.dma_start(out=self.vscr_h.ap(),
                                           in_=bass.AP(self.kvg_h, off, dims))
                    else:
                        off = hsel * (16 * 96 * 128)
                        dims = [[RSTRIDE_Q, 4], [128, 1536], [1, 128]]
                        return e.dma_start(out=self.qscr_h.ap(),
                                           in_=bass.AP(self.qg2_h, off, dims))
                return fn
            for kind in ("k", "v"):
                ms_ldq.append(self.op(q_eng, mk_d2d(kind),
                                      wait=[ms_cc1, ms_prev_pe], inc=q_sem, n=16))
            ms_ldq.append(self.op(q_eng, mk_d2d("q"),
                                  wait=[ms_cc3, ms_prev_pe], inc=q_sem, n=16))
            self._wait(q_eng, q_sem, ms_ldq[-1][1])
            for r in range(4):
                ksrc = self.kscr_h.ap()[r * 1024:(r + 1) * 1024, :].rearrange(
                    "(l f) t -> f l t", l=16)
                ms_ldq.append(self.dma(q_eng, self.big2[0:64, r * 2048:(r + 1) * 2048],
                                       ksrc, q_sem))
                vsrc = self.vscr_h.ap()[r * 1040:(r + 1) * 1040, :].rearrange(
                    "(l r2) c -> l (r2 c)", l=16).rearrange("l (t c) -> t l c", t=128, c=65)
                ms_ldq.append(self.dma(q_eng, self.big2[:, 8192 + r * 1040: 8192 + (r + 1) * 1040].rearrange("p (l c) -> p l c", l=16),
                                       vsrc, q_sem))
                qsrc = self.qscr_h.ap()[r * 1536:(r + 1) * 1536, :].rearrange(
                    "(l f) t -> f l t", l=16)
                ms_ldq.append(self.dma(q_eng, self.big1[0:96, r * 2048:(r + 1) * 2048],
                                       qsrc, q_sem))
            ms_ld = ms_ldq[-1]
            ms_ld = ms_ldq[-1]
            kmax = self.misc2[0:1, 600:601]
            qmax = self.misc2[0:1, 601:602]
            tmp1 = self.misc2[0:1, 602:603]
            ksq = self.big1[0:96, 8192:16384]
            msqk = self.op("act", lambda e, ksq=ksq:
                           e.activation(out=ksq, in_=self.big2[0:96, 0:8192],
                                        func=AF.Square),
                           wait=[ms_ld], inc="act")
            ms_nmax = None
            for src_i in range(2):
                msq_ = msqk
                if src_i == 1:
                    msq_ = self.op("act", lambda e, ksq=ksq:
                                   e.activation(out=ksq, in_=self.big1[0:96, 0:8192],
                                                func=AF.Square),
                                   wait=[ms_ld, self.ms("pe")], inc="act")
                dstmax = kmax if src_i == 0 else qmax
                for j16 in range(16):
                    mmx = self.mm(self.pb[0][0:1, 0:512], self.onesb[0:96, 0:1],
                                  ksq[:, j16 * 512:(j16 + 1) * 512], start=True, stop=True,
                                  wait=[msq_, self.bfree(0)], inc=True)
                    self.bank_written(0, mmx)
                    msr = self.op("dve", lambda e, tmp1=tmp1:
                                  e.reduce_max(tmp1, self.pb[0][0:1, 0:512], AX),
                                  wait=[mmx, ms_prev_pe, self.ms("act")], inc="dve")
                    self.bank_read(0, msr)
                    if j16 == 0:
                        ms_nmax = self.op("dve", lambda e, dstmax=dstmax, tmp1=tmp1:
                                          e.tensor_copy(dstmax, tmp1), wait=[msr], inc="dve")
                    else:
                        ms_nmax = self.op("dve", lambda e, dstmax=dstmax, tmp1=tmp1:
                                          e.tensor_max(dstmax, dstmax, tmp1),
                                          wait=[msr], inc="dve")
            msc1 = self.op("dve", lambda e, kmax=kmax, qmax=qmax, tmp1=tmp1:
                           e.tensor_mul(tmp1, kmax, qmax), wait=[ms_nmax], inc="dve")
            msc2 = self.op("act", lambda e, tmp1=tmp1:
                           e.activation(out=tmp1, in_=tmp1, func=AF.Sqrt),
                           wait=[msc1], inc="act")
            msc3 = self.op("dve", lambda e, tmp1=tmp1:
                           e.tensor_scalar_mul(tmp1, tmp1, -ISQ), wait=[msc2], inc="dve")
            mmb = self.mm(self.pb[0][:, 0:1], self.onesf[0:1, 0:128], tmp1,
                          start=True, stop=True, wait=[msc3, self.bfree(0)], inc=True)
            self.bank_written(0, mmb)
            ncc = self.misc2[:, self.COL_NC:self.COL_NC + 1]
            msnc = self.op("dve", lambda e, ncc=ncc:
                           e.tensor_copy(ncc, self.pb[0][:, 0:1]),
                           wait=[mmb, self.ms("act")], inc="dve")
            self.bank_read(0, msnc)

            for qg in range(16):
                mlast = None
                nj = 4 * qg + 4
                for jb in range(0, nj, 4):
                    msS = None
                    widths = []
                    for jj in range(4):
                        j = jb + jj
                        i = j - 4 * qg
                        if i < 0:
                            c0, wdt = qg * 512, 512
                        else:
                            c0, wdt = qg * 512 + i * 128, (4 - i) * 128
                        widths.append((j, i, c0, wdt))
                        sbk = 4 + jj
                        wtS = [ms_ld]
                        if jj == 0:
                            wtS.append(self.bfree(4))
                        msS = self.mm(self.pb[sbk][:, 0:wdt],
                                      self.big2[0:96, j * 128:(j + 1) * 128],
                                      self.big1[0:96, c0:c0 + wdt],
                                      start=True, stop=True,
                                      wait=wtS, inc=(jj == 3))
                    for b4 in range(4, 8):
                        self.bank_written(b4, msS)
                    msE = self.op("act", lambda e:
                                  e.activation(out=self.ptb[:], in_=self.pbs[:],
                                               func=AF.Exp, scale=ISQ,
                                               bias=self.misc2[:, self.COL_NC:self.COL_NC + 1]),
                                  wait=[msS, msnc], inc="act")
                    for b4 in range(4, 8):
                        self.bank_read(b4, msE)
                    msM = None
                    for (j, i, c0, wdt) in widths:
                        jj = j - jb
                        if i >= 0:
                            dsl = self.ptb[:, jj * 512: jj * 512 + 128]
                            msM = self.op("dve", lambda e, dsl=dsl:
                                          e.tensor_mul(dsl, dsl, self.causal[:]),
                                          wait=[msE], inc="dve")
                    for (j, i, c0, wdt) in widths:
                        jj = j - jb
                        wtp = [msE] if msM is None else [msM]
                        if j == 0:
                            wtp.append(self.bfree(2))
                        mlast = self.mm(self.pb[2][0:65, (i * 128 if i > 0 else 0):
                                                   (i * 128 if i > 0 else 0) + wdt],
                                        self.big2[:, 8192 + j * 65: 8192 + (j + 1) * 65],
                                        self.ptb[:, jj * 512: jj * 512 + wdt],
                                        start=(j == 0), stop=(j == nj - 1),
                                        wait=wtp, inc=(j == nj - 1))
                self.bank_written(2, mlast)
                row = self.misc2[0:1, 0:512]
                msn1 = self.op("dve", lambda e, row=row:
                               e.reciprocal(row, self.pb[2][64:65, 0:512]),
                               wait=[mlast, self.ms("pe")], inc="dve")
                mmb2 = self.mm(self.pb[3][0:64, 0:512], self.onesf[0:1, 0:64], row,
                               start=True, stop=True, wait=[msn1, self.bfree(3)], inc=True)
                self.bank_written(3, mmb2)
                msc = self.op("dve", lambda e:
                              e.tensor_copy(self.o2b[:], self.pb[2][0:64, 0:512]),
                              wait=[mlast], inc="dve")
                self.bank_read(2, msc)
                mso = self.op("dve", lambda e:
                              e.tensor_mul(self.stg[0:64, 0:512], self.o2b[:],
                                           self.pb[3][0:64, 0:512]),
                              wait=[mmb2, msc], inc="dve")
                self.bank_read(3, mso)
                src = self.stg[0:64, 0:512].rearrange("p (l t) -> p l t", l=4)
                off = (hl * 64 + 4 * qg) * 8192
                dst = bass.AP(self.ocon_h, off, [[128, 64], [8192, 4], [1, 128]])
                ms_o_dma = self.dma("sp", dst, src, "sstg", wait=[mso])
                self._wait("dve", ms_o_dma[0], ms_o_dma[1])
        self.op("pool", lambda e: e.collective_compute(
            "AllGather", ALU.bypass, replica_groups=[[0, 1, 2, 3], [4, 5, 6, 7]],
            ins=[self.ocon_h.ap().opt()], outs=[self.og_h.ap().opt()]),
            wait=[ms_o_dma], inc="cc2")
        ms_cc2 = self.ms("cc2")
        ms_wo = []
        ms_big1_pe2 = self.ms("pe")
        for k in range(8):
            ms_wo.append(self.dma("pool", self.warena[:, k * 1024:(k + 1) * 1024],
                                  w["bWO"][k * 128:(k + 1) * 128, :], "wg",
                                  wait=[self.ms("pe")]))
        ms_og = []
        for hl in range(4):
            def fno(e, hl=hl):
                off = (hl * 64) * 8192 + self.cg["pool"] * (16 * 8192)
                dims = [[256 * 8192, 4], [8192, 16], [1, 8192]]
                return e.dma_start(out=self.oscr_h.ap()[hl * 64:(hl + 1) * 64, :]
                                   .rearrange("(p l) c -> p l c", p=4),
                                   in_=bass.AP(self.og_h, off, dims))
            ms_og.append(self.op("pool", fno, wait=[ms_cc2, ms_big1_pe2], inc="gm", n=16))
        self._wait("pool", "gm", ms_og[-1][1])
        for p in range(4):
            for hl in range(4):
                h = 4 * p + hl
                src = self.oscr_h.ap()[hl * 64 + p * 16: hl * 64 + (p + 1) * 16, :]
                src = src.rearrange("l (f t) -> f l t", f=64)
                dst = self.y[(h % 2) * 64:(h % 2) * 64 + 64,
                             (h // 2) * TLOC:(h // 2 + 1) * TLOC]
                ms_og.append(self.dma("pool", dst, src, "gm"))
        ms_og_done = ms_og[-1]
        for m_ in range(8):
            for g in range(NG):
                bk = (m_ * NG + g) % 4
                pbk = self.pb[bk]
                mmm = None
                for k in range(8):
                    lhsT = self.warena[:, k * 1024 + m_ * 128: k * 1024 + (m_ + 1) * 128]
                    rhs = self.y[:, k * TLOC + g * 512: k * TLOC + (g + 1) * 512]
                    mmm = self.mm(pbk[:, 0:512], lhsT, rhs, start=(k == 0), stop=(k == 7),
                                  wait=[ms_wo[7], ms_og_done, self.bfree(bk)], inc=(k == 7))
                self.bank_written(bk, mmm)
                xs = self.x[:, m_ * TLOC + g * 512: m_ * TLOC + (g + 1) * 512]
                msd = self.op("dve", lambda e, xs=xs, pbk=pbk:
                              e.tensor_add(xs, xs, pbk[:, 0:512]), wait=[mmm], inc="dve")
                self.bank_read(bk, msd)


    def _mk_hsel_dma(self, view, hl, dst, eng):
        def fn(e, view=view, hl=hl, dst=dst, eng=eng):
            hsel = self.cg[eng] * 4 + hl
            return e.dma_start(out=dst, in_=view[hsel, :, :, :])
        return fn

    def _mk_cg_dma(self, view, dst):
        def fn(e, view=view, dst=dst):
            return e.dma_start(out=dst, in_=view[self.cg["pool"], :, :, :])
        return fn

    def p_store_y(self):
        self._store_stream(self.y, cast=True)

    def p_store(self):
        self._store_stream(self.x, cast=False)

    def _store_stream(self, src, cast):
        ms_x = self.ms("dve")
        x3 = self.xout.rearrange("(t p) d -> t p d", p=128)
        last = None
        ms_half_free = [None, None]
        for ti in range(NT):
            half = ti % 2
            msd = None
            for d in range(8):
                bk = 4 + (d % 4)
                ssl = src[:, d * TLOC + ti * 128: d * TLOC + (ti + 1) * 128]
                if cast:
                    tmp = self.misc2[:, (d % 2) * 128:(d % 2) * 128 + 128]
                    msc = self.op("dve", lambda e, tmp=tmp, ssl=ssl:
                                  e.tensor_copy(tmp, ssl),
                                  wait=[ms_x, self.bfree(bk)], inc="dve")
                    tsrc = tmp
                    twait = [msc]
                else:
                    tsrc = ssl
                    twait = [ms_x, self.bfree(bk)]
                msmm = self.op("pe", lambda e, tsrc=tsrc, bk=bk:
                               e.transpose(self.pb[bk][:, 0:128], tsrc, self.ident[:]),
                               wait=twait, inc="pe")
                self.bank_written(bk, msmm)
                dst = self.fstage[:, half * 1024 + d * 128: half * 1024 + (d + 1) * 128]
                wt = [msmm]
                if d == 0 and ms_half_free[half]:
                    wt.append(ms_half_free[half])
                msd = self.op("dve", lambda e, dst=dst, bk=bk:
                              e.tensor_copy(dst, self.pb[bk][:, 0:128]),
                              wait=wt, inc="dve")
                self.bank_read(bk, msd)
            last = self.dma("sp", x3[ti, :, :],
                            self.fstage[:, half * 1024:(half + 1) * 1024], "sxo",
                            wait=[msd])
            ms_half_free[half] = last
        self._wait("sp", last[0], last[1])


_CACHE = {}


def _build(stage=None):
    key = stage
    if key not in _CACHE:
        kb = KB(stage)
        nc = kb.build()
        _CACHE[key] = nc
    return _CACHE[key]


def _consts(cg):
    ident = np.eye(128, dtype=np.float32)
    pq = np.arange(128)
    causal = (pq[:, None] <= pq[None, :]).astype(ml_dtypes.bfloat16)
    inv = (1.0 / (np.float32(10000.0) **
                  (np.arange(0, DHR, 2, dtype=np.float32) / np.float32(DHR)))).astype(np.float32)
    t = np.arange(S, dtype=np.float32)
    freqs = (t[:, None] * inv[None, :]).astype(np.float32)
    emb = np.concatenate([freqs, freqs], axis=-1)
    tab = (np.cos(emb) + np.sin(emb)).astype(np.float32)
    sl = tab[2048 * cg: 2048 * (cg + 1), :].T
    rope = np.tile(sl, (H, 1)).astype(ml_dtypes.bfloat16)
    return ident, np.ascontiguousarray(causal), np.ascontiguousarray(rope)


def kernel(**inputs):
    stage = inputs.pop("_stage", None)
    nc = _build(stage)
    x = np.asarray(inputs["L0_attn_a_hidden_states"])
    wmap = {
        "aWq": "L0_attn_a_Wq", "aWk": "L0_attn_a_Wk", "aWv": "L0_attn_a_Wv",
        "aproj": "L0_attn_a_proj", "aWo": "L0_attn_a_Wo", "abo": "L0_attn_a_bo",
        "bWDQ": "L0_attn_b_W_DQ", "bWUQ": "L0_attn_b_W_UQ", "bWDKV": "L0_attn_b_W_DKV",
        "bWUK": "L0_attn_b_W_UK", "bWUV": "L0_attn_b_W_UV", "bWQR": "L0_attn_b_W_QR",
        "bWKR": "L0_attn_b_W_KR", "bWO": "L0_attn_b_W_O",
        "fW1": "L0_ffn_W1", "fb1": "L0_ffn_b1", "fW2": "L0_ffn_W2", "fb2": "L0_ffn_b2",
    }
    in_maps = []
    for c in range(8):
        b, cg = c // 4, c % 4
        m = {"xin": np.ascontiguousarray(x[b, 2048 * cg:2048 * (cg + 1), :])}
        for k, src in wmap.items():
            m[k] = np.ascontiguousarray(np.asarray(inputs[src]))
        ident, causal, rope = _consts(cg)
        m["identi"] = ident
        m["causali"] = causal
        m["ropei"] = rope
        in_maps.append(m)
    res = run_bass_kernel_spmd(nc, in_maps, core_ids=list(range(8)))
    out = np.empty((B, S, D), dtype=np.float32)
    for c in range(8):
        b, cg = c // 4, c % 4
        out[b, 2048 * cg:2048 * (cg + 1), :] = res.results[c]["xout"]
    return out


# version 11
# speedup vs baseline: 1.0165x; 1.0165x over previous
import numpy as np
import ml_dtypes
import concourse.bass as bass
import concourse.mybir as mybir
from concourse.bass_utils import run_bass_kernel_spmd

F32 = mybir.dt.float32
BF16 = mybir.dt.bfloat16
AF = mybir.ActivationFunctionType
ALU = mybir.AluOpType
AX = mybir.AxisListType.X

B, S, D = 2, 8192, 1024
H, DH, PF = 16, 64, 256
HS, DHR = 64, 32
DK = HS + DHR
DFF = 4096
DCH = 4096
TLOC = 2048
NT = 16
NG = 4
GT = 64
ISQ = float(1.0 / np.sqrt(np.float32(DK)))
EPS_LN = 1e-5
EPS_A = 1e-8
EPS_Q = 1e-5

ROW_K = 0
ROW_KR = 16384
ROW_V = 16896
ROW_Q = 33536
XCON_ROWS = 58112


class KB:
    ENGS = ("pe", "act", "dve", "pool", "sp")

    def __init__(self, stage=None):
        self.stage = stage
        nc = bass.Bass("TRN2", target_bir_lowering=False)
        self.nc = nc
        self.cmds = {e: [] for e in self.ENGS}
        sem_names = ["pe", "act", "dve", "pool",
                     "sx", "sx2", "sw2a", "sw2b", "sw2c", "sw2d",
                     "sw10", "sw11", "sw12", "sw13", "sw14", "sw15", "sw16", "sw17", "satt", "aatt", "srow",
                     "sr0", "sr1", "sog", "sstg", "svst", "skv", "sscr", "sxo",
                     "wp0", "wp1", "wp2", "wp3", "wg", "gm",
                     "cc0", "cc1", "cc2", "cc3"]
        self.cnt = {s: 0 for s in sem_names}
        self.sem_names = sem_names
        self.last_wait = {}
        self.cg = {}
        self._bw = {}
        self._br = {}

        dt = nc.dram_tensor
        self.xin = dt("xin", [TLOC, D], F32, kind="ExternalInput").ap()
        self.xout = dt("xout", [TLOC, D], F32, kind="ExternalOutput").ap()
        w = {}
        for name, shape in [
            ("aWq", [D, D]), ("aWk", [D, D]), ("aWv", [D, D]), ("aproj", [DH, PF]),
            ("aWo", [D, D]), ("abo", [D]),
            ("bWDQ", [D, DCH]), ("bWUQ", [DCH, D]), ("bWDKV", [D, PF]),
            ("bWUK", [PF, D]), ("bWUV", [PF, D]), ("bWQR", [DCH, H * DHR]),
            ("bWKR", [D, DHR]), ("bWO", [D, D]),
            ("fW1", [DFF, D]), ("fb1", [DFF]), ("fW2", [D, DFF]), ("fb2", [D]),
        ]:
            w[name] = dt(name, shape, F32, kind="ExternalInput").ap()
        self.w = w
        self.identi = dt("identi", [128, 128], F32, kind="ExternalInput").ap()
        self.causali = dt("causali", [128, 128], BF16, kind="ExternalInput").ap()
        self.ropei = dt("ropei", [H * DHR, TLOC], BF16, kind="ExternalInput").ap()

        self.wq1s = dt("wq1s", [DFF, D], BF16)
        self.wq2s = dt("wq2s", [D, DFF], BF16)
        self.acon_h = dt("acon", [H * PF, 65], F32)
        self.ag_h = dt("ag", [4 * H * PF, 65], F32)
        self.kvcon_h = dt("kvcon", [33536, 128], BF16)
        self.kvg_h = dt("kvg", [4 * 33536, 128], BF16)
        self.qcon_h = dt("qcon", [24576, 128], BF16)
        self.qg2_h = dt("qg2", [4 * 24576, 128], BF16)
        self.ocon_h = dt("ocon", [4 * GT, 8192], BF16)
        self.og_h = dt("og", [16 * GT, 8192], BF16)
        self.wdq_bf = dt("wdq_bf", [D, DCH], BF16)
        self.wuq_bf = dt("wuq_bf", [DCH, D], BF16)
        self.wqr_bf = dt("wqr_bf", [DCH, H * DHR], BF16)
        self.kscr_h = dt("kscr", [4 * 1024, 128], BF16)
        self.vscr_h = dt("vscr", [4 * 1040, 128], BF16)
        self.qscr_h = dt("qscr", [4 * 1536, 128], BF16)
        self.oscr_h = dt("oscr", [4 * GT, 8192], BF16)

        from contextlib import ExitStack
        self.es = ExitStack()
        sb = lambda n, shp, d: self.es.enter_context(nc.sbuf_tensor(n, shp, d))
        self.x = sb("x_fm", [128, 8 * TLOC], F32)
        self.y = sb("y_fm", [128, 8 * TLOC], BF16)
        self.big1 = sb("big1", [128, 16384], BF16)
        self.big2 = sb("big2", [128, 16640], BF16)
        self.warena = sb("warena", [128, 8192], BF16)
        self.fstage = sb("fstage", [128, 2048], F32)
        self.misc2 = sb("misc2", [128, 1536], F32)
        self.ident = sb("ident", [128, 128], F32)
        self.causal = sb("causal", [128, 128], BF16)
        self.onesb = sb("onesb", [128, 128], BF16)
        self.onesf = sb("onesf", [1, 128], F32)
        self.ones128f = sb("ones128f", [128, 128], F32)
        self.projb = sb("projb", [128, PF], BF16)
        self.ptb = sb("ptb", [128, 2048], BF16)
        self.stg = sb("stg", [128, 1024], BF16)
        self.o2b = sb("o2b", [64, 512], BF16)
        self.kvb = sb("kvb", [128, 2 * H * 65], BF16)
        self.vstg = self.kvb
        self.ropeb = sb("ropeb", [128, 1024], BF16)
        self.constv = sb("constv", [128, 8], F32)
        nc.const_aps.register(F32, EPS_LN, self.constv[:, 0:1])
        nc.const_aps.register(F32, -0.5, self.constv[:, 1:2])
        nc.const_aps.register(F32, 0.5, self.constv[:, 2:3])
        self.pb = [self.es.enter_context(nc.psum_tensor(f"pb{i}", [128, 512], F32))
                   for i in range(4)]
        self.pbs = self.es.enter_context(nc.psum_tensor("pbs", [128, 2048], F32))
        self.pb += [self.pbs[:, k * 512:(k + 1) * 512] for k in range(4)]
        self.sems = {s: self.es.enter_context(nc.semaphore(s)) for s in sem_names}

        self.COL_ROW = 0
        self.COL_KV = 1024
        self.COL_B1 = 1160
        self.COL_B2 = 1192
        self.COL_BO = 1200
        self.COL_AR = 1208
        self.COL_G = 1240
        self.COL_NC = 1246

    def _wait(self, eng, sem, val):
        if val is None or val <= 0:
            return
        key = (eng, sem)
        if self.last_wait.get(key, 0) >= val:
            return
        self.last_wait[key] = val
        self.cmds[eng].append(("w", sem, val))

    def op(self, eng, fn, wait=(), inc=None, n=1):
        for m in wait:
            if m is None:
                continue
            s, v = m
            self._wait(eng, s, v)
        if inc is None:
            self.cmds[eng].append(("i", fn, None, 0))
            return None
        self.cnt[inc] += n
        ms = (inc, self.cnt[inc])
        self.cmds[eng].append(("i", fn, inc, n))
        return ms

    def ms(self, sem):
        return (sem, self.cnt[sem])

    def dma(self, eng, out, in_, sem, wait=(), **kw):
        return self.op(eng, lambda e, o=out, i=in_, k=dict(kw): e.dma_start(out=o, in_=i, **k),
                       wait=wait, inc=sem, n=16)

    def dmat(self, eng, out, in_, sem, wait=()):
        return self.op(eng, lambda e, o=out, i=in_: e.dma_start_transpose(o, i),
                       wait=wait, inc=sem, n=16)

    def mm(self, out, lhsT, rhs, start, stop, wait=(), inc=False):
        return self.op("pe",
                       lambda e, o=out, l=lhsT, r=rhs, st=start, sp=stop:
                       e.matmul(o, l, r, start=st, stop=sp),
                       wait=wait, inc=("pe" if inc else None))

    def bank_written(self, i, ms):
        self._bw[i] = ms

    def bank_read(self, i, ms):
        self._br[i] = ms

    def bfree(self, i):
        return self._br.get(i)

    def bready(self, i):
        return self._bw.get(i)

    def build(self):
        self.emit_all()
        nc = self.nc
        with nc.Block() as block:
            def runner(eng_name):
                def run(e):
                    if eng_name in ("sp", "act", "pool"):
                        cid = e.partition_id()
                        self.cg[eng_name] = cid % 4
                    for cmd in self.cmds[eng_name]:
                        if cmd[0] == "w":
                            e.wait_ge(self.sems[cmd[1]], cmd[2])
                        else:
                            _, fn, inc, n = cmd
                            ins = fn(e)
                            if inc is not None:
                                ins.then_inc(self.sems[inc], n)
                return run
            block.tensor(runner("pe"))
            block.scalar(runner("act"))
            block.vector(runner("dve"))
            block.gpsimd(runner("pool"))
            block.sync(runner("sp"))
        return nc

    def emit_all(self):
        st = self.stage
        if st == "wq1":
            self.p_init()
            self.p_quantize()
            m = self.dma("pool", self.xout, self.wq1s.ap()[0:2048, :], "gm",
                         wait=[self.ms_wq_scr])
            self._wait("pool", m[0], m[1])
            return
        self.p_init()
        self.p_load_x()
        if st == "x0":
            self.p_store(); return
        self.emit_ln()
        if st == "ln1":
            self.p_store_y(); return
        self.p_attn_a()
        if st in ("kv", "aout", "acon", "vtm", "kfm"):
            return
        if st == "x1":
            self.p_store(); return
        self.emit_ln()
        self.p_ffn()
        if st == "x2":
            self.p_store(); return
        self.emit_ln()
        self.p_attn_b()
        if st == "x3":
            self.p_store(); return
        self.emit_ln()
        self.p_ffn()
        self.p_store()

    def p_init(self):
        self.dma("sp", self.ident[:], self.identi, "srow")
        self.dma("sp", self.causal[:], self.causali, "srow")
        self.op("pool", lambda e: e.memset(self.constv[:, 0:1], EPS_LN))
        self.op("pool", lambda e: e.memset(self.constv[:, 1:2], -0.5))
        self.op("pool", lambda e: e.memset(self.constv[:, 2:3], 0.5))
        self.op("pool", lambda e: e.memset(self.onesb[:], 1.0))
        self.op("pool", lambda e: e.memset(self.onesf[:], 1.0))
        self.op("pool", lambda e: e.memset(self.ones128f[:], 1.0), inc="pool")
        self.ms_ones = self.ms("pool")
        self.dma("pool", self.projb[0:64, :], self.w["aproj"], "gm")
        self.dma("pool", self.projb[64:128, :], self.w["aproj"], "gm")
        self.ms_proj = self.ms("gm")

    def p_load_x(self):
        x3 = self.xin.rearrange("(t p) d -> t p d", p=128)
        ms_fs_free = [None, None]
        msd = None
        for ti in range(NT):
            half = ti % 2
            fs = self.fstage[:, half * 1024:(half + 1) * 1024]
            msl = self.dma("sp", fs, x3[ti, :, :], "sx" if half == 0 else "sx2",
                           wait=[ms_fs_free[half]] if ms_fs_free[half] else [])
            for d in range(8):
                bk = 4 + (d % 4)
                fsl = fs[:, d * 128:(d + 1) * 128]
                msmm = self.op("pe", lambda e, fsl=fsl, bk=bk:
                               e.transpose(self.pb[bk][:, 0:128], fsl, self.ident[:]),
                               wait=[msl, ("srow", 32), self.bfree(bk)], inc="pe")
                self.bank_written(bk, msmm)
                dst = self.x[:, d * TLOC + ti * 128: d * TLOC + (ti + 1) * 128]
                msd = self.op("dve", lambda e, dst=dst, bk=bk:
                              e.tensor_copy(dst, self.pb[bk][:, 0:128]),
                              wait=[msmm], inc="dve")
                self.bank_read(bk, msd)
                if d == 7:
                    ms_fs_free[half] = msmm

    def emit_ln(self):
        ms_x = self.ms("dve")
        ms_sx0 = self.ms("sx")
        ms_sx20 = self.ms("sx2")
        for g in range(NG):
            xb = self.big1[:, 0:4096]
            xb2 = self.big1[:, 4096:8192]
            msc = None
            for d in range(8):
                src = self.x[:, d * TLOC + g * 512: d * TLOC + (g + 1) * 512]
                dst = xb[:, d * 512:(d + 1) * 512]
                msc = self.op("act", lambda e, src=src, dst=dst:
                              e.activation(out=dst, in_=src, func=AF.Copy),
                              wait=[ms_x, self.ms("pe")], inc="act")
            mss = self.op("dve", lambda e, xb=xb, xb2=xb2:
                          e.tensor_mul(xb2, xb, xb), wait=[msc], inc="dve")
            m1 = None
            for d in range(8):
                m1 = self.mm(self.pb[4][:, 0:512], self.onesb[:, 0:128],
                             xb[:, d * 512:(d + 1) * 512], start=(d == 0), stop=(d == 7),
                             wait=[msc, self.ms_ones, self.bfree(4)], inc=(d == 7))
            self.bank_written(4, m1)
            m2 = None
            for d in range(8):
                m2 = self.mm(self.pb[5][:, 0:512], self.onesb[:, 0:128],
                             xb2[:, d * 512:(d + 1) * 512], start=(d == 0), stop=(d == 7),
                             wait=[mss, self.bfree(5)], inc=(d == 7))
            self.bank_written(5, m2)
            mb = self.fstage[:, 0:512]
            vr = self.fstage[:, 512:1024]
            ms3 = self.op("dve", lambda e, mb=mb:
                          e.tensor_scalar_mul(mb, self.pb[4][:, 0:512], 1.0 / D),
                          wait=[m1, ms_sx0, ms_sx20], inc="dve")
            self.bank_read(4, ms3)
            ms4 = self.op("dve", lambda e, mb=mb, vr=vr: e.tensor_mul(vr, mb, mb),
                          wait=[ms3], inc="dve")
            ms5 = self.op("dve", lambda e, vr=vr:
                          e.scalar_tensor_tensor(vr, self.pb[5][:, 0:512], 1.0 / D, vr,
                                                 ALU.mult, ALU.subtract),
                          wait=[m2, ms4], inc="dve")
            self.bank_read(5, ms5)
            ms6 = self.op("act", lambda e, vr=vr:
                          e.activation(out=vr, in_=vr, func=AF.Sqrt, bias=EPS_LN),
                          wait=[ms5, self.ms_ones], inc="act")
            ms7 = self.op("dve", lambda e, vr=vr: e.reciprocal(vr, vr),
                          wait=[ms6], inc="dve")
            for d in range(8):
                xs = self.x[:, d * TLOC + g * 512: d * TLOC + (g + 1) * 512]
                ys = self.y[:, d * TLOC + g * 512: d * TLOC + (g + 1) * 512]
                tmp = self.fstage[:, 1024 + 512 * (d % 2): 1536 + 512 * (d % 2)]
                msa = self.op("dve", lambda e, xs=xs, tmp=tmp, mb=mb:
                              e.tensor_sub(tmp, xs, mb), wait=[ms3], inc="dve")
                self.op("dve", lambda e, ys=ys, tmp=tmp, vr=vr:
                        e.tensor_mul(ys, tmp, vr), wait=[ms7, msa], inc="dve")
        self.ms_y = self.ms("dve")

    def p_attn_a(self):
        w = self.w
        ms_y = self.ms_y
        self.dma("pool", self.wdq_bf.ap(), w["bWDQ"], "gm")
        self.dma("pool", self.wuq_bf.ap(), w["bWUQ"], "gm")
        self.dma("pool", self.wqr_bf.ap(), w["bWQR"], "gm")
        self.ms_wbf = self.ms("gm")
        msq = self.op("pool", lambda e: e.memset(self.big2[:], 1.0), inc="pool")
        ms_wv = []
        for k in range(8):
            ms_wv.append(self.dma("pool", self.warena[:, k * 1024:(k + 1) * 1024],
                                  w["aWv"][k * 128:(k + 1) * 128, :], "wg"))
        ms_w_last = None
        for ti in range(NT):
            for eh in range(2):
                bk = (ti * 2 + eh) % 4
                pbk = self.pb[bk]
                mmm = None
                for k in range(8):
                    lhsT = self.y[:, k * TLOC + ti * 128: k * TLOC + (ti + 1) * 128]
                    rhs = self.warena[:, k * 1024 + eh * 512: k * 1024 + (eh + 1) * 512]
                    mmm = self.mm(pbk[:, 0:512], lhsT, rhs, start=(k == 0), stop=(k == 7),
                                  wait=[ms_y, ms_wv[7], self.bfree(bk)], inc=(k == 7))
                self.bank_written(bk, mmm)
                ms_w_last = mmm
                msd = None
                for h8 in range(8):
                    h = eh * 8 + h8
                    dst = self.big2[:, ti * 1040 + h * 65: ti * 1040 + h * 65 + 64]
                    msd = self.op("dve", lambda e, dst=dst, pbk=pbk, h8=h8:
                                  e.tensor_copy(dst, pbk[:, h8 * 64:(h8 + 1) * 64]),
                                  wait=[mmm, msq] if h8 == 0 else [], inc="dve")
                self.bank_read(bk, msd)
        ms_v_done = self.ms("dve")
        if self.stage == "vtm":
            xo = self.xout.rearrange("(a p) d -> p a d", p=128)
            m2 = self.dma("pool", xo[:, 0:16, :],
                          self.big2[:, 0:16640].rearrange("p (a d) -> p a d", a=16, d=1040)[:, :, 0:1024],
                          "gm", wait=[ms_v_done])
            self._wait("pool", m2[0], m2[1])
            return
        ms_wk = []
        for k in range(8):
            ms_wk.append(self.dma("pool", self.warena[:, k * 1024:(k + 1) * 1024],
                                  w["aWk"][k * 128:(k + 1) * 128, :], "wg",
                                  wait=[ms_w_last]))
        for m_ in range(8):
            for g in range(NG):
                bk = (m_ * NG + g) % 4
                pbk = self.pb[bk]
                mmm = None
                for k in range(8):
                    lhsT = self.warena[:, k * 1024 + m_ * 128: k * 1024 + (m_ + 1) * 128]
                    rhs = self.y[:, k * TLOC + g * 512: k * TLOC + (g + 1) * 512]
                    mmm = self.mm(pbk[:, 0:512], lhsT, rhs, start=(k == 0), stop=(k == 7),
                                  wait=[ms_wk[7], self.bfree(bk)], inc=(k == 7))
                self.bank_written(bk, mmm)
                ms_w_last = mmm
                dst = self.big1[:, m_ * TLOC + g * 512: m_ * TLOC + (g + 1) * 512]
                msd = self.op("dve", lambda e, dst=dst, pbk=pbk:
                              e.tensor_copy(dst, pbk[:, 0:512]), wait=[mmm], inc="dve")
                self.bank_read(bk, msd)
        ms_k_done = self.ms("dve")
        if self.stage == "kfm":
            xo = self.xout.rearrange("(a p) d -> p a d", p=128)
            m2 = self.dma("pool", xo[:, 0:16, :],
                          self.big1[:, 0:16384].rearrange("p (a d) -> p a d", a=16),
                          "gm", wait=[ms_k_done])
            self._wait("pool", m2[0], m2[1])
            return
        for h in range(H):
            mkv = None
            for ti in range(NT):
                bk = ti % 4
                pbk = self.pb[bk]
                lhsT = self.big1[(h % 2) * 64:(h % 2) * 64 + 64,
                                 (h // 2) * TLOC + ti * 128:(h // 2) * TLOC + (ti + 1) * 128]
                mm1 = self.mm(pbk[:, 0:256], lhsT,
                              self.projb[(h % 2) * 64:(h % 2) * 64 + 64, :],
                              start=True, stop=True,
                              wait=[ms_k_done, self.ms_proj, self.bfree(bk)], inc=True)
                self.bank_written(bk, mm1)
                stag = self.stg[:, (ti % 2) * 256: (ti % 2) * 256 + 256]
                msr = self.op("dve", lambda e, stag=stag, pbk=pbk:
                              e.tensor_relu(stag, pbk[:, 0:256]), wait=[mm1], inc="dve")
                self.bank_read(bk, msr)
                for fc in range(2):
                    kvbk = 6 + fc
                    mkv = self.mm(self.pb[kvbk][:, 0:65],
                                  stag[:, fc * 128:(fc + 1) * 128],
                                  self.big2[:, ti * 1040 + h * 65: ti * 1040 + (h + 1) * 65],
                                  start=(ti == 0), stop=(ti == NT - 1),
                                  wait=([msr, ms_v_done, self.bfree(kvbk)] if ti == 0
                                        else [msr]),
                                  inc=(ti == NT - 1))
                    if ti == NT - 1:
                        self.bank_written(kvbk, mkv)
            for fc in range(2):
                kvbk = 6 + fc
                col = self.COL_KV + fc * 65
                msd = self.op("dve", lambda e, col=col, kvbk=kvbk:
                              e.tensor_copy(self.misc2[:, col:col + 65],
                                            self.pb[kvbk][:, 0:65]),
                              wait=[self.bready(kvbk)], inc="dve")
                self.bank_read(kvbk, msd)
                mda = self.dma("sp", self.acon_h[h * 256 + fc * 128: h * 256 + (fc + 1) * 128, :],
                               self.misc2[:, col:col + 65], "skv", wait=[msd])
                self._wait("dve", mda[0], mda[1])
        ms_acon = self.ms("skv")
        if self.stage == "acon":
            self._wait("pool", "skv", ms_acon[1])
            m2 = self.dma("pool", self.xout[0:2048, 0:65],
                          self.acon_h.ap()[0:2048, :], "gm", wait=[])
            self._wait("pool", m2[0], m2[1])
            return

        ms_wq = []
        for k in range(8):
            ms_wq.append(self.dma("pool", self.warena[:, k * 1024:(k + 1) * 1024],
                                  w["aWq"][k * 128:(k + 1) * 128, :], "wg",
                                  wait=[ms_w_last]))
        ms_k_pe = self.ms("pe")
        self.op("pool", lambda e: e.collective_compute(
            "AllGather", ALU.bypass, replica_groups=[[0, 1, 2, 3], [4, 5, 6, 7]],
            ins=[self.acon_h.ap().opt()], outs=[self.ag_h.ap().opt()]),
            wait=[ms_acon], inc="cc0")
        ms_cc0 = self.ms("cc0")
        self.p_quantize(qbufs=[(self.ropeb[:, 0:1024], self.kvb[:, 0:1024]),
                               (self.kvb[:, 1024:2048], self.stg[:, 0:1024])],
                        pe_guard=self.ms("pe"))
        self._wait("dve", "sscr", self.ms("sscr")[1])
        for m_ in range(8):
            for g in range(NG):
                bk = (m_ * NG + g) % 4
                pbk = self.pb[bk]
                mmm = None
                for k in range(8):
                    lhsT = self.warena[:, k * 1024 + m_ * 128: k * 1024 + (m_ + 1) * 128]
                    rhs = self.y[:, k * TLOC + g * 512: k * TLOC + (g + 1) * 512]
                    wt = [ms_wq[7], self.bfree(bk)]
                    if m_ == 0 and g == 0:
                        wt.append(ms_k_pe)
                    mmm = self.mm(pbk[:, 0:512], lhsT, rhs, start=(k == 0), stop=(k == 7),
                                  wait=wt, inc=(k == 7))
                self.bank_written(bk, mmm)
                ms_w_last = mmm
                dst = self.big1[:, m_ * TLOC + g * 512: m_ * TLOC + (g + 1) * 512]
                msd = self.op("dve", lambda e, dst=dst, pbk=pbk:
                              e.tensor_copy(dst, pbk[:, 0:512]), wait=[mmm], inc="dve")
                self.bank_read(bk, msd)
        ms_q_done = self.ms("dve")

        ag4 = self.ag_h.ap().rearrange("(r hf p) c -> r p hf c", r=4, p=128)
        ms_v_pe = self.ms("pe")
        msl = []
        for r in range(4):
            dst = self.big2[:, r * 2080:(r + 1) * 2080]
            msl.append(self.dma("pool", dst, ag4[r], "gm", wait=[ms_cc0, ms_v_pe]))
        msadd = self.op("dve", lambda e: e.tensor_add(self.kvb[:], self.big2[:, 0:2080],
                                                      self.big2[:, 2080:4160]),
                        wait=[msl[0], msl[1]], inc="dve")
        msadd = self.op("dve", lambda e: e.tensor_add(self.kvb[:], self.kvb[:],
                                                      self.big2[:, 4160:6240]),
                        wait=[msl[2]], inc="dve")
        msadd = self.op("dve", lambda e: e.tensor_add(self.kvb[:], self.kvb[:],
                                                      self.big2[:, 6240:8320]),
                        wait=[msl[3]], inc="dve")
        if self.stage == "kv":
            xo = self.xout.rearrange("(a p) d -> p a d", p=128)
            m2 = self.dma("pool", xo[:, 0:2, :], self.kvb[:, 0:2048], "gm", wait=[msadd])
            self._wait("pool", m2[0], m2[1])
            return

        for h in range(H):
            for g in range(NG):
                msr = None
                for fc in range(2):
                    bk = ((h * NG + g) % 2) * 2 + fc
                    pbk = self.pb[bk]
                    lhsT = self.projb[(h % 2) * 64:(h % 2) * 64 + 64,
                                      fc * 128:(fc + 1) * 128]
                    rhs = self.big1[(h % 2) * 64:(h % 2) * 64 + 64,
                                    (h // 2) * TLOC + g * 512:(h // 2) * TLOC + (g + 1) * 512]
                    mm1 = self.mm(pbk[:, 0:512], lhsT, rhs, start=True, stop=True,
                                  wait=[ms_q_done, self.bfree(bk)], inc=True)
                    self.bank_written(bk, mm1)
                    dst = self.stg[:, fc * 512:(fc + 1) * 512]
                    msr = self.op("dve", lambda e, dst=dst, pbk=pbk:
                                  e.tensor_relu(dst, pbk[:, 0:512]), wait=[mm1], inc="dve")
                    self.bank_read(bk, msr)
                mma = mmn = None
                for fc in range(2):
                    lk = self.kvb[:, (2 * h + fc) * 65:(2 * h + fc) * 65 + 64]
                    ln_ = self.kvb[:, (2 * h + fc) * 65 + 64:(2 * h + fc) * 65 + 65]
                    rhs = self.stg[:, fc * 512:(fc + 1) * 512]
                    wt = [msadd, msr]
                    if fc == 0:
                        wt.append(self.bfree(2))
                    mma = self.mm(self.pb[2][0:64, 0:512], lk, rhs,
                                  start=(fc == 0), stop=(fc == 1), wait=wt, inc=(fc == 1))
                    wt2 = [self.bfree(3)] if fc == 0 else []
                    mmn = self.mm(self.pb[3][0:1, 0:512], ln_, rhs,
                                  start=(fc == 0), stop=(fc == 1), wait=wt2, inc=(fc == 1))
                self.bank_written(2, mma)
                self.bank_written(3, mmn)
                row = self.misc2[0:1, 0:512]
                msn1 = self.op("dve", lambda e, row=row:
                               e.tensor_scalar_add(row, self.pb[3][0:1, 0:512], EPS_A),
                               wait=[mmn, self.ms("pe")], inc="dve")
                self.bank_read(3, msn1)
                msn2 = self.op("dve", lambda e, row=row: e.reciprocal(row, row),
                               wait=[msn1], inc="dve")
                mmb = self.mm(self.pb[3][0:64, 0:512], self.onesf[0:1, 0:64], row,
                              start=True, stop=True, wait=[msn2], inc=True)
                self.bank_written(3, mmb)
                msc = self.op("dve", lambda e:
                              e.tensor_copy(self.o2b[:], self.pb[2][0:64, 0:512]),
                              wait=[mma], inc="dve")
                self.bank_read(2, msc)
                ydst = self.y[(h % 2) * 64:(h % 2) * 64 + 64,
                              (h // 2) * TLOC + g * 512:(h // 2) * TLOC + (g + 1) * 512]
                msm = self.op("dve", lambda e, ydst=ydst:
                              e.tensor_mul(ydst, self.o2b[:], self.pb[3][0:64, 0:512]),
                              wait=[mmb, msc], inc="dve")
                self.bank_read(3, msm)
        ms_attn_out = self.ms("dve")
        if self.stage == "aout":
            self.p_store_y()
            return

        ms_wo = []
        for k in range(8):
            ms_wo.append(self.dma("pool", self.warena[:, k * 1024:(k + 1) * 1024],
                                  w["aWo"][k * 128:(k + 1) * 128, :], "wg",
                                  wait=[ms_w_last]))
        bo_r = self.w["abo"].rearrange("(a d) -> a d", a=1)
        msbo = self.dma("sp", self.fstage[0:1, 0:1024], bo_r, "srow",
                        wait=[self.ms("dve")])
        self._wait("pe", msbo[0], msbo[1])
        self.p_biascol_from_row(self.fstage, 8, self.COL_BO, 5)
        ms_bo = self.ms("dve")
        for m_ in range(8):
            self.op("dve", lambda e, m_=m_:
                    e.tensor_scalar_add(self.x[:, m_ * TLOC:(m_ + 1) * TLOC],
                                        self.x[:, m_ * TLOC:(m_ + 1) * TLOC],
                                        self.misc2[:, self.COL_BO + m_:self.COL_BO + m_ + 1]),
                    wait=[ms_bo], inc="dve")
        for m_ in range(8):
            for g in range(NG):
                bk = (m_ * NG + g) % 4
                pbk = self.pb[bk]
                mmm = None
                for k in range(8):
                    lhsT = self.warena[:, k * 1024 + m_ * 128: k * 1024 + (m_ + 1) * 128]
                    rhs = self.y[:, k * TLOC + g * 512: k * TLOC + (g + 1) * 512]
                    mmm = self.mm(pbk[:, 0:512], lhsT, rhs, start=(k == 0), stop=(k == 7),
                                  wait=[ms_wo[7], ms_attn_out, self.bfree(bk)], inc=(k == 7))
                self.bank_written(bk, mmm)
                xs = self.x[:, m_ * TLOC + g * 512: m_ * TLOC + (g + 1) * 512]
                msd = self.op("dve", lambda e, xs=xs, pbk=pbk:
                              e.tensor_add(xs, xs, pbk[:, 0:512]), wait=[mmm], inc="dve")
                self.bank_read(bk, msd)
        self.ms_warena_last = self.ms("pe")

    def p_biascol_from_row(self, row_t, ncols, dst_col0, bank):
        for m in range(ncols):
            pbk = self.pb[bank]
            msmm = self.mm(pbk[:, 0:1], row_t[0:1, m * 128:(m + 1) * 128],
                           self.onesf[0:1, 0:1], start=True, stop=True,
                           wait=[self.bfree(bank), self.ms_ones], inc=True)
            self.bank_written(bank, msmm)
            msd = self.op("dve", lambda e, m=m, c=dst_col0, pbk=pbk:
                          e.tensor_copy(self.misc2[:, c + m:c + m + 1], pbk[:, 0:1]),
                          wait=[msmm], inc="dve")
            self.bank_read(bank, msd)

    def p_quantize(self, qbufs=None, pe_guard=None):
        if qbufs is None:
            qbufs = [(self.stg[:, 0:1024], self.ptb[:, 0:1024]),
                     (self.big1[:, 0:1024], self.big1[:, 1024:2048])]
        for wi in range(2):
            wname = "fW1" if wi == 0 else "fW2"
            scr = self.wq1s if wi == 0 else self.wq2s
            rcol = self.COL_G if wi == 0 else self.COL_G + 1
            W = self.w[wname]
            rows = DFF if wi == 0 else D
            cols = D if wi == 0 else DFF
            nr, ncb = rows // 128, cols // 1024
            blocks = [(r, c) for r in range(nr) for c in range(ncb)]
            ms_fs_free = [self.ms("dve"), self.ms("dve")]
            msr = None
            for bi, (r, c) in enumerate(blocks):
                half = bi % 2
                fs = self.fstage[:, half * 1024:(half + 1) * 1024]
                msl = self.dma("sp", fs, W[r * 128:(r + 1) * 128, c * 1024:(c + 1) * 1024],
                               "sx" if half == 0 else "sx2",
                               wait=[ms_fs_free[half]] if ms_fs_free[half] else [])
                col = self.COL_AR + bi
                msr = self.op("dve", lambda e, fs=fs, col=col:
                              e.tensor_reduce(self.misc2[:, col:col + 1], fs, AX, ALU.add,
                                              apply_absolute_value=True),
                              wait=[msl], inc="dve")
                ms_fs_free[half] = msr
            mst = self.op("dve", lambda e:
                          e.tensor_reduce(self.misc2[:, self.COL_G + 2:self.COL_G + 3],
                                          self.misc2[:, self.COL_AR:self.COL_AR + 32],
                                          AX, ALU.add), wait=[msr], inc="dve")
            mmg = self.mm(self.pb[5][:, 0:1], self.ones128f[:],
                          self.misc2[:, self.COL_G + 2:self.COL_G + 3],
                          start=True, stop=True,
                          wait=[mst, self.bfree(5), self.ms_ones], inc=True)
            self.bank_written(5, mmg)
            msg1 = self.op("dve", lambda e, rcol=rcol:
                           e.tensor_scalar(self.misc2[:, rcol:rcol + 1], self.pb[5][:, 0:1],
                                           1.0 / (DFF * D), EPS_Q, ALU.mult, ALU.add),
                           wait=[mmg], inc="dve")
            self.bank_read(5, msg1)
            msg2 = self.op("dve", lambda e, rcol=rcol:
                           e.reciprocal(self.misc2[:, rcol:rcol + 1],
                                        self.misc2[:, rcol:rcol + 1]),
                           wait=[msg1], inc="dve")
            ms_fs_free = [self.ms("dve"), self.ms("dve")]
            ms_set_free = [None, None]
            for bi, (r, c) in enumerate(blocks):
                half = bi % 2
                fs = self.fstage[:, half * 1024:(half + 1) * 1024]
                msl = self.dma("sp", fs, W[r * 128:(r + 1) * 128, c * 1024:(c + 1) * 1024],
                               "sx" if half == 0 else "sx2",
                               wait=[ms_fs_free[half]] if ms_fs_free[half] else [])
                s1, s2 = qbufs[half]
                rc = self.misc2[:, rcol:rcol + 1]
                wt = [msl, msg2, self.ms_ones]
                if pe_guard is not None:
                    wt.append(pe_guard)
                if ms_set_free[half]:
                    wt.append(ms_set_free[half])
                ma = self.op("act", lambda e, fs=fs, s1=s1, rc=rc:
                             e.activation(out=s1, in_=fs, func=AF.Sign, scale=rc, bias=-0.5),
                             wait=wt, inc="act")
                mb = self.op("act", lambda e, fs=fs, s2=s2, rc=rc:
                             e.activation(out=s2, in_=fs, func=AF.Sign, scale=rc, bias=0.5),
                             wait=[], inc="act")
                ms_fs_free[half] = mb
                mc = self.op("dve", lambda e, s1=s1, s2=s2:
                             e.tensor_add(s1, s1, s2), wait=[ma, mb], inc="dve")
                if wi == 0:
                    dst = scr.ap()[r * 128:(r + 1) * 128, :]
                else:
                    dst = scr.ap()[r * 128:(r + 1) * 128, c * 1024:(c + 1) * 1024]
                msw = self.dma("sp", dst, s1, "sscr", wait=[mc])
                ms_set_free[half] = msw
        self.ms_wq_scr = self.ms("sscr")

    def p_ffn(self):
        ms_y = self.ms("dve")
        if not getattr(self, "_bias_built", False):
            self._bias_built = True
            b1r = self.w["fb1"].rearrange("(a d) -> a d", a=1)
            m1 = self.dma("sp", self.fstage[0:1, 0:2048], b1r[0:1, 0:2048], "srow",
                          wait=[self.ms("dve")])
            self._wait("pe", m1[0], m1[1])
            self.p_biascol_from_row(self.fstage, 16, self.COL_B1, 5)
            m2 = self.dma("sp", self.fstage[0:1, 0:2048], b1r[0:1, 2048:4096], "srow",
                          wait=[self.ms("pe")])
            self._wait("pe", m2[0], m2[1])
            self.p_biascol_from_row(self.fstage, 16, self.COL_B1 + 16, 5)
            b2r = self.w["fb2"].rearrange("(a d) -> a d", a=1)
            m3 = self.dma("sp", self.fstage[0:1, 0:1024], b2r, "srow",
                          wait=[self.ms("pe")])
            self._wait("pe", m3[0], m3[1])
            self.p_biascol_from_row(self.fstage, 8, self.COL_B2, 5)
        ms_bias = self.ms("dve")
        for m_ in range(8):
            self.op("dve", lambda e, m_=m_:
                    e.tensor_scalar_add(self.x[:, m_ * TLOC:(m_ + 1) * TLOC],
                                        self.x[:, m_ * TLOC:(m_ + 1) * TLOC],
                                        self.misc2[:, self.COL_B2 + m_:self.COL_B2 + m_ + 1]),
                    wait=[ms_bias], inc="dve")
        ms_b2 = self.ms("dve")

        ms_big2_pe = getattr(self, "ms_big2_free", None) or self.ms("pe")
        self.ms_big2_free = None
        self.ffn_g_ms = {}
        for g in range(NG):
            for fh in range(2):
                ms_w1 = []
                for d in range(8):
                    src = self.wq1s.ap()[fh * 2048:(fh + 1) * 2048, d * 128:(d + 1) * 128]
                    m = self.dmat("sp", self.big2[:, d * 2048:(d + 1) * 2048], src,
                                  "sw1%d" % d, wait=[self.ms_wq_scr, ms_big2_pe])
                    ms_w1.append(m)
                for fl in range(16):
                    f = fh * 16 + fl
                    bk = f % 4
                    pbk = self.pb[bk]
                    mmm = None
                    for d in range(8):
                        lhsT = self.big2[:, d * 2048 + fl * 128: d * 2048 + (fl + 1) * 128]
                        rhs = self.y[:, d * TLOC + g * 512: d * TLOC + (g + 1) * 512]
                        mmm = self.mm(pbk[:, 0:512], lhsT, rhs,
                                      start=(d == 0), stop=(d == 7),
                                      wait=[ms_w1[d], ms_y, self.bfree(bk)], inc=(d == 7))
                    self.bank_written(bk, mmm)
                    if fl == 15:
                        ms_big2_pe = mmm
                    dst = self.big1[:, f * 512:(f + 1) * 512]
                    b1c = self.misc2[:, self.COL_B1 + f:self.COL_B1 + f + 1]
                    msh = self.op("act", lambda e, dst=dst, pbk=pbk, b1c=b1c:
                                  e.activation(out=dst, in_=pbk[:, 0:512], func=AF.Relu,
                                               scale=0.5, bias=b1c),
                                  wait=[mmm], inc="act")
                    self.bank_read(bk, msh)
            ms_h = self.ms("act")
            ms_w2use = [self.ms("pe")] * 4
            w2sems = ["sw2a", "sw2b", "sw2c", "sw2d"]
            mlast = None
            for f in range(32):
                slot = f % 4
                wsl = self.warena[:, slot * 1024:(slot + 1) * 1024]
                wt = [self.ms_wq_scr]
                if ms_w2use[slot]:
                    wt.append(ms_w2use[slot])
                mw = self.dmat("sp", wsl, self.wq2s.ap()[:, f * 128:(f + 1) * 128],
                               w2sems[slot], wait=wt)
                for m_ in range(8):
                    wtm = [mw, ms_h]
                    if f == 0:
                        wtm.append(self.bfree(m_))
                    mlast = self.mm(self.pb[m_][:, 0:512],
                                    wsl[:, m_ * 128:(m_ + 1) * 128],
                                    self.big1[:, f * 512:(f + 1) * 512],
                                    start=(f == 0), stop=(f == 31),
                                    wait=wtm, inc=(m_ == 7))
                ms_w2use[slot] = mlast
            for m_ in range(8):
                self.bank_written(m_, mlast)
                xs = self.x[:, m_ * TLOC + g * 512: m_ * TLOC + (g + 1) * 512]
                msd = self.op("dve", lambda e, xs=xs, m_=m_:
                              e.scalar_tensor_tensor(xs, self.pb[m_][:, 0:512], 0.5, xs,
                                                     ALU.mult, ALU.add),
                              wait=[mlast, ms_b2], inc="dve")
                self.bank_read(m_, msd)
            self.ffn_g_ms[g] = self.ms("dve")

    def p_attn_b(self):
        w = self.w
        ms_y = self.ms("dve")
        ms_warena_pe = self.ms("pe")
        ms_wdkv = []
        for k in range(8):
            ms_wdkv.append(self.dma("pool", self.warena[:, k * 256:(k + 1) * 256],
                                    w["bWDKV"][k * 128:(k + 1) * 128, :], "wg",
                                    wait=[ms_warena_pe]))
        for m_ in range(2):
            for g in range(NG):
                bk = (m_ * NG + g) % 4
                pbk = self.pb[bk]
                mmm = None
                for k in range(8):
                    lhsT = self.warena[:, k * 256 + m_ * 128: k * 256 + (m_ + 1) * 128]
                    rhs = self.y[:, k * TLOC + g * 512: k * TLOC + (g + 1) * 512]
                    mmm = self.mm(pbk[:, 0:512], lhsT, rhs, start=(k == 0), stop=(k == 7),
                                  wait=[ms_wdkv[7], ms_y, self.bfree(bk)], inc=(k == 7))
                self.bank_written(bk, mmm)
                dst = self.warena[:, 2048 + m_ * 2048 + g * 512: 2048 + m_ * 2048 + (g + 1) * 512]
                msd = self.op("dve", lambda e, dst=dst, pbk=pbk:
                              e.tensor_copy(dst, pbk[:, 0:512]), wait=[mmm], inc="dve")
                self.bank_read(bk, msd)
        ms_ckv = self.ms("dve")

        ms_wuk = []
        for ck in range(2):
            ms_wuk.append(self.dma("pool", self.warena[:, 6144 + ck * 1024: 6144 + (ck + 1) * 1024],
                                   w["bWUK"][ck * 128:(ck + 1) * 128, :], "wg"))
        ms_kc_pe = None
        for m_ in range(8):
            for g in range(NG):
                bk = (m_ * NG + g) % 4
                pbk = self.pb[bk]
                mmm = None
                for ck in range(2):
                    lhsT = self.warena[:, 6144 + ck * 1024 + m_ * 128: 6144 + ck * 1024 + (m_ + 1) * 128]
                    rhs = self.warena[:, 2048 + ck * 2048 + g * 512: 2048 + ck * 2048 + (g + 1) * 512]
                    mmm = self.mm(pbk[:, 0:512], lhsT, rhs, start=(ck == 0), stop=(ck == 1),
                                  wait=[ms_wuk[1], ms_ckv, self.bfree(bk)], inc=(ck == 1))
                self.bank_written(bk, mmm)
                ms_kc_pe = mmm
                shalf = (m_ * NG + g) % 2
                sdst = self.stg[:, shalf * 512:(shalf + 1) * 512]
                msd = self.op("dve", lambda e, sdst=sdst, pbk=pbk:
                              e.tensor_copy(sdst, pbk[:, 0:512]), wait=[mmm], inc="dve")
                self.bank_read(bk, msd)
                mdk = None
                for hh in range(2):
                    h = 2 * m_ + hh
                    src = sdst[hh * 64:(hh + 1) * 64, :].rearrange("p (l t) -> p l t", l=4)
                    off = (ROW_K + (h * 16 + g * 4) * 64) * 128
                    dst = bass.AP(self.kvcon_h, off,
                                  [[128, 64], [64 * 128, 4], [1, 128]])
                    mdk = self.dma("sp", dst, src, "sstg", wait=[msd])
                self._wait("dve", mdk[0], mdk[1])

        msvset = self.op("pool", lambda e: e.memset(self.vstg[:], 1.0), inc="pool",
                         wait=[ms_warena_pe])
        ms_wuv = []
        for ck in range(2):
            ms_wuv.append(self.dma("pool", self.warena[:, 6144 + ck * 1024: 6144 + (ck + 1) * 1024],
                                   w["bWUV"][ck * 128:(ck + 1) * 128, :], "wg",
                                   wait=[ms_kc_pe]))
        ms_vdma = [None, None]
        for lt in range(NT):
            vhalf = lt % 2
            vsl = self.vstg[:, vhalf * 1040:(vhalf + 1) * 1040]
            for eh in range(2):
                bk = (lt % 2) * 2 + eh
                pbk = self.pb[bk]
                mmm = None
                for ck in range(2):
                    lhsT = self.warena[:, 2048 + ck * 2048 + lt * 128: 2048 + ck * 2048 + (lt + 1) * 128]
                    rhs = self.warena[:, 6144 + ck * 1024 + eh * 512: 6144 + ck * 1024 + (eh + 1) * 512]
                    mmm = self.mm(pbk[:, 0:512], lhsT, rhs, start=(ck == 0), stop=(ck == 1),
                                  wait=[ms_wuv[1], self.bfree(bk)], inc=(ck == 1))
                self.bank_written(bk, mmm)
                msd = None
                for h8 in range(8):
                    h = eh * 8 + h8
                    wt = [mmm, msvset]
                    if h8 == 0 and ms_vdma[vhalf]:
                        wt.append(ms_vdma[vhalf])
                    msd = self.op("dve", lambda e, vsl=vsl, h=h, pbk=pbk, h8=h8:
                                  e.tensor_copy(vsl[:, h * 65: h * 65 + 64],
                                                pbk[:, h8 * 64:(h8 + 1) * 64]),
                                  wait=wt, inc="dve")
                self.bank_read(bk, msd)
            msv = None
            for h in range(H):
                off = (ROW_V + (h * 16 + lt) * 65) * 128
                dst = bass.AP(self.kvcon_h, off, [[65, 128], [1, 65]])
                msv = self.dma("sp", dst, vsl[:, h * 65:(h + 1) * 65], "svst", wait=[msd])
            ms_vdma[vhalf] = msv

        ms_wkr = self.dma("pool", self.warena[:, 0:256],
                          w["bWKR"].rearrange("(k p) m -> p k m", p=128), "wg",
                          wait=[ms_kc_pe])
        for g in range(NG):
            bk = g % 2
            pbk = self.pb[bk]
            mmm = None
            for k in range(8):
                lhsT = self.warena[:, k * 32:(k + 1) * 32]
                rhs = self.y[:, k * TLOC + g * 512: k * TLOC + (g + 1) * 512]
                mmm = self.mm(pbk[0:32, 0:512], lhsT, rhs, start=(k == 0), stop=(k == 7),
                              wait=[ms_wkr, self.bfree(bk)], inc=(k == 7))
            self.bank_written(bk, mmm)
            mrl = self.dma("act", self.ropeb[0:32, 0:512],
                           self.ropei[0:32, g * 512:(g + 1) * 512], "sr0",
                           wait=[self.ms("dve")])
            msd = self.op("dve", lambda e, pbk=pbk:
                          e.tensor_mul(self.stg[0:32, 0:512], pbk[0:32, 0:512],
                                       self.ropeb[0:32, 0:512]),
                          wait=[mmm, mrl], inc="dve")
            self.bank_read(bk, msd)
            src = self.stg[0:32, 0:512].rearrange("p (l t) -> p l t", l=4)
            off = (ROW_KR + g * 4 * 32) * 128
            dst = bass.AP(self.kvcon_h, off, [[128, 32], [32 * 128, 4], [1, 128]])
            msk = self.dma("sp", dst, src, "sstg", wait=[msd])
            self._wait("dve", msk[0], msk[1])

        ms_kvcon = self.ms("sstg")
        ms_kvcon2 = self.ms("svst")
        self.op("pool", lambda e: e.collective_compute(
            "AllGather", ALU.bypass, replica_groups=[[0, 1, 2, 3], [4, 5, 6, 7]],
            ins=[self.kvcon_h.ap().opt()], outs=[self.kvg_h.ap().opt()]),
            wait=[ms_kvcon, ms_kvcon2], inc="cc1")
        ms_cc1 = self.ms("cc1")

        ms_big1_pe = self.ms("pe")
        for g in range(NG):
            ms_wslot = [self.ms("pe")] * 4
            for mc in range(32):
                slot = mc % 4
                wsl = self.warena[:, slot * 1024:(slot + 1) * 1024]
                src = self.wdq_bf.ap()[:, mc * 128:(mc + 1) * 128].rearrange(
                    "(k p) m -> p k m", p=128)
                wt = [ms_wslot[slot], self.ms_wbf]
                mw = self.dma("act", wsl, src, "wp%d" % slot, wait=wt)
                bk = mc % 4
                pbk = self.pb[bk]
                mmm = None
                for k in range(8):
                    lhsT = wsl[:, k * 128:(k + 1) * 128]
                    rhs = self.y[:, k * TLOC + g * 512: k * TLOC + (g + 1) * 512]
                    wtm = [mw, self.bfree(bk)]
                    if g == 0 and mc == 0:
                        wtm.append(ms_big1_pe)
                    mmm = self.mm(pbk[:, 0:512], lhsT, rhs, start=(k == 0), stop=(k == 7),
                                  wait=wtm, inc=(k == 7))
                self.bank_written(bk, mmm)
                ms_wslot[slot] = mmm
                dst = self.big1[:, mc * 512:(mc + 1) * 512]
                msd = self.op("dve", lambda e, dst=dst, pbk=pbk:
                              e.tensor_copy(dst, pbk[:, 0:512]), wait=[mmm], inc="dve")
                self.bank_read(bk, msd)
            ms_cq = self.ms("dve")
            ms_whalf = [None, None]
            for m_ in range(8):
                half = m_ % 2
                wsl = self.warena[:, half * 4096:(half + 1) * 4096]
                src = self.wuq_bf.ap()[:, m_ * 128:(m_ + 1) * 128].rearrange(
                    "(k p) m -> p k m", p=128)
                wt = [ms_wslot[2 * half], ms_wslot[2 * half + 1]] if m_ < 2 else []
                if ms_whalf[half]:
                    wt = [ms_whalf[half]]
                mw = self.dma("act", wsl, src, "wp%d" % half, wait=wt)
                bk = m_ % 4
                pbk = self.pb[bk]
                mmm = None
                for k in range(32):
                    lhsT = wsl[:, k * 128:(k + 1) * 128]
                    rhs = self.big1[:, k * 512:(k + 1) * 512]
                    mmm = self.mm(pbk[:, 0:512], lhsT, rhs, start=(k == 0), stop=(k == 31),
                                  wait=[mw, ms_cq, self.bfree(bk)], inc=(k == 31))
                self.bank_written(bk, mmm)
                ms_whalf[half] = mmm
                shalf = m_ % 2
                sdst = self.stg[:, shalf * 512:(shalf + 1) * 512]
                msd = self.op("dve", lambda e, sdst=sdst, pbk=pbk:
                              e.tensor_copy(sdst, pbk[:, 0:512]), wait=[mmm], inc="dve")
                self.bank_read(bk, msd)
                for hh in range(2):
                    h = 2 * m_ + hh
                    src2 = sdst[hh * 64:(hh + 1) * 64, :].rearrange("p (l t) -> p l t", l=4)
                    off = ((h * 16 + g * 4) * 96) * 128
                    dst = bass.AP(self.qcon_h, off,
                                  [[128, 64], [96 * 128, 4], [1, 128]])
                    msq = self.dma("sp", dst, src2, "sstg", wait=[msd])
                self._wait("dve", msq[0], msq[1])
            for mr in range(4):
                half = mr % 2
                wsl = self.warena[:, half * 4096:(half + 1) * 4096]
                src = self.wqr_bf.ap()[:, mr * 128:(mr + 1) * 128].rearrange(
                    "(k p) m -> p k m", p=128)
                mw = self.dma("act", wsl, src, "wp%d" % half,
                              wait=[ms_whalf[half]] if ms_whalf[half] else [])
                bk = mr % 4
                pbk = self.pb[bk]
                mmm = None
                for k in range(32):
                    lhsT = wsl[:, k * 128:(k + 1) * 128]
                    rhs = self.big1[:, k * 512:(k + 1) * 512]
                    mmm = self.mm(pbk[:, 0:512], lhsT, rhs, start=(k == 0), stop=(k == 31),
                                  wait=[mw, self.bfree(bk)], inc=(k == 31))
                self.bank_written(bk, mmm)
                ms_whalf[half] = mmm
                rhalf = mr % 2
                rsl = self.ropeb[:, rhalf * 512:(rhalf + 1) * 512]
                mrl = self.dma("act", rsl, self.ropei[mr * 128:(mr + 1) * 128,
                                                     g * 512:(g + 1) * 512],
                               "sr%d" % rhalf, wait=[self.ms("dve")])
                shalf = mr % 2
                sdst = self.stg[:, shalf * 512:(shalf + 1) * 512]
                msd = self.op("dve", lambda e, sdst=sdst, pbk=pbk, rsl=rsl:
                              e.tensor_mul(sdst, pbk[:, 0:512], rsl),
                              wait=[mmm, mrl], inc="dve")
                self.bank_read(bk, msd)
                msq = None
                for hh in range(4):
                    h = 4 * mr + hh
                    src2 = sdst[hh * 32:(hh + 1) * 32, :].rearrange("p (l t) -> p l t", l=4)
                    off = ((h * 16 + g * 4) * 96 + 64) * 128
                    dst = bass.AP(self.qcon_h, off,
                                  [[128, 32], [96 * 128, 4], [1, 128]])
                    msq = self.dma("sp", dst, src2, "sstg", wait=[msd])
                self._wait("dve", msq[0], msq[1])

        ms_qcon = self.ms("sstg")
        self.op("pool", lambda e: e.collective_compute(
            "AllGather", ALU.bypass, replica_groups=[[0, 1, 2, 3], [4, 5, 6, 7]],
            ins=[self.qcon_h.ap().opt()], outs=[self.qg2_h.ap().opt()]),
            wait=[ms_qcon], inc="cc3")
        ms_cc3 = self.ms("cc3")

        xgt = self.kvg_h.ap()
        ms_big2_pe = self.ms("pe")
        for r in range(4):
            reg = xgt[r * 33536 + ROW_KR: r * 33536 + ROW_KR + 512, :]
            src = reg.rearrange("(l f) t -> f l t", l=16)
            self.dma("sp", self.big2[64:96, r * 2048:(r + 1) * 2048], src, "satt",
                     wait=[ms_cc1, ms_big2_pe])
        msw_pe = self.ms("pe")
        for k in range(8):
            mwo = self.dma("pool", self.warena[:, k * 1024:(k + 1) * 1024],
                           self.w["bWO"][k * 128:(k + 1) * 128, :], "wg",
                           wait=[msw_pe])
        self.ms_wo_loaded = mwo
        ms_o_dma = None
        RSTRIDE_KV = 33536 * 128
        RSTRIDE_Q = 24576 * 128
        for hl in range(4):
            q_eng = "sp" if hl % 2 == 0 else "act"
            q_sem = "satt" if hl % 2 == 0 else "aatt"
            ms_prev_pe = self.ms("pe")
            ms_ldq = []

            def mk_d2d(kind, hl=hl, eng=q_eng):
                def fn(e, kind=kind, hl=hl, eng=eng):
                    hsel = self.cg[eng] * 4 + hl
                    if kind == "k":
                        off = ROW_K * 128 + hsel * (16 * 64 * 128)
                        dims = [[RSTRIDE_KV, 4], [128, 1024], [1, 128]]
                        return e.dma_start(out=self.kscr_h.ap(),
                                           in_=bass.AP(self.kvg_h, off, dims))
                    elif kind == "v":
                        off = ROW_V * 128 + hsel * (16 * 65 * 128)
                        dims = [[RSTRIDE_KV, 4], [128, 1040], [1, 128]]
                        return e.dma_start(out=self.vscr_h.ap(),
                                           in_=bass.AP(self.kvg_h, off, dims))
                    else:
                        off = hsel * (16 * 96 * 128)
                        dims = [[RSTRIDE_Q, 4], [128, 1536], [1, 128]]
                        return e.dma_start(out=self.qscr_h.ap(),
                                           in_=bass.AP(self.qg2_h, off, dims))
                return fn
            for kind in ("k", "v"):
                ms_ldq.append(self.op(q_eng, mk_d2d(kind),
                                      wait=[ms_cc1, ms_prev_pe], inc=q_sem, n=16))
            self._wait(q_eng, q_sem, ms_ldq[-1][1])
            for r in range(4):
                ksrc = self.kscr_h.ap()[r * 1024:(r + 1) * 1024, :].rearrange(
                    "(l f) t -> f l t", l=16)
                ms_ldq.append(self.dma(q_eng, self.big2[0:64, r * 2048:(r + 1) * 2048],
                                       ksrc, q_sem))
                vsrc = self.vscr_h.ap()[r * 1040:(r + 1) * 1040, :].rearrange(
                    "(l r2) c -> l (r2 c)", l=16).rearrange("l (t c) -> t l c", t=128, c=65)
                ms_ldq.append(self.dma(q_eng, self.big2[:, 8192 + r * 1040: 8192 + (r + 1) * 1040].rearrange("p (l c) -> p l c", l=16),
                                       vsrc, q_sem))
            ms_ld_kv = ms_ldq[-1]
            ms_ldq.append(self.op(q_eng, mk_d2d("q"),
                                  wait=[ms_cc3, ms_prev_pe], inc=q_sem, n=16))
            self._wait(q_eng, q_sem, ms_ldq[-1][1])
            for r in range(4):
                qsrc = self.qscr_h.ap()[r * 1536:(r + 1) * 1536, :].rearrange(
                    "(l f) t -> f l t", l=16)
                ms_ldq.append(self.dma(q_eng, self.big1[0:96, r * 2048:(r + 1) * 2048],
                                       qsrc, q_sem))
            ms_ld = ms_ldq[-1]
            ms_ld = ms_ldq[-1]
            kmax = self.misc2[0:1, 600:601]
            qmax = self.misc2[0:1, 601:602]
            tmp1 = self.misc2[0:1, 602:603]
            ksq = self.big1[0:96, 8192:16384]
            msqk = self.op("act", lambda e, ksq=ksq:
                           e.activation(out=ksq, in_=self.big2[0:96, 0:8192],
                                        func=AF.Square),
                           wait=[ms_ld_kv], inc="act")
            ms_nmax = None
            for src_i in range(2):
                msq_ = msqk
                if src_i == 1:
                    msq_ = self.op("act", lambda e, ksq=ksq:
                                   e.activation(out=ksq, in_=self.big1[0:96, 0:8192],
                                                func=AF.Square),
                                   wait=[ms_ld, self.ms("pe")], inc="act")
                dstmax = kmax if src_i == 0 else qmax
                for j16 in range(16):
                    mmx = self.mm(self.pb[0][0:1, 0:512], self.onesb[0:96, 0:1],
                                  ksq[:, j16 * 512:(j16 + 1) * 512], start=True, stop=True,
                                  wait=[msq_, self.bfree(0)], inc=True)
                    self.bank_written(0, mmx)
                    msr = self.op("dve", lambda e, tmp1=tmp1:
                                  e.reduce_max(tmp1, self.pb[0][0:1, 0:512], AX),
                                  wait=[mmx, ms_prev_pe, self.ms("act")], inc="dve")
                    self.bank_read(0, msr)
                    if j16 == 0:
                        ms_nmax = self.op("dve", lambda e, dstmax=dstmax, tmp1=tmp1:
                                          e.tensor_copy(dstmax, tmp1), wait=[msr], inc="dve")
                    else:
                        ms_nmax = self.op("dve", lambda e, dstmax=dstmax, tmp1=tmp1:
                                          e.tensor_max(dstmax, dstmax, tmp1),
                                          wait=[msr], inc="dve")
            msc1 = self.op("dve", lambda e, kmax=kmax, qmax=qmax, tmp1=tmp1:
                           e.tensor_mul(tmp1, kmax, qmax), wait=[ms_nmax], inc="dve")
            msc2 = self.op("act", lambda e, tmp1=tmp1:
                           e.activation(out=tmp1, in_=tmp1, func=AF.Sqrt),
                           wait=[msc1], inc="act")
            msc3 = self.op("dve", lambda e, tmp1=tmp1:
                           e.tensor_scalar_mul(tmp1, tmp1, -ISQ), wait=[msc2], inc="dve")
            mmb = self.mm(self.pb[0][:, 0:1], self.onesf[0:1, 0:128], tmp1,
                          start=True, stop=True, wait=[msc3, self.bfree(0)], inc=True)
            self.bank_written(0, mmb)
            ncc = self.misc2[:, self.COL_NC:self.COL_NC + 1]
            msnc = self.op("dve", lambda e, ncc=ncc:
                           e.tensor_copy(ncc, self.pb[0][:, 0:1]),
                           wait=[mmb, self.ms("act")], inc="dve")
            self.bank_read(0, msnc)

            for qg in range(16):
                mlast = None
                nj = 4 * qg + 4
                for jb in range(0, nj, 4):
                    msS = None
                    widths = []
                    for jj in range(4):
                        j = jb + jj
                        i = j - 4 * qg
                        if i < 0:
                            c0, wdt = qg * 512, 512
                        else:
                            c0, wdt = qg * 512 + i * 128, (4 - i) * 128
                        widths.append((j, i, c0, wdt))
                        sbk = 4 + jj
                        wtS = [ms_ld]
                        if jj == 0:
                            wtS.append(self.bfree(4))
                        msS = self.mm(self.pb[sbk][:, 0:wdt],
                                      self.big2[0:96, j * 128:(j + 1) * 128],
                                      self.big1[0:96, c0:c0 + wdt],
                                      start=True, stop=True,
                                      wait=wtS, inc=(jj == 3))
                    for b4 in range(4, 8):
                        self.bank_written(b4, msS)
                    msE = self.op("act", lambda e:
                                  e.activation(out=self.ptb[:], in_=self.pbs[:],
                                               func=AF.Exp, scale=ISQ,
                                               bias=self.misc2[:, self.COL_NC:self.COL_NC + 1]),
                                  wait=[msS, msnc], inc="act")
                    for b4 in range(4, 8):
                        self.bank_read(b4, msE)
                    msM = None
                    for (j, i, c0, wdt) in widths:
                        jj = j - jb
                        if i >= 0:
                            dsl = self.ptb[:, jj * 512: jj * 512 + 128]
                            msM = self.op("dve", lambda e, dsl=dsl:
                                          e.tensor_mul(dsl, dsl, self.causal[:]),
                                          wait=[msE], inc="dve")
                    for (j, i, c0, wdt) in widths:
                        jj = j - jb
                        wtp = [msE] if msM is None else [msM]
                        if j == 0:
                            wtp.append(self.bfree(2))
                        mlast = self.mm(self.pb[2][0:65, (i * 128 if i > 0 else 0):
                                                   (i * 128 if i > 0 else 0) + wdt],
                                        self.big2[:, 8192 + j * 65: 8192 + (j + 1) * 65],
                                        self.ptb[:, jj * 512: jj * 512 + wdt],
                                        start=(j == 0), stop=(j == nj - 1),
                                        wait=wtp, inc=(j == nj - 1))
                self.bank_written(2, mlast)
                row = self.misc2[0:1, 0:512]
                msn1 = self.op("dve", lambda e, row=row:
                               e.reciprocal(row, self.pb[2][64:65, 0:512]),
                               wait=[mlast, self.ms("pe")], inc="dve")
                mmb2 = self.mm(self.pb[3][0:64, 0:512], self.onesf[0:1, 0:64], row,
                               start=True, stop=True, wait=[msn1, self.bfree(3)], inc=True)
                self.bank_written(3, mmb2)
                msc = self.op("dve", lambda e:
                              e.tensor_copy(self.o2b[:], self.pb[2][0:64, 0:512]),
                              wait=[mlast], inc="dve")
                self.bank_read(2, msc)
                mso = self.op("dve", lambda e:
                              e.tensor_mul(self.stg[0:64, 0:512], self.o2b[:],
                                           self.pb[3][0:64, 0:512]),
                              wait=[mmb2, msc], inc="dve")
                self.bank_read(3, mso)
                src = self.stg[0:64, 0:512].rearrange("p (l t) -> p l t", l=4)
                off = (hl * 64 + 4 * qg) * 8192
                dst = bass.AP(self.ocon_h, off, [[128, 64], [8192, 4], [1, 128]])
                ms_o_dma = self.dma("sp", dst, src, "sstg", wait=[mso])
                self._wait("dve", ms_o_dma[0], ms_o_dma[1])
        self.ms_big2_free = self.ms("pe")
        self.op("pool", lambda e: e.collective_compute(
            "AllGather", ALU.bypass, replica_groups=[[0, 1, 2, 3], [4, 5, 6, 7]],
            ins=[self.ocon_h.ap().opt()], outs=[self.og_h.ap().opt()]),
            wait=[ms_o_dma], inc="cc2")
        ms_cc2 = self.ms("cc2")
        ms_wo = self.ms_wo_loaded
        ms_big1_pe2 = self.ms("pe")
        ms_og = []
        for hl in range(4):
            def fno(e, hl=hl):
                off = (hl * 64) * 8192 + self.cg["pool"] * (16 * 8192)
                dims = [[256 * 8192, 4], [8192, 16], [1, 8192]]
                return e.dma_start(out=self.oscr_h.ap()[hl * 64:(hl + 1) * 64, :]
                                   .rearrange("(p l) c -> p l c", p=4),
                                   in_=bass.AP(self.og_h, off, dims))
            ms_og.append(self.op("pool", fno, wait=[ms_cc2, ms_big1_pe2], inc="gm", n=16))
        self._wait("pool", "gm", ms_og[-1][1])
        for p in range(4):
            for hl in range(4):
                h = 4 * p + hl
                src = self.oscr_h.ap()[hl * 64 + p * 16: hl * 64 + (p + 1) * 16, :]
                src = src.rearrange("l (f t) -> f l t", f=64)
                dst = self.y[(h % 2) * 64:(h % 2) * 64 + 64,
                             (h // 2) * TLOC:(h // 2 + 1) * TLOC]
                ms_og.append(self.dma("pool", dst, src, "gm"))
        ms_og_done = ms_og[-1]
        for m_ in range(8):
            for g in range(NG):
                bk = (m_ * NG + g) % 4
                pbk = self.pb[bk]
                mmm = None
                for k in range(8):
                    lhsT = self.warena[:, k * 1024 + m_ * 128: k * 1024 + (m_ + 1) * 128]
                    rhs = self.y[:, k * TLOC + g * 512: k * TLOC + (g + 1) * 512]
                    mmm = self.mm(pbk[:, 0:512], lhsT, rhs, start=(k == 0), stop=(k == 7),
                                  wait=[ms_wo, ms_og_done, self.bfree(bk)], inc=(k == 7))
                self.bank_written(bk, mmm)
                xs = self.x[:, m_ * TLOC + g * 512: m_ * TLOC + (g + 1) * 512]
                msd = self.op("dve", lambda e, xs=xs, pbk=pbk:
                              e.tensor_add(xs, xs, pbk[:, 0:512]), wait=[mmm], inc="dve")
                self.bank_read(bk, msd)


    def _mk_hsel_dma(self, view, hl, dst, eng):
        def fn(e, view=view, hl=hl, dst=dst, eng=eng):
            hsel = self.cg[eng] * 4 + hl
            return e.dma_start(out=dst, in_=view[hsel, :, :, :])
        return fn

    def _mk_cg_dma(self, view, dst):
        def fn(e, view=view, dst=dst):
            return e.dma_start(out=dst, in_=view[self.cg["pool"], :, :, :])
        return fn

    def p_store_y(self):
        self._store_stream(self.y, cast=True)

    def p_store(self):
        self._store_stream(self.x, cast=False)

    def _store_stream(self, src, cast):
        ms_x = self.ms("dve")
        g_ms = getattr(self, "ffn_g_ms", None)
        x3 = self.xout.rearrange("(t p) d -> t p d", p=128)
        last = None
        ms_half_free = [None, None]
        for ti in range(NT):
            half = ti % 2
            msd = None
            for d in range(8):
                bk = 4 + (d % 4)
                ssl = src[:, d * TLOC + ti * 128: d * TLOC + (ti + 1) * 128]
                if cast:
                    tmp = self.misc2[:, (d % 2) * 128:(d % 2) * 128 + 128]
                    msc = self.op("dve", lambda e, tmp=tmp, ssl=ssl:
                                  e.tensor_copy(tmp, ssl),
                                  wait=[ms_x, self.bfree(bk)], inc="dve")
                    tsrc = tmp
                    twait = [msc]
                else:
                    tsrc = ssl
                    mg = g_ms.get(ti // 4) if (g_ms and not cast) else None
                    twait = [mg or ms_x, self.bfree(bk)]
                msmm = self.op("pe", lambda e, tsrc=tsrc, bk=bk:
                               e.transpose(self.pb[bk][:, 0:128], tsrc, self.ident[:]),
                               wait=twait, inc="pe")
                self.bank_written(bk, msmm)
                dst = self.fstage[:, half * 1024 + d * 128: half * 1024 + (d + 1) * 128]
                wt = [msmm]
                if d == 0 and ms_half_free[half]:
                    wt.append(ms_half_free[half])
                msd = self.op("dve", lambda e, dst=dst, bk=bk:
                              e.tensor_copy(dst, self.pb[bk][:, 0:128]),
                              wait=wt, inc="dve")
                self.bank_read(bk, msd)
            last = self.dma("sp", x3[ti, :, :],
                            self.fstage[:, half * 1024:(half + 1) * 1024], "sxo",
                            wait=[msd])
            ms_half_free[half] = last
        self._wait("sp", last[0], last[1])


_CACHE = {}


def _build(stage=None):
    key = stage
    if key not in _CACHE:
        kb = KB(stage)
        nc = kb.build()
        _CACHE[key] = nc
    return _CACHE[key]


def _consts(cg):
    ident = np.eye(128, dtype=np.float32)
    pq = np.arange(128)
    causal = (pq[:, None] <= pq[None, :]).astype(ml_dtypes.bfloat16)
    inv = (1.0 / (np.float32(10000.0) **
                  (np.arange(0, DHR, 2, dtype=np.float32) / np.float32(DHR)))).astype(np.float32)
    t = np.arange(S, dtype=np.float32)
    freqs = (t[:, None] * inv[None, :]).astype(np.float32)
    emb = np.concatenate([freqs, freqs], axis=-1)
    tab = (np.cos(emb) + np.sin(emb)).astype(np.float32)
    sl = tab[2048 * cg: 2048 * (cg + 1), :].T
    rope = np.tile(sl, (H, 1)).astype(ml_dtypes.bfloat16)
    return ident, np.ascontiguousarray(causal), np.ascontiguousarray(rope)


def kernel(**inputs):
    stage = inputs.pop("_stage", None)
    nc = _build(stage)
    x = np.asarray(inputs["L0_attn_a_hidden_states"])
    wmap = {
        "aWq": "L0_attn_a_Wq", "aWk": "L0_attn_a_Wk", "aWv": "L0_attn_a_Wv",
        "aproj": "L0_attn_a_proj", "aWo": "L0_attn_a_Wo", "abo": "L0_attn_a_bo",
        "bWDQ": "L0_attn_b_W_DQ", "bWUQ": "L0_attn_b_W_UQ", "bWDKV": "L0_attn_b_W_DKV",
        "bWUK": "L0_attn_b_W_UK", "bWUV": "L0_attn_b_W_UV", "bWQR": "L0_attn_b_W_QR",
        "bWKR": "L0_attn_b_W_KR", "bWO": "L0_attn_b_W_O",
        "fW1": "L0_ffn_W1", "fb1": "L0_ffn_b1", "fW2": "L0_ffn_W2", "fb2": "L0_ffn_b2",
    }
    in_maps = []
    for c in range(8):
        b, cg = c // 4, c % 4
        m = {"xin": np.ascontiguousarray(x[b, 2048 * cg:2048 * (cg + 1), :])}
        for k, src in wmap.items():
            m[k] = np.ascontiguousarray(np.asarray(inputs[src]))
        ident, causal, rope = _consts(cg)
        m["identi"] = ident
        m["causali"] = causal
        m["ropei"] = rope
        in_maps.append(m)
    res = run_bass_kernel_spmd(nc, in_maps, core_ids=list(range(8)))
    out = np.empty((B, S, D), dtype=np.float32)
    for c in range(8):
        b, cg = c // 4, c % 4
        out[b, 2048 * cg:2048 * (cg + 1), :] = res.results[c]["xout"]
    return out


# version 15 (final)
# speedup vs baseline: 1.2366x; 1.2165x over previous
import numpy as np
import ml_dtypes
import concourse.bass as bass
import concourse.mybir as mybir
from concourse.bass_utils import run_bass_kernel_spmd

F32 = mybir.dt.float32
BF16 = mybir.dt.bfloat16
AF = mybir.ActivationFunctionType
ALU = mybir.AluOpType
AX = mybir.AxisListType.X

B, S, D = 2, 8192, 1024
H, DH, PF = 16, 64, 256
HS, DHR = 64, 32
DK = HS + DHR
DFF = 4096
DCH = 4096
TLOC = 2048
NT = 16
NG = 4
GT = 64
ISQ = float(1.0 / np.sqrt(np.float32(DK)))
EPS_LN = 1e-5
EPS_A = 1e-8
EPS_Q = 1e-5

ROW_K = 0
ROW_KR = 16384
ROW_V = 16896
ROW_Q = 33536
XCON_ROWS = 58112


class KB:
    ENGS = ("pe", "act", "dve", "pool", "sp")

    def __init__(self, stage=None):
        self.stage = stage
        nc = bass.Bass("TRN2", target_bir_lowering=False)
        self.nc = nc
        self.cmds = {e: [] for e in self.ENGS}
        sem_names = ["pe", "act", "dve", "pool",
                     "sx", "sx2", "sw2a", "sw2b", "sw2c", "sw2d",
                     "sw10", "sw11", "sw12", "sw13", "sw14", "sw15", "sw16", "sw17", "satt", "aatt", "srow",
                     "sr0", "sr1", "sog", "sstg", "svst", "skv", "sscr", "sxo",
                     "wp0", "wp1", "wp2", "wp3", "wg", "gm",
                     "cc0", "cc1", "cc2", "cc3"]
        self.cnt = {s: 0 for s in sem_names}
        self.sem_names = sem_names
        self.last_wait = {}
        self.cg = {}
        self._bw = {}
        self._br = {}

        dt = nc.dram_tensor
        self.xin = dt("xin", [TLOC, D], F32, kind="ExternalInput").ap()
        self.xout = dt("xout", [TLOC, D], F32, kind="ExternalOutput").ap()
        w = {}
        for name, shape in [
            ("aWq", [D, D]), ("aWk", [D, D]), ("aWv", [D, D]), ("aproj", [DH, PF]),
            ("aWo", [D, D]), ("abo", [D]),
            ("bWDQ", [D, DCH]), ("bWUQ", [DCH, D]), ("bWDKV", [D, PF]),
            ("bWUK", [PF, D]), ("bWUV", [PF, D]), ("bWQR", [DCH, H * DHR]),
            ("bWKR", [D, DHR]), ("bWO", [D, D]),
            ("fW1", [DFF, D]), ("fb1", [DFF]), ("fW2", [D, DFF]), ("fb2", [D]),
        ]:
            w[name] = dt(name, shape, F32, kind="ExternalInput").ap()
        self.w = w
        self.identi = dt("identi", [128, 128], F32, kind="ExternalInput").ap()
        self.causali = dt("causali", [128, 128], BF16, kind="ExternalInput").ap()
        self.ropei = dt("ropei", [H * DHR, TLOC], BF16, kind="ExternalInput").ap()

        self.wq1s = dt("wq1s", [DFF, D], BF16)
        self.wq2s = dt("wq2s", [D, DFF], BF16)
        self.acon_h = dt("acon", [H * PF, 65], F32)
        self.ag_h = dt("ag", [4 * H * PF, 65], F32)
        self.kvcon_h = dt("kvcon", [33536, 128], BF16)
        self.kvg_h = dt("kvg", [4 * 33536, 128], BF16)
        self.qcon_h = dt("qcon", [24576, 128], BF16)
        self.qg2_h = dt("qg2", [4 * 24576, 128], BF16)
        self.ocon_h = dt("ocon", [4 * GT, 8192], BF16)
        self.og_h = dt("og", [16 * GT, 8192], BF16)
        self.wdq_bf = dt("wdq_bf", [D, DCH], BF16)
        self.wuq_bf = dt("wuq_bf", [DCH, D], BF16)
        self.wqr_bf = dt("wqr_bf", [DCH, H * DHR], BF16)
        self.kscr_h = dt("kscr", [4 * 1024, 128], BF16)
        self.vscr_h = dt("vscr", [4 * 1040, 128], BF16)
        self.qscr_h = dt("qscr", [4 * 1536, 128], BF16)
        self.oscr_h = dt("oscr", [4 * GT, 8192], BF16)

        from contextlib import ExitStack
        self.es = ExitStack()
        sb = lambda n, shp, d: self.es.enter_context(nc.sbuf_tensor(n, shp, d))
        self.x = sb("x_fm", [128, 8 * TLOC], F32)
        self.y = sb("y_fm", [128, 8 * TLOC], BF16)
        self.big1 = sb("big1", [128, 16384], BF16)
        self.big2 = sb("big2", [128, 16640], BF16)
        self.warena = sb("warena", [128, 8192], BF16)
        self.fstage = sb("fstage", [128, 2048], F32)
        self.misc2 = sb("misc2", [128, 1280], F32)
        self.ident = sb("ident", [128, 128], F32)
        self.causal = sb("causal", [128, 128], BF16)
        self.onesb = sb("onesb", [128, 128], BF16)
        self.onesf = sb("onesf", [1, 128], F32)
        self.ones128f = sb("ones128f", [128, 128], F32)
        self.projb = sb("projb", [128, PF], BF16)
        self.ptb = sb("ptb", [128, 3072], BF16)
        self.stg = sb("stg", [128, 1024], BF16)
        self.o2b = sb("o2b", [64, 512], BF16)
        self.kvb = sb("kvb", [128, 2 * H * 65], BF16)
        self.vstg = self.kvb
        self.ropeb = sb("ropeb", [128, 1024], BF16)
        self.constv = sb("constv", [128, 8], F32)
        nc.const_aps.register(F32, EPS_LN, self.constv[:, 0:1])
        nc.const_aps.register(F32, -0.5, self.constv[:, 1:2])
        nc.const_aps.register(F32, 0.5, self.constv[:, 2:3])
        self.pball = self.es.enter_context(nc.psum_tensor("pball", [128, 4096], F32))
        self.pb = [self.pball[:, k * 512:(k + 1) * 512] for k in range(8)]
        self.pbs = self.pball[:, 2048:4096]
        self.sems = {s: self.es.enter_context(nc.semaphore(s)) for s in sem_names}

        self.COL_ROW = 0
        self.COL_KV = 1024
        self.COL_B1 = 1160
        self.COL_B2 = 1192
        self.COL_BO = 1200
        self.COL_AR = 1208
        self.COL_G = 1240
        self.COL_NC = 1246

    def _wait(self, eng, sem, val):
        if val is None or val <= 0:
            return
        key = (eng, sem)
        if self.last_wait.get(key, 0) >= val:
            return
        self.last_wait[key] = val
        self.cmds[eng].append(("w", sem, val))

    def op(self, eng, fn, wait=(), inc=None, n=1):
        for m in wait:
            if m is None:
                continue
            s, v = m
            self._wait(eng, s, v)
        if inc is None:
            self.cmds[eng].append(("i", fn, None, 0))
            return None
        self.cnt[inc] += n
        ms = (inc, self.cnt[inc])
        self.cmds[eng].append(("i", fn, inc, n))
        return ms

    def ms(self, sem):
        return (sem, self.cnt[sem])

    def dma(self, eng, out, in_, sem, wait=(), **kw):
        return self.op(eng, lambda e, o=out, i=in_, k=dict(kw): e.dma_start(out=o, in_=i, **k),
                       wait=wait, inc=sem, n=16)

    def dmat(self, eng, out, in_, sem, wait=()):
        return self.op(eng, lambda e, o=out, i=in_: e.dma_start_transpose(o, i),
                       wait=wait, inc=sem, n=16)

    def mm(self, out, lhsT, rhs, start, stop, wait=(), inc=False):
        return self.op("pe",
                       lambda e, o=out, l=lhsT, r=rhs, st=start, sp=stop:
                       e.matmul(o, l, r, start=st, stop=sp),
                       wait=wait, inc=("pe" if inc else None))

    def bank_written(self, i, ms):
        self._bw[i] = ms

    def bank_read(self, i, ms):
        self._br[i] = ms

    def bfree(self, i):
        return self._br.get(i)

    def bready(self, i):
        return self._bw.get(i)

    def build(self):
        self.emit_all()
        nc = self.nc
        with nc.Block() as block:
            def runner(eng_name):
                def run(e):
                    if eng_name in ("sp", "act", "pool"):
                        cid = e.partition_id()
                        self.cg[eng_name] = cid % 4
                    for cmd in self.cmds[eng_name]:
                        if cmd[0] == "w":
                            e.wait_ge(self.sems[cmd[1]], cmd[2])
                        else:
                            _, fn, inc, n = cmd
                            ins = fn(e)
                            if inc is not None:
                                ins.then_inc(self.sems[inc], n)
                return run
            block.tensor(runner("pe"))
            block.scalar(runner("act"))
            block.vector(runner("dve"))
            block.gpsimd(runner("pool"))
            block.sync(runner("sp"))
        return nc

    def emit_all(self):
        st = self.stage
        if st == "wq1":
            self.p_init()
            self.p_quantize()
            m = self.dma("pool", self.xout, self.wq1s.ap()[0:2048, :], "gm",
                         wait=[self.ms_wq_scr])
            self._wait("pool", m[0], m[1])
            return
        self.p_init()
        self.p_load_x()
        if st == "x0":
            self.p_store(); return
        self.emit_ln()
        if st == "ln1":
            self.p_store_y(); return
        self.p_attn_a()
        if st in ("kv", "aout", "acon", "vtm", "kfm"):
            return
        if st == "x1":
            self.p_store(); return
        self.emit_ln()
        self.p_ffn()
        if st == "x2":
            self.p_store(); return
        self.emit_ln()
        self.p_attn_b()
        if st == "x3":
            self.p_store(); return
        self.emit_ln()
        self.p_ffn()
        self.p_store()

    def p_init(self):
        self.dma("sp", self.ident[:], self.identi, "srow")
        self.dma("sp", self.causal[:], self.causali, "srow")
        self.op("pool", lambda e: e.memset(self.constv[:, 0:1], EPS_LN))
        self.op("pool", lambda e: e.memset(self.constv[:, 1:2], -0.5))
        self.op("pool", lambda e: e.memset(self.constv[:, 2:3], 0.5))
        self.op("pool", lambda e: e.memset(self.onesb[:], 1.0))
        self.op("pool", lambda e: e.memset(self.onesf[:], 1.0))
        self.op("pool", lambda e: e.memset(self.ones128f[:], 1.0), inc="pool")
        self.ms_ones = self.ms("pool")
        self.dma("pool", self.projb[0:64, :], self.w["aproj"], "gm")
        self.dma("pool", self.projb[64:128, :], self.w["aproj"], "gm")
        self.ms_proj = self.ms("gm")

    def p_load_x(self):
        x3 = self.xin.rearrange("(t p) d -> t p d", p=128)
        ms_fs_free = [None, None]
        msd = None
        for ti in range(NT):
            half = ti % 2
            fs = self.fstage[:, half * 1024:(half + 1) * 1024]
            msl = self.dma("sp", fs, x3[ti, :, :], "sx" if half == 0 else "sx2",
                           wait=[ms_fs_free[half]] if ms_fs_free[half] else [])
            for d in range(8):
                bk = 4 + (d % 4)
                fsl = fs[:, d * 128:(d + 1) * 128]
                msmm = self.op("pe", lambda e, fsl=fsl, bk=bk:
                               e.transpose(self.pb[bk][:, 0:128], fsl, self.ident[:]),
                               wait=[msl, ("srow", 32), self.bfree(bk)], inc="pe")
                self.bank_written(bk, msmm)
                dst = self.x[:, d * TLOC + ti * 128: d * TLOC + (ti + 1) * 128]
                msd = self.op("dve", lambda e, dst=dst, bk=bk:
                              e.tensor_copy(dst, self.pb[bk][:, 0:128]),
                              wait=[msmm], inc="dve")
                self.bank_read(bk, msd)
                if d == 7:
                    ms_fs_free[half] = msmm

    def emit_ln(self):
        ms_x = self.ms("dve")
        ms_sx0 = self.ms("sx")
        ms_sx20 = self.ms("sx2")
        for g in range(NG):
            xb = self.big1[:, 0:4096]
            xb2 = self.big1[:, 4096:8192]
            msc = None
            for d in range(8):
                src = self.x[:, d * TLOC + g * 512: d * TLOC + (g + 1) * 512]
                dst = xb[:, d * 512:(d + 1) * 512]
                msc = self.op("act", lambda e, src=src, dst=dst:
                              e.activation(out=dst, in_=src, func=AF.Copy),
                              wait=[ms_x, self.ms("pe")], inc="act")
            mss = self.op("dve", lambda e, xb=xb, xb2=xb2:
                          e.tensor_mul(xb2, xb, xb), wait=[msc], inc="dve")
            m1 = None
            for d in range(8):
                m1 = self.mm(self.pb[4][:, 0:512], self.onesb[:, 0:128],
                             xb[:, d * 512:(d + 1) * 512], start=(d == 0), stop=(d == 7),
                             wait=[msc, self.ms_ones, self.bfree(4)], inc=(d == 7))
            self.bank_written(4, m1)
            m2 = None
            for d in range(8):
                m2 = self.mm(self.pb[5][:, 0:512], self.onesb[:, 0:128],
                             xb2[:, d * 512:(d + 1) * 512], start=(d == 0), stop=(d == 7),
                             wait=[mss, self.bfree(5)], inc=(d == 7))
            self.bank_written(5, m2)
            mb = self.fstage[:, 0:512]
            vr = self.fstage[:, 512:1024]
            ms3 = self.op("dve", lambda e, mb=mb:
                          e.tensor_scalar_mul(mb, self.pb[4][:, 0:512], 1.0 / D),
                          wait=[m1, ms_sx0, ms_sx20], inc="dve")
            self.bank_read(4, ms3)
            ms4 = self.op("dve", lambda e, mb=mb, vr=vr: e.tensor_mul(vr, mb, mb),
                          wait=[ms3], inc="dve")
            ms5 = self.op("dve", lambda e, vr=vr:
                          e.scalar_tensor_tensor(vr, self.pb[5][:, 0:512], 1.0 / D, vr,
                                                 ALU.mult, ALU.subtract),
                          wait=[m2, ms4], inc="dve")
            self.bank_read(5, ms5)
            ms6 = self.op("act", lambda e, vr=vr:
                          e.activation(out=vr, in_=vr, func=AF.Sqrt, bias=EPS_LN),
                          wait=[ms5, self.ms_ones], inc="act")
            ms7 = self.op("dve", lambda e, vr=vr: e.reciprocal(vr, vr),
                          wait=[ms6], inc="dve")
            for d in range(8):
                xs = self.x[:, d * TLOC + g * 512: d * TLOC + (g + 1) * 512]
                ys = self.y[:, d * TLOC + g * 512: d * TLOC + (g + 1) * 512]
                tmp = self.fstage[:, 1024 + 512 * (d % 2): 1536 + 512 * (d % 2)]
                msa = self.op("dve", lambda e, xs=xs, tmp=tmp, mb=mb:
                              e.tensor_sub(tmp, xs, mb), wait=[ms3], inc="dve")
                self.op("dve", lambda e, ys=ys, tmp=tmp, vr=vr:
                        e.tensor_mul(ys, tmp, vr), wait=[ms7, msa], inc="dve")
        self.ms_y = self.ms("dve")

    def p_attn_a(self):
        w = self.w
        ms_y = self.ms_y
        self.dma("pool", self.wdq_bf.ap(), w["bWDQ"], "gm")
        self.dma("pool", self.wuq_bf.ap(), w["bWUQ"], "gm")
        self.dma("pool", self.wqr_bf.ap(), w["bWQR"], "gm")
        self.ms_wbf = self.ms("gm")
        msq = self.op("pool", lambda e: e.memset(self.big2[:], 1.0), inc="pool")
        ms_wv = []
        for k in range(8):
            ms_wv.append(self.dma("pool", self.warena[:, k * 1024:(k + 1) * 1024],
                                  w["aWv"][k * 128:(k + 1) * 128, :], "wg"))
        ms_w_last = None
        for ti in range(NT):
            for eh in range(2):
                bk = (ti * 2 + eh) % 4
                pbk = self.pb[bk]
                mmm = None
                for k in range(8):
                    lhsT = self.y[:, k * TLOC + ti * 128: k * TLOC + (ti + 1) * 128]
                    rhs = self.warena[:, k * 1024 + eh * 512: k * 1024 + (eh + 1) * 512]
                    mmm = self.mm(pbk[:, 0:512], lhsT, rhs, start=(k == 0), stop=(k == 7),
                                  wait=[ms_y, ms_wv[7], self.bfree(bk)], inc=(k == 7))
                self.bank_written(bk, mmm)
                ms_w_last = mmm
                msd = None
                for h8 in range(8):
                    h = eh * 8 + h8
                    dst = self.big2[:, ti * 1040 + h * 65: ti * 1040 + h * 65 + 64]
                    msd = self.op("dve", lambda e, dst=dst, pbk=pbk, h8=h8:
                                  e.tensor_copy(dst, pbk[:, h8 * 64:(h8 + 1) * 64]),
                                  wait=[mmm, msq] if h8 == 0 else [], inc="dve")
                self.bank_read(bk, msd)
        ms_v_done = self.ms("dve")
        if self.stage == "vtm":
            xo = self.xout.rearrange("(a p) d -> p a d", p=128)
            m2 = self.dma("pool", xo[:, 0:16, :],
                          self.big2[:, 0:16640].rearrange("p (a d) -> p a d", a=16, d=1040)[:, :, 0:1024],
                          "gm", wait=[ms_v_done])
            self._wait("pool", m2[0], m2[1])
            return
        ms_wk = []
        for k in range(8):
            ms_wk.append(self.dma("pool", self.warena[:, k * 1024:(k + 1) * 1024],
                                  w["aWk"][k * 128:(k + 1) * 128, :], "wg",
                                  wait=[ms_w_last]))
        for m_ in range(8):
            for g in range(NG):
                bk = (m_ * NG + g) % 4
                pbk = self.pb[bk]
                mmm = None
                for k in range(8):
                    lhsT = self.warena[:, k * 1024 + m_ * 128: k * 1024 + (m_ + 1) * 128]
                    rhs = self.y[:, k * TLOC + g * 512: k * TLOC + (g + 1) * 512]
                    mmm = self.mm(pbk[:, 0:512], lhsT, rhs, start=(k == 0), stop=(k == 7),
                                  wait=[ms_wk[7], self.bfree(bk)], inc=(k == 7))
                self.bank_written(bk, mmm)
                ms_w_last = mmm
                dst = self.big1[:, m_ * TLOC + g * 512: m_ * TLOC + (g + 1) * 512]
                msd = self.op("dve", lambda e, dst=dst, pbk=pbk:
                              e.tensor_copy(dst, pbk[:, 0:512]), wait=[mmm], inc="dve")
                self.bank_read(bk, msd)
        ms_k_done = self.ms("dve")
        if self.stage == "kfm":
            xo = self.xout.rearrange("(a p) d -> p a d", p=128)
            m2 = self.dma("pool", xo[:, 0:16, :],
                          self.big1[:, 0:16384].rearrange("p (a d) -> p a d", a=16),
                          "gm", wait=[ms_k_done])
            self._wait("pool", m2[0], m2[1])
            return
        for h in range(H):
            mkv = None
            for ti in range(NT):
                bk = ti % 4
                pbk = self.pb[bk]
                lhsT = self.big1[(h % 2) * 64:(h % 2) * 64 + 64,
                                 (h // 2) * TLOC + ti * 128:(h // 2) * TLOC + (ti + 1) * 128]
                mm1 = self.mm(pbk[:, 0:256], lhsT,
                              self.projb[(h % 2) * 64:(h % 2) * 64 + 64, :],
                              start=True, stop=True,
                              wait=[ms_k_done, self.ms_proj, self.bfree(bk)], inc=True)
                self.bank_written(bk, mm1)
                stag = self.stg[:, (ti % 2) * 256: (ti % 2) * 256 + 256]
                msr = self.op("dve", lambda e, stag=stag, pbk=pbk:
                              e.tensor_relu(stag, pbk[:, 0:256]), wait=[mm1], inc="dve")
                self.bank_read(bk, msr)
                for fc in range(2):
                    kvbk = 6 + fc
                    mkv = self.mm(self.pb[kvbk][:, 0:65],
                                  stag[:, fc * 128:(fc + 1) * 128],
                                  self.big2[:, ti * 1040 + h * 65: ti * 1040 + (h + 1) * 65],
                                  start=(ti == 0), stop=(ti == NT - 1),
                                  wait=([msr, ms_v_done, self.bfree(kvbk)] if ti == 0
                                        else [msr]),
                                  inc=(ti == NT - 1))
                    if ti == NT - 1:
                        self.bank_written(kvbk, mkv)
            for fc in range(2):
                kvbk = 6 + fc
                col = self.COL_KV + fc * 65
                msd = self.op("dve", lambda e, col=col, kvbk=kvbk:
                              e.tensor_copy(self.misc2[:, col:col + 65],
                                            self.pb[kvbk][:, 0:65]),
                              wait=[self.bready(kvbk)], inc="dve")
                self.bank_read(kvbk, msd)
                mda = self.dma("sp", self.acon_h[h * 256 + fc * 128: h * 256 + (fc + 1) * 128, :],
                               self.misc2[:, col:col + 65], "skv", wait=[msd])
                self._wait("dve", mda[0], mda[1])
        ms_acon = self.ms("skv")
        if self.stage == "acon":
            self._wait("pool", "skv", ms_acon[1])
            m2 = self.dma("pool", self.xout[0:2048, 0:65],
                          self.acon_h.ap()[0:2048, :], "gm", wait=[])
            self._wait("pool", m2[0], m2[1])
            return

        ms_wq = []
        for k in range(8):
            ms_wq.append(self.dma("pool", self.warena[:, k * 1024:(k + 1) * 1024],
                                  w["aWq"][k * 128:(k + 1) * 128, :], "wg",
                                  wait=[ms_w_last]))
        ms_k_pe = self.ms("pe")
        self.op("pool", lambda e: e.collective_compute(
            "AllGather", ALU.bypass, replica_groups=[[0, 1, 2, 3], [4, 5, 6, 7]],
            ins=[self.acon_h.ap().opt()], outs=[self.ag_h.ap().opt()]),
            wait=[ms_acon], inc="cc0")
        ms_cc0 = self.ms("cc0")
        self.p_quantize(qbufs=[(self.ropeb[:, 0:1024], self.kvb[:, 0:1024]),
                               (self.kvb[:, 1024:2048], self.stg[:, 0:1024])],
                        pe_guard=self.ms("pe"))
        self._wait("dve", "sscr", self.ms("sscr")[1])
        for m_ in range(8):
            for g in range(NG):
                bk = (m_ * NG + g) % 4
                pbk = self.pb[bk]
                mmm = None
                for k in range(8):
                    lhsT = self.warena[:, k * 1024 + m_ * 128: k * 1024 + (m_ + 1) * 128]
                    rhs = self.y[:, k * TLOC + g * 512: k * TLOC + (g + 1) * 512]
                    wt = [ms_wq[7], self.bfree(bk)]
                    if m_ == 0 and g == 0:
                        wt.append(ms_k_pe)
                    mmm = self.mm(pbk[:, 0:512], lhsT, rhs, start=(k == 0), stop=(k == 7),
                                  wait=wt, inc=(k == 7))
                self.bank_written(bk, mmm)
                ms_w_last = mmm
                dst = self.big1[:, m_ * TLOC + g * 512: m_ * TLOC + (g + 1) * 512]
                msd = self.op("dve", lambda e, dst=dst, pbk=pbk:
                              e.tensor_copy(dst, pbk[:, 0:512]), wait=[mmm], inc="dve")
                self.bank_read(bk, msd)
        ms_q_done = self.ms("dve")

        ag4 = self.ag_h.ap().rearrange("(r hf p) c -> r p hf c", r=4, p=128)
        ms_v_pe = self.ms("pe")
        msl = []
        for r in range(4):
            dst = self.big2[:, r * 2080:(r + 1) * 2080]
            msl.append(self.dma("pool", dst, ag4[r], "gm", wait=[ms_cc0, ms_v_pe]))
        msadd = self.op("dve", lambda e: e.tensor_add(self.kvb[:], self.big2[:, 0:2080],
                                                      self.big2[:, 2080:4160]),
                        wait=[msl[0], msl[1]], inc="dve")
        msadd = self.op("dve", lambda e: e.tensor_add(self.kvb[:], self.kvb[:],
                                                      self.big2[:, 4160:6240]),
                        wait=[msl[2]], inc="dve")
        msadd = self.op("dve", lambda e: e.tensor_add(self.kvb[:], self.kvb[:],
                                                      self.big2[:, 6240:8320]),
                        wait=[msl[3]], inc="dve")
        if self.stage == "kv":
            xo = self.xout.rearrange("(a p) d -> p a d", p=128)
            m2 = self.dma("pool", xo[:, 0:2, :], self.kvb[:, 0:2048], "gm", wait=[msadd])
            self._wait("pool", m2[0], m2[1])
            return

        for h in range(H):
            for g in range(NG):
                msr = None
                for fc in range(2):
                    bk = ((h * NG + g) % 2) * 2 + fc
                    pbk = self.pb[bk]
                    lhsT = self.projb[(h % 2) * 64:(h % 2) * 64 + 64,
                                      fc * 128:(fc + 1) * 128]
                    rhs = self.big1[(h % 2) * 64:(h % 2) * 64 + 64,
                                    (h // 2) * TLOC + g * 512:(h // 2) * TLOC + (g + 1) * 512]
                    mm1 = self.mm(pbk[:, 0:512], lhsT, rhs, start=True, stop=True,
                                  wait=[ms_q_done, self.bfree(bk)], inc=True)
                    self.bank_written(bk, mm1)
                    dst = self.stg[:, fc * 512:(fc + 1) * 512]
                    msr = self.op("dve", lambda e, dst=dst, pbk=pbk:
                                  e.tensor_relu(dst, pbk[:, 0:512]), wait=[mm1], inc="dve")
                    self.bank_read(bk, msr)
                mma = mmn = None
                for fc in range(2):
                    lk = self.kvb[:, (2 * h + fc) * 65:(2 * h + fc) * 65 + 64]
                    ln_ = self.kvb[:, (2 * h + fc) * 65 + 64:(2 * h + fc) * 65 + 65]
                    rhs = self.stg[:, fc * 512:(fc + 1) * 512]
                    wt = [msadd, msr]
                    if fc == 0:
                        wt.append(self.bfree(2))
                    mma = self.mm(self.pb[2][0:64, 0:512], lk, rhs,
                                  start=(fc == 0), stop=(fc == 1), wait=wt, inc=(fc == 1))
                    wt2 = [self.bfree(3)] if fc == 0 else []
                    mmn = self.mm(self.pb[3][0:1, 0:512], ln_, rhs,
                                  start=(fc == 0), stop=(fc == 1), wait=wt2, inc=(fc == 1))
                self.bank_written(2, mma)
                self.bank_written(3, mmn)
                row = self.misc2[0:1, 0:512]
                msn1 = self.op("dve", lambda e, row=row:
                               e.tensor_scalar_add(row, self.pb[3][0:1, 0:512], EPS_A),
                               wait=[mmn, self.ms("pe")], inc="dve")
                self.bank_read(3, msn1)
                msn2 = self.op("dve", lambda e, row=row: e.reciprocal(row, row),
                               wait=[msn1], inc="dve")
                mmb = self.mm(self.pb[3][0:64, 0:512], self.onesf[0:1, 0:64], row,
                              start=True, stop=True, wait=[msn2], inc=True)
                self.bank_written(3, mmb)
                msc = self.op("dve", lambda e:
                              e.tensor_copy(self.o2b[:], self.pb[2][0:64, 0:512]),
                              wait=[mma], inc="dve")
                self.bank_read(2, msc)
                ydst = self.y[(h % 2) * 64:(h % 2) * 64 + 64,
                              (h // 2) * TLOC + g * 512:(h // 2) * TLOC + (g + 1) * 512]
                msm = self.op("dve", lambda e, ydst=ydst:
                              e.tensor_mul(ydst, self.o2b[:], self.pb[3][0:64, 0:512]),
                              wait=[mmb, msc], inc="dve")
                self.bank_read(3, msm)
        ms_attn_out = self.ms("dve")
        if self.stage == "aout":
            self.p_store_y()
            return

        ms_wo = []
        for k in range(8):
            ms_wo.append(self.dma("pool", self.warena[:, k * 1024:(k + 1) * 1024],
                                  w["aWo"][k * 128:(k + 1) * 128, :], "wg",
                                  wait=[ms_w_last]))
        bo_r = self.w["abo"].rearrange("(a d) -> a d", a=1)
        msbo = self.dma("sp", self.fstage[0:1, 0:1024], bo_r, "srow",
                        wait=[self.ms("dve")])
        self._wait("pe", msbo[0], msbo[1])
        self.p_biascol_from_row(self.fstage, 8, self.COL_BO, 5)
        ms_bo = self.ms("dve")
        for m_ in range(8):
            self.op("dve", lambda e, m_=m_:
                    e.tensor_scalar_add(self.x[:, m_ * TLOC:(m_ + 1) * TLOC],
                                        self.x[:, m_ * TLOC:(m_ + 1) * TLOC],
                                        self.misc2[:, self.COL_BO + m_:self.COL_BO + m_ + 1]),
                    wait=[ms_bo], inc="dve")
        for m_ in range(8):
            for g in range(NG):
                bk = (m_ * NG + g) % 4
                pbk = self.pb[bk]
                mmm = None
                for k in range(8):
                    lhsT = self.warena[:, k * 1024 + m_ * 128: k * 1024 + (m_ + 1) * 128]
                    rhs = self.y[:, k * TLOC + g * 512: k * TLOC + (g + 1) * 512]
                    mmm = self.mm(pbk[:, 0:512], lhsT, rhs, start=(k == 0), stop=(k == 7),
                                  wait=[ms_wo[7], ms_attn_out, self.bfree(bk)], inc=(k == 7))
                self.bank_written(bk, mmm)
                xs = self.x[:, m_ * TLOC + g * 512: m_ * TLOC + (g + 1) * 512]
                msd = self.op("dve", lambda e, xs=xs, pbk=pbk:
                              e.tensor_add(xs, xs, pbk[:, 0:512]), wait=[mmm], inc="dve")
                self.bank_read(bk, msd)
        self.ms_warena_last = self.ms("pe")

    def p_biascol_from_row(self, row_t, ncols, dst_col0, bank):
        for m in range(ncols):
            pbk = self.pb[bank]
            msmm = self.mm(pbk[:, 0:1], row_t[0:1, m * 128:(m + 1) * 128],
                           self.onesf[0:1, 0:1], start=True, stop=True,
                           wait=[self.bfree(bank), self.ms_ones], inc=True)
            self.bank_written(bank, msmm)
            msd = self.op("dve", lambda e, m=m, c=dst_col0, pbk=pbk:
                          e.tensor_copy(self.misc2[:, c + m:c + m + 1], pbk[:, 0:1]),
                          wait=[msmm], inc="dve")
            self.bank_read(bank, msd)

    def p_quantize(self, qbufs=None, pe_guard=None):
        if qbufs is None:
            qbufs = [(self.stg[:, 0:1024], self.ptb[:, 0:1024]),
                     (self.big1[:, 0:1024], self.big1[:, 1024:2048])]
        for wi in range(2):
            wname = "fW1" if wi == 0 else "fW2"
            scr = self.wq1s if wi == 0 else self.wq2s
            rcol = self.COL_G if wi == 0 else self.COL_G + 1
            W = self.w[wname]
            rows = DFF if wi == 0 else D
            cols = D if wi == 0 else DFF
            nr, ncb = rows // 128, cols // 1024
            blocks = [(r, c) for r in range(nr) for c in range(ncb)]
            ms_fs_free = [self.ms("dve"), self.ms("dve")]
            msr = None
            for bi, (r, c) in enumerate(blocks):
                half = bi % 2
                fs = self.fstage[:, half * 1024:(half + 1) * 1024]
                msl = self.dma("sp", fs, W[r * 128:(r + 1) * 128, c * 1024:(c + 1) * 1024],
                               "sx" if half == 0 else "sx2",
                               wait=[ms_fs_free[half]] if ms_fs_free[half] else [])
                col = self.COL_AR + bi
                msr = self.op("dve", lambda e, fs=fs, col=col:
                              e.tensor_reduce(self.misc2[:, col:col + 1], fs, AX, ALU.add,
                                              apply_absolute_value=True),
                              wait=[msl], inc="dve")
                ms_fs_free[half] = msr
            mst = self.op("dve", lambda e:
                          e.tensor_reduce(self.misc2[:, self.COL_G + 2:self.COL_G + 3],
                                          self.misc2[:, self.COL_AR:self.COL_AR + 32],
                                          AX, ALU.add), wait=[msr], inc="dve")
            mmg = self.mm(self.pb[5][:, 0:1], self.ones128f[:],
                          self.misc2[:, self.COL_G + 2:self.COL_G + 3],
                          start=True, stop=True,
                          wait=[mst, self.bfree(5), self.ms_ones], inc=True)
            self.bank_written(5, mmg)
            msg1 = self.op("dve", lambda e, rcol=rcol:
                           e.tensor_scalar(self.misc2[:, rcol:rcol + 1], self.pb[5][:, 0:1],
                                           1.0 / (DFF * D), EPS_Q, ALU.mult, ALU.add),
                           wait=[mmg], inc="dve")
            self.bank_read(5, msg1)
            msg2 = self.op("dve", lambda e, rcol=rcol:
                           e.reciprocal(self.misc2[:, rcol:rcol + 1],
                                        self.misc2[:, rcol:rcol + 1]),
                           wait=[msg1], inc="dve")
            ms_fs_free = [self.ms("dve"), self.ms("dve")]
            ms_set_free = [None, None]
            for bi, (r, c) in enumerate(blocks):
                half = bi % 2
                fs = self.fstage[:, half * 1024:(half + 1) * 1024]
                msl = self.dma("sp", fs, W[r * 128:(r + 1) * 128, c * 1024:(c + 1) * 1024],
                               "sx" if half == 0 else "sx2",
                               wait=[ms_fs_free[half]] if ms_fs_free[half] else [])
                s1, s2 = qbufs[half]
                rc = self.misc2[:, rcol:rcol + 1]
                wt = [msl, msg2, self.ms_ones]
                if pe_guard is not None:
                    wt.append(pe_guard)
                if ms_set_free[half]:
                    wt.append(ms_set_free[half])
                ma = self.op("act", lambda e, fs=fs, s1=s1, rc=rc:
                             e.activation(out=s1, in_=fs, func=AF.Sign, scale=rc, bias=-0.5),
                             wait=wt, inc="act")
                mb = self.op("act", lambda e, fs=fs, s2=s2, rc=rc:
                             e.activation(out=s2, in_=fs, func=AF.Sign, scale=rc, bias=0.5),
                             wait=[], inc="act")
                ms_fs_free[half] = mb
                mc = self.op("dve", lambda e, s1=s1, s2=s2:
                             e.tensor_add(s1, s1, s2), wait=[ma, mb], inc="dve")
                if wi == 0:
                    dst = scr.ap()[r * 128:(r + 1) * 128, :]
                else:
                    dst = scr.ap()[r * 128:(r + 1) * 128, c * 1024:(c + 1) * 1024]
                msw = self.dma("sp", dst, s1, "sscr", wait=[mc])
                ms_set_free[half] = msw
        self.ms_wq_scr = self.ms("sscr")

    def p_ffn(self):
        ms_y = self.ms("dve")
        if not getattr(self, "_bias_built", False):
            self._bias_built = True
            b1r = self.w["fb1"].rearrange("(a d) -> a d", a=1)
            m1 = self.dma("sp", self.fstage[0:1, 0:2048], b1r[0:1, 0:2048], "srow",
                          wait=[self.ms("dve")])
            self._wait("pe", m1[0], m1[1])
            self.p_biascol_from_row(self.fstage, 16, self.COL_B1, 5)
            m2 = self.dma("sp", self.fstage[0:1, 0:2048], b1r[0:1, 2048:4096], "srow",
                          wait=[self.ms("pe")])
            self._wait("pe", m2[0], m2[1])
            self.p_biascol_from_row(self.fstage, 16, self.COL_B1 + 16, 5)
            b2r = self.w["fb2"].rearrange("(a d) -> a d", a=1)
            m3 = self.dma("sp", self.fstage[0:1, 0:1024], b2r, "srow",
                          wait=[self.ms("pe")])
            self._wait("pe", m3[0], m3[1])
            self.p_biascol_from_row(self.fstage, 8, self.COL_B2, 5)
        ms_bias = self.ms("dve")
        for m_ in range(8):
            self.op("dve", lambda e, m_=m_:
                    e.tensor_scalar_add(self.x[:, m_ * TLOC:(m_ + 1) * TLOC],
                                        self.x[:, m_ * TLOC:(m_ + 1) * TLOC],
                                        self.misc2[:, self.COL_B2 + m_:self.COL_B2 + m_ + 1]),
                    wait=[ms_bias], inc="dve")
        ms_b2 = self.ms("dve")

        ms_big2_pe = getattr(self, "ms_big2_free", None) or self.ms("pe")
        self.ms_big2_free = None
        self.ffn_g_ms = {}
        for g in range(NG):
            for fh in range(2):
                ms_w1 = []
                for d in range(8):
                    src = self.wq1s.ap()[fh * 2048:(fh + 1) * 2048, d * 128:(d + 1) * 128]
                    m = self.dmat("sp", self.big2[:, d * 2048:(d + 1) * 2048], src,
                                  "sw1%d" % d, wait=[self.ms_wq_scr, ms_big2_pe])
                    ms_w1.append(m)
                for fl in range(16):
                    f = fh * 16 + fl
                    bk = f % 4
                    pbk = self.pb[bk]
                    mmm = None
                    for d in range(8):
                        lhsT = self.big2[:, d * 2048 + fl * 128: d * 2048 + (fl + 1) * 128]
                        rhs = self.y[:, d * TLOC + g * 512: d * TLOC + (g + 1) * 512]
                        mmm = self.mm(pbk[:, 0:512], lhsT, rhs,
                                      start=(d == 0), stop=(d == 7),
                                      wait=[ms_w1[d], ms_y, self.bfree(bk)], inc=(d == 7))
                    self.bank_written(bk, mmm)
                    if fl == 15:
                        ms_big2_pe = mmm
                    dst = self.big1[:, f * 512:(f + 1) * 512]
                    b1c = self.misc2[:, self.COL_B1 + f:self.COL_B1 + f + 1]
                    msh = self.op("act", lambda e, dst=dst, pbk=pbk, b1c=b1c:
                                  e.activation(out=dst, in_=pbk[:, 0:512], func=AF.Relu,
                                               scale=0.5, bias=b1c),
                                  wait=[mmm], inc="act")
                    self.bank_read(bk, msh)
            ms_h = self.ms("act")
            ms_w2use = [self.ms("pe")] * 4
            w2sems = ["sw2a", "sw2b", "sw2c", "sw2d"]
            mlast = None
            for f in range(32):
                slot = f % 4
                wsl = self.warena[:, slot * 1024:(slot + 1) * 1024]
                wt = [self.ms_wq_scr]
                if ms_w2use[slot]:
                    wt.append(ms_w2use[slot])
                mw = self.dmat("sp", wsl, self.wq2s.ap()[:, f * 128:(f + 1) * 128],
                               w2sems[slot], wait=wt)
                for m_ in range(8):
                    wtm = [mw, ms_h]
                    if f == 0:
                        wtm.append(self.bfree(m_))
                    mlast = self.mm(self.pb[m_][:, 0:512],
                                    wsl[:, m_ * 128:(m_ + 1) * 128],
                                    self.big1[:, f * 512:(f + 1) * 512],
                                    start=(f == 0), stop=(f == 31),
                                    wait=wtm, inc=(m_ == 7))
                ms_w2use[slot] = mlast
            for m_ in range(8):
                self.bank_written(m_, mlast)
                xs = self.x[:, m_ * TLOC + g * 512: m_ * TLOC + (g + 1) * 512]
                msd = self.op("dve", lambda e, xs=xs, m_=m_:
                              e.scalar_tensor_tensor(xs, self.pb[m_][:, 0:512], 0.5, xs,
                                                     ALU.mult, ALU.add),
                              wait=[mlast, ms_b2], inc="dve")
                self.bank_read(m_, msd)
            self.ffn_g_ms[g] = self.ms("dve")

    def p_attn_b(self):
        w = self.w
        ms_y = self.ms("dve")
        ms_warena_pe = self.ms("pe")
        ms_wdkv = []
        for k in range(8):
            ms_wdkv.append(self.dma("pool", self.warena[:, k * 256:(k + 1) * 256],
                                    w["bWDKV"][k * 128:(k + 1) * 128, :], "wg",
                                    wait=[ms_warena_pe]))
        for m_ in range(2):
            for g in range(NG):
                bk = (m_ * NG + g) % 4
                pbk = self.pb[bk]
                mmm = None
                for k in range(8):
                    lhsT = self.warena[:, k * 256 + m_ * 128: k * 256 + (m_ + 1) * 128]
                    rhs = self.y[:, k * TLOC + g * 512: k * TLOC + (g + 1) * 512]
                    mmm = self.mm(pbk[:, 0:512], lhsT, rhs, start=(k == 0), stop=(k == 7),
                                  wait=[ms_wdkv[7], ms_y, self.bfree(bk)], inc=(k == 7))
                self.bank_written(bk, mmm)
                dst = self.warena[:, 2048 + m_ * 2048 + g * 512: 2048 + m_ * 2048 + (g + 1) * 512]
                msd = self.op("dve", lambda e, dst=dst, pbk=pbk:
                              e.tensor_copy(dst, pbk[:, 0:512]), wait=[mmm], inc="dve")
                self.bank_read(bk, msd)
        ms_ckv = self.ms("dve")

        ms_wuk = []
        for ck in range(2):
            ms_wuk.append(self.dma("pool", self.warena[:, 6144 + ck * 1024: 6144 + (ck + 1) * 1024],
                                   w["bWUK"][ck * 128:(ck + 1) * 128, :], "wg"))
        ms_kc_pe = None
        for m_ in range(8):
            for g in range(NG):
                bk = (m_ * NG + g) % 4
                pbk = self.pb[bk]
                mmm = None
                for ck in range(2):
                    lhsT = self.warena[:, 6144 + ck * 1024 + m_ * 128: 6144 + ck * 1024 + (m_ + 1) * 128]
                    rhs = self.warena[:, 2048 + ck * 2048 + g * 512: 2048 + ck * 2048 + (g + 1) * 512]
                    mmm = self.mm(pbk[:, 0:512], lhsT, rhs, start=(ck == 0), stop=(ck == 1),
                                  wait=[ms_wuk[1], ms_ckv, self.bfree(bk)], inc=(ck == 1))
                self.bank_written(bk, mmm)
                ms_kc_pe = mmm
                shalf = (m_ * NG + g) % 2
                sdst = self.stg[:, shalf * 512:(shalf + 1) * 512]
                msd = self.op("dve", lambda e, sdst=sdst, pbk=pbk:
                              e.tensor_copy(sdst, pbk[:, 0:512]), wait=[mmm], inc="dve")
                self.bank_read(bk, msd)
                mdk = None
                for hh in range(2):
                    h = 2 * m_ + hh
                    src = sdst[hh * 64:(hh + 1) * 64, :].rearrange("p (l t) -> p l t", l=4)
                    off = (ROW_K + (h * 16 + g * 4) * 64) * 128
                    dst = bass.AP(self.kvcon_h, off,
                                  [[128, 64], [64 * 128, 4], [1, 128]])
                    mdk = self.dma("sp", dst, src, "sstg", wait=[msd])
                self._wait("dve", mdk[0], mdk[1])

        msvset = self.op("pool", lambda e: e.memset(self.vstg[:], 1.0), inc="pool",
                         wait=[ms_warena_pe])
        ms_wuv = []
        for ck in range(2):
            ms_wuv.append(self.dma("pool", self.warena[:, 6144 + ck * 1024: 6144 + (ck + 1) * 1024],
                                   w["bWUV"][ck * 128:(ck + 1) * 128, :], "wg",
                                   wait=[ms_kc_pe]))
        ms_vdma = [None, None]
        for lt in range(NT):
            vhalf = lt % 2
            vsl = self.vstg[:, vhalf * 1040:(vhalf + 1) * 1040]
            for eh in range(2):
                bk = (lt % 2) * 2 + eh
                pbk = self.pb[bk]
                mmm = None
                for ck in range(2):
                    lhsT = self.warena[:, 2048 + ck * 2048 + lt * 128: 2048 + ck * 2048 + (lt + 1) * 128]
                    rhs = self.warena[:, 6144 + ck * 1024 + eh * 512: 6144 + ck * 1024 + (eh + 1) * 512]
                    mmm = self.mm(pbk[:, 0:512], lhsT, rhs, start=(ck == 0), stop=(ck == 1),
                                  wait=[ms_wuv[1], self.bfree(bk)], inc=(ck == 1))
                self.bank_written(bk, mmm)
                msd = None
                for h8 in range(8):
                    h = eh * 8 + h8
                    wt = [mmm, msvset]
                    if h8 == 0 and ms_vdma[vhalf]:
                        wt.append(ms_vdma[vhalf])
                    msd = self.op("dve", lambda e, vsl=vsl, h=h, pbk=pbk, h8=h8:
                                  e.tensor_copy(vsl[:, h * 65: h * 65 + 64],
                                                pbk[:, h8 * 64:(h8 + 1) * 64]),
                                  wait=wt, inc="dve")
                self.bank_read(bk, msd)
            msv = None
            for h in range(H):
                off = (ROW_V + (h * 16 + lt) * 65) * 128
                dst = bass.AP(self.kvcon_h, off, [[65, 128], [1, 65]])
                msv = self.dma("sp", dst, vsl[:, h * 65:(h + 1) * 65], "svst", wait=[msd])
            ms_vdma[vhalf] = msv

        ms_wkr = self.dma("pool", self.warena[:, 0:256],
                          w["bWKR"].rearrange("(k p) m -> p k m", p=128), "wg",
                          wait=[ms_kc_pe])
        for g in range(NG):
            bk = g % 2
            pbk = self.pb[bk]
            mmm = None
            for k in range(8):
                lhsT = self.warena[:, k * 32:(k + 1) * 32]
                rhs = self.y[:, k * TLOC + g * 512: k * TLOC + (g + 1) * 512]
                mmm = self.mm(pbk[0:32, 0:512], lhsT, rhs, start=(k == 0), stop=(k == 7),
                              wait=[ms_wkr, self.bfree(bk)], inc=(k == 7))
            self.bank_written(bk, mmm)
            mrl = self.dma("act", self.ropeb[0:32, 0:512],
                           self.ropei[0:32, g * 512:(g + 1) * 512], "sr0",
                           wait=[self.ms("dve")])
            msd = self.op("dve", lambda e, pbk=pbk:
                          e.tensor_mul(self.stg[0:32, 0:512], pbk[0:32, 0:512],
                                       self.ropeb[0:32, 0:512]),
                          wait=[mmm, mrl], inc="dve")
            self.bank_read(bk, msd)
            src = self.stg[0:32, 0:512].rearrange("p (l t) -> p l t", l=4)
            off = (ROW_KR + g * 4 * 32) * 128
            dst = bass.AP(self.kvcon_h, off, [[128, 32], [32 * 128, 4], [1, 128]])
            msk = self.dma("sp", dst, src, "sstg", wait=[msd])
            self._wait("dve", msk[0], msk[1])

        ms_kvcon = self.ms("sstg")
        ms_kvcon2 = self.ms("svst")
        self.op("pool", lambda e: e.collective_compute(
            "AllGather", ALU.bypass, replica_groups=[[0, 1, 2, 3], [4, 5, 6, 7]],
            ins=[self.kvcon_h.ap().opt()], outs=[self.kvg_h.ap().opt()]),
            wait=[ms_kvcon, ms_kvcon2], inc="cc1")
        ms_cc1 = self.ms("cc1")

        ms_big1_pe = self.ms("pe")
        for g in range(NG):
            ms_wslot = [self.ms("pe")] * 4
            for mc in range(32):
                slot = mc % 4
                wsl = self.warena[:, slot * 1024:(slot + 1) * 1024]
                src = self.wdq_bf.ap()[:, mc * 128:(mc + 1) * 128].rearrange(
                    "(k p) m -> p k m", p=128)
                wt = [ms_wslot[slot], self.ms_wbf]
                mw = self.dma("act", wsl, src, "wp%d" % slot, wait=wt)
                bk = mc % 4
                pbk = self.pb[bk]
                mmm = None
                for k in range(8):
                    lhsT = wsl[:, k * 128:(k + 1) * 128]
                    rhs = self.y[:, k * TLOC + g * 512: k * TLOC + (g + 1) * 512]
                    wtm = [mw, self.bfree(bk)]
                    if g == 0 and mc == 0:
                        wtm.append(ms_big1_pe)
                    mmm = self.mm(pbk[:, 0:512], lhsT, rhs, start=(k == 0), stop=(k == 7),
                                  wait=wtm, inc=(k == 7))
                self.bank_written(bk, mmm)
                ms_wslot[slot] = mmm
                dst = self.big1[:, mc * 512:(mc + 1) * 512]
                msd = self.op("dve", lambda e, dst=dst, pbk=pbk:
                              e.tensor_copy(dst, pbk[:, 0:512]), wait=[mmm], inc="dve")
                self.bank_read(bk, msd)
            ms_cq = self.ms("dve")
            ms_whalf = [None, None]
            for m_ in range(8):
                half = m_ % 2
                wsl = self.warena[:, half * 4096:(half + 1) * 4096]
                src = self.wuq_bf.ap()[:, m_ * 128:(m_ + 1) * 128].rearrange(
                    "(k p) m -> p k m", p=128)
                wt = [ms_wslot[2 * half], ms_wslot[2 * half + 1]] if m_ < 2 else []
                if ms_whalf[half]:
                    wt = [ms_whalf[half]]
                mw = self.dma("act", wsl, src, "wp%d" % half, wait=wt)
                bk = m_ % 4
                pbk = self.pb[bk]
                mmm = None
                for k in range(32):
                    lhsT = wsl[:, k * 128:(k + 1) * 128]
                    rhs = self.big1[:, k * 512:(k + 1) * 512]
                    mmm = self.mm(pbk[:, 0:512], lhsT, rhs, start=(k == 0), stop=(k == 31),
                                  wait=[mw, ms_cq, self.bfree(bk)], inc=(k == 31))
                self.bank_written(bk, mmm)
                ms_whalf[half] = mmm
                shalf = m_ % 2
                sdst = self.stg[:, shalf * 512:(shalf + 1) * 512]
                msd = self.op("dve", lambda e, sdst=sdst, pbk=pbk:
                              e.tensor_copy(sdst, pbk[:, 0:512]), wait=[mmm], inc="dve")
                self.bank_read(bk, msd)
                for hh in range(2):
                    h = 2 * m_ + hh
                    src2 = sdst[hh * 64:(hh + 1) * 64, :].rearrange("p (l t) -> p l t", l=4)
                    off = ((h * 16 + g * 4) * 96) * 128
                    dst = bass.AP(self.qcon_h, off,
                                  [[128, 64], [96 * 128, 4], [1, 128]])
                    msq = self.dma("sp", dst, src2, "sstg", wait=[msd])
                self._wait("dve", msq[0], msq[1])
            for mr in range(4):
                half = mr % 2
                wsl = self.warena[:, half * 4096:(half + 1) * 4096]
                src = self.wqr_bf.ap()[:, mr * 128:(mr + 1) * 128].rearrange(
                    "(k p) m -> p k m", p=128)
                mw = self.dma("act", wsl, src, "wp%d" % half,
                              wait=[ms_whalf[half]] if ms_whalf[half] else [])
                bk = mr % 4
                pbk = self.pb[bk]
                mmm = None
                for k in range(32):
                    lhsT = wsl[:, k * 128:(k + 1) * 128]
                    rhs = self.big1[:, k * 512:(k + 1) * 512]
                    mmm = self.mm(pbk[:, 0:512], lhsT, rhs, start=(k == 0), stop=(k == 31),
                                  wait=[mw, self.bfree(bk)], inc=(k == 31))
                self.bank_written(bk, mmm)
                ms_whalf[half] = mmm
                rhalf = mr % 2
                rsl = self.ropeb[:, rhalf * 512:(rhalf + 1) * 512]
                mrl = self.dma("act", rsl, self.ropei[mr * 128:(mr + 1) * 128,
                                                     g * 512:(g + 1) * 512],
                               "sr%d" % rhalf, wait=[self.ms("dve")])
                shalf = mr % 2
                sdst = self.stg[:, shalf * 512:(shalf + 1) * 512]
                msd = self.op("dve", lambda e, sdst=sdst, pbk=pbk, rsl=rsl:
                              e.tensor_mul(sdst, pbk[:, 0:512], rsl),
                              wait=[mmm, mrl], inc="dve")
                self.bank_read(bk, msd)
                msq = None
                for hh in range(4):
                    h = 4 * mr + hh
                    src2 = sdst[hh * 32:(hh + 1) * 32, :].rearrange("p (l t) -> p l t", l=4)
                    off = ((h * 16 + g * 4) * 96 + 64) * 128
                    dst = bass.AP(self.qcon_h, off,
                                  [[128, 32], [96 * 128, 4], [1, 128]])
                    msq = self.dma("sp", dst, src2, "sstg", wait=[msd])
                self._wait("dve", msq[0], msq[1])

        ms_qcon = self.ms("sstg")
        self.op("pool", lambda e: e.collective_compute(
            "AllGather", ALU.bypass, replica_groups=[[0, 1, 2, 3], [4, 5, 6, 7]],
            ins=[self.qcon_h.ap().opt()], outs=[self.qg2_h.ap().opt()]),
            wait=[ms_qcon], inc="cc3")
        ms_cc3 = self.ms("cc3")

        xgt = self.kvg_h.ap()
        ms_big2_pe = self.ms("pe")
        for r in range(4):
            reg = xgt[r * 33536 + ROW_KR: r * 33536 + ROW_KR + 512, :]
            src = reg.rearrange("(l f) t -> f l t", l=16)
            self.dma("sp", self.big2[64:96, r * 2048:(r + 1) * 2048], src, "satt",
                     wait=[ms_cc1, ms_big2_pe])
        msw_pe = self.ms("pe")
        for k in range(8):
            mwo = self.dma("pool", self.warena[:, k * 1024:(k + 1) * 1024],
                           self.w["bWO"][k * 128:(k + 1) * 128, :], "wg",
                           wait=[msw_pe])
        self.ms_wo_loaded = mwo
        ms_o_dma = None
        RSTRIDE_KV = 33536 * 128
        RSTRIDE_Q = 24576 * 128
        for hl in range(4):
            q_eng = "sp" if hl % 2 == 0 else "act"
            q_sem = "satt" if hl % 2 == 0 else "aatt"
            ms_prev_pe = self.ms("pe")
            ms_ldq = []

            def mk_d2d(kind, hl=hl, eng=q_eng):
                def fn(e, kind=kind, hl=hl, eng=eng):
                    hsel = self.cg[eng] * 4 + hl
                    if kind == "k":
                        off = ROW_K * 128 + hsel * (16 * 64 * 128)
                        dims = [[RSTRIDE_KV, 4], [128, 1024], [1, 128]]
                        return e.dma_start(out=self.kscr_h.ap(),
                                           in_=bass.AP(self.kvg_h, off, dims))
                    elif kind == "v":
                        off = ROW_V * 128 + hsel * (16 * 65 * 128)
                        dims = [[RSTRIDE_KV, 4], [128, 1040], [1, 128]]
                        return e.dma_start(out=self.vscr_h.ap(),
                                           in_=bass.AP(self.kvg_h, off, dims))
                    else:
                        off = hsel * (16 * 96 * 128)
                        dims = [[RSTRIDE_Q, 4], [128, 1536], [1, 128]]
                        return e.dma_start(out=self.qscr_h.ap(),
                                           in_=bass.AP(self.qg2_h, off, dims))
                return fn
            for kind in ("k", "v"):
                ms_ldq.append(self.op(q_eng, mk_d2d(kind),
                                      wait=[ms_cc1, ms_prev_pe], inc=q_sem, n=16))
            self._wait(q_eng, q_sem, ms_ldq[-1][1])
            for r in range(4):
                ksrc = self.kscr_h.ap()[r * 1024:(r + 1) * 1024, :].rearrange(
                    "(l f) t -> f l t", l=16)
                ms_ldq.append(self.dma(q_eng, self.big2[0:64, r * 2048:(r + 1) * 2048],
                                       ksrc, q_sem))
                vsrc = self.vscr_h.ap()[r * 1040:(r + 1) * 1040, :].rearrange(
                    "(l r2) c -> l (r2 c)", l=16).rearrange("l (t c) -> t l c", t=128, c=65)
                ms_ldq.append(self.dma(q_eng, self.big2[:, 8192 + r * 1040: 8192 + (r + 1) * 1040].rearrange("p (l c) -> p l c", l=16),
                                       vsrc, q_sem))
            ms_ld_kv = ms_ldq[-1]
            ms_ldq.append(self.op(q_eng, mk_d2d("q"),
                                  wait=[ms_cc3, ms_prev_pe], inc=q_sem, n=16))
            self._wait(q_eng, q_sem, ms_ldq[-1][1])
            for r in range(4):
                qsrc = self.qscr_h.ap()[r * 1536:(r + 1) * 1536, :].rearrange(
                    "(l f) t -> f l t", l=16)
                ms_ldq.append(self.dma(q_eng, self.big1[0:96, r * 2048:(r + 1) * 2048],
                                       qsrc, q_sem))
            ms_ld = ms_ldq[-1]
            ms_ld = ms_ldq[-1]
            kmax = self.misc2[0:1, 600:601]
            qmax = self.misc2[0:1, 601:602]
            tmp1 = self.misc2[0:1, 602:603]
            ksq = self.big1[0:96, 8192:16384]
            msqk = self.op("act", lambda e, ksq=ksq:
                           e.activation(out=ksq, in_=self.big2[0:96, 0:8192],
                                        func=AF.Square),
                           wait=[ms_ld_kv], inc="act")
            ms_nmax = None
            for src_i in range(2):
                msq_ = msqk
                if src_i == 1:
                    msq_ = self.op("act", lambda e, ksq=ksq:
                                   e.activation(out=ksq, in_=self.big1[0:96, 0:8192],
                                                func=AF.Square),
                                   wait=[ms_ld, self.ms("pe")], inc="act")
                dstmax = kmax if src_i == 0 else qmax
                for j16 in range(16):
                    mmx = self.mm(self.pb[0][0:1, 0:512], self.onesb[0:96, 0:1],
                                  ksq[:, j16 * 512:(j16 + 1) * 512], start=True, stop=True,
                                  wait=[msq_, self.bfree(0)], inc=True)
                    self.bank_written(0, mmx)
                    msr = self.op("dve", lambda e, tmp1=tmp1:
                                  e.reduce_max(tmp1, self.pb[0][0:1, 0:512], AX),
                                  wait=[mmx, ms_prev_pe, self.ms("act")], inc="dve")
                    self.bank_read(0, msr)
                    if j16 == 0:
                        ms_nmax = self.op("dve", lambda e, dstmax=dstmax, tmp1=tmp1:
                                          e.tensor_copy(dstmax, tmp1), wait=[msr], inc="dve")
                    else:
                        ms_nmax = self.op("dve", lambda e, dstmax=dstmax, tmp1=tmp1:
                                          e.tensor_max(dstmax, dstmax, tmp1),
                                          wait=[msr], inc="dve")
            msc1 = self.op("dve", lambda e, kmax=kmax, qmax=qmax, tmp1=tmp1:
                           e.tensor_mul(tmp1, kmax, qmax), wait=[ms_nmax], inc="dve")
            msc2 = self.op("act", lambda e, tmp1=tmp1:
                           e.activation(out=tmp1, in_=tmp1, func=AF.Sqrt),
                           wait=[msc1], inc="act")
            msc3 = self.op("dve", lambda e, tmp1=tmp1:
                           e.tensor_scalar_mul(tmp1, tmp1, -ISQ), wait=[msc2], inc="dve")
            mmb = self.mm(self.pb[0][:, 0:1], self.onesf[0:1, 0:128], tmp1,
                          start=True, stop=True, wait=[msc3, self.bfree(0)], inc=True)
            self.bank_written(0, mmb)
            ncc = self.misc2[:, self.COL_NC:self.COL_NC + 1]
            msnc = self.op("dve", lambda e, ncc=ncc:
                           e.tensor_copy(ncc, self.pb[0][:, 0:1]),
                           wait=[mmb, self.ms("act")], inc="dve")
            self.bank_read(0, msnc)

            pairs = []
            for qg in range(16):
                for jb in range(0, 4 * qg + 4, 2):
                    pairs.append((qg, jb))
            LOOK = 2
            pair_ms = {}

            def emit_spair(p, hl=hl, ms_ld=ms_ld, msnc=msnc):
                qg, jb = pairs[p]
                grp = p % 3
                msS = None
                for jj in range(2):
                    j = jb + jj
                    i = j - 4 * qg
                    if i < 0:
                        c0, wdt = qg * 512, 512
                    else:
                        c0, wdt = qg * 512 + i * 128, (4 - i) * 128
                    bkc = (1 + grp) * 1024 + jj * 512
                    wtS = [ms_ld]
                    if jj == 0:
                        wtS.append(self.bfree(2 + 2 * grp))
                    msS = self.mm(self.pball[:, bkc:bkc + wdt],
                                  self.big2[0:96, j * 128:(j + 1) * 128],
                                  self.big1[0:96, c0:c0 + wdt],
                                  start=True, stop=True, wait=wtS, inc=(jj == 1))
                self.bank_written(2 + 2 * grp, msS)
                self.bank_written(3 + 2 * grp, msS)
                msE = self.op("act", lambda e, grp=grp:
                              e.activation(out=self.ptb[:, grp * 1024:(grp + 1) * 1024],
                                           in_=self.pball[:, (1 + grp) * 1024:(2 + grp) * 1024],
                                           func=AF.Exp, scale=ISQ,
                                           bias=self.misc2[:, self.COL_NC:self.COL_NC + 1]),
                              wait=[msS, msnc], inc="act")
                self.bank_read(2 + 2 * grp, msE)
                self.bank_read(3 + 2 * grp, msE)
                msM = None
                for jj in range(2):
                    j = jb + jj
                    i = j - 4 * qg
                    if i >= 0:
                        dsl = self.ptb[:, grp * 1024 + jj * 512: grp * 1024 + jj * 512 + 128]
                        msM = self.op("dve", lambda e, dsl=dsl:
                                      e.tensor_mul(dsl, dsl, self.causal[:]),
                                      wait=[msE], inc="dve")
                pair_ms[p] = (msE, msM)

            def emit_pv(p, hl=hl):
                qg, jb = pairs[p]
                grp = p % 3
                nj = 4 * qg + 4
                msE, msM = pair_ms.pop(p)
                mlast = None
                for jj in range(2):
                    j = jb + jj
                    i = j - 4 * qg
                    wdt = 512 if i < 0 else (4 - i) * 128
                    wtp = [msE] if msM is None else [msM]
                    if j == 0:
                        wtp.append(self.bfree(0))
                    mlast = self.mm(self.pb[0][0:65, (i * 128 if i > 0 else 0):
                                               (i * 128 if i > 0 else 0) + wdt],
                                    self.big2[:, 8192 + j * 65: 8192 + (j + 1) * 65],
                                    self.ptb[:, grp * 1024 + jj * 512: grp * 1024 + jj * 512 + wdt],
                                    start=(j == 0), stop=(j == nj - 1),
                                    wait=wtp, inc=(j == nj - 1))
                if jb + 2 >= nj:
                    self.bank_written(0, mlast)
                    self.finalize_qg(qg, hl, mlast)

            for p in range(len(pairs)):
                emit_spair(p)
                if p >= LOOK:
                    emit_pv(p - LOOK)
            for p in range(len(pairs) - LOOK, len(pairs)):
                emit_pv(p)
        self.ms_big2_free = self.ms("pe")
        self.op("pool", lambda e: e.collective_compute(
            "AllGather", ALU.bypass, replica_groups=[[0, 1, 2, 3], [4, 5, 6, 7]],
            ins=[self.ocon_h.ap().opt()], outs=[self.og_h.ap().opt()]),
            wait=[self.ms_o_dma], inc="cc2")
        ms_cc2 = self.ms("cc2")
        ms_wo = self.ms_wo_loaded
        ms_big1_pe2 = self.ms("pe")
        ms_og = []
        for hl in range(4):
            def fno(e, hl=hl):
                off = (hl * 64) * 8192 + self.cg["pool"] * (16 * 8192)
                dims = [[256 * 8192, 4], [8192, 16], [1, 8192]]
                return e.dma_start(out=self.oscr_h.ap()[hl * 64:(hl + 1) * 64, :]
                                   .rearrange("(p l) c -> p l c", p=4),
                                   in_=bass.AP(self.og_h, off, dims))
            ms_og.append(self.op("pool", fno, wait=[ms_cc2, ms_big1_pe2], inc="gm", n=16))
        self._wait("pool", "gm", ms_og[-1][1])
        for p in range(4):
            for hl in range(4):
                h = 4 * p + hl
                src = self.oscr_h.ap()[hl * 64 + p * 16: hl * 64 + (p + 1) * 16, :]
                src = src.rearrange("l (f t) -> f l t", f=64)
                dst = self.y[(h % 2) * 64:(h % 2) * 64 + 64,
                             (h // 2) * TLOC:(h // 2 + 1) * TLOC]
                ms_og.append(self.dma("pool", dst, src, "gm"))
        ms_og_done = ms_og[-1]
        for m_ in range(8):
            for g in range(NG):
                bk = (m_ * NG + g) % 4
                pbk = self.pb[bk]
                mmm = None
                for k in range(8):
                    lhsT = self.warena[:, k * 1024 + m_ * 128: k * 1024 + (m_ + 1) * 128]
                    rhs = self.y[:, k * TLOC + g * 512: k * TLOC + (g + 1) * 512]
                    mmm = self.mm(pbk[:, 0:512], lhsT, rhs, start=(k == 0), stop=(k == 7),
                                  wait=[ms_wo, ms_og_done, self.bfree(bk)], inc=(k == 7))
                self.bank_written(bk, mmm)
                xs = self.x[:, m_ * TLOC + g * 512: m_ * TLOC + (g + 1) * 512]
                msd = self.op("dve", lambda e, xs=xs, pbk=pbk:
                              e.tensor_add(xs, xs, pbk[:, 0:512]), wait=[mmm], inc="dve")
                self.bank_read(bk, msd)



    def finalize_qg(self, qg, hl, mlast):
        row = self.misc2[0:1, 0:512]
        msn1 = self.op("dve", lambda e, row=row:
                       e.reciprocal(row, self.pb[0][64:65, 0:512]),
                       wait=[mlast, self.ms("pe")], inc="dve")
        mmb2 = self.mm(self.pb[1][0:64, 0:512], self.onesf[0:1, 0:64], row,
                       start=True, stop=True, wait=[msn1, self.bfree(1)], inc=True)
        self.bank_written(1, mmb2)
        msc = self.op("dve", lambda e:
                      e.tensor_copy(self.o2b[:], self.pb[0][0:64, 0:512]),
                      wait=[mlast], inc="dve")
        self.bank_read(0, msc)
        mso = self.op("dve", lambda e:
                      e.tensor_mul(self.stg[0:64, 0:512], self.o2b[:],
                                   self.pb[1][0:64, 0:512]),
                      wait=[mmb2, msc], inc="dve")
        self.bank_read(1, mso)
        src = self.stg[0:64, 0:512].rearrange("p (l t) -> p l t", l=4)
        off = (hl * 64 + 4 * qg) * 8192
        dst = bass.AP(self.ocon_h, off, [[128, 64], [8192, 4], [1, 128]])
        self.ms_o_dma = self.dma("sp", dst, src, "sstg", wait=[mso])
        self._wait("dve", self.ms_o_dma[0], self.ms_o_dma[1])

    def _mk_hsel_dma(self, view, hl, dst, eng):
        def fn(e, view=view, hl=hl, dst=dst, eng=eng):
            hsel = self.cg[eng] * 4 + hl
            return e.dma_start(out=dst, in_=view[hsel, :, :, :])
        return fn

    def _mk_cg_dma(self, view, dst):
        def fn(e, view=view, dst=dst):
            return e.dma_start(out=dst, in_=view[self.cg["pool"], :, :, :])
        return fn

    def p_store_y(self):
        self._store_stream(self.y, cast=True)

    def p_store(self):
        self._store_stream(self.x, cast=False)

    def _store_stream(self, src, cast):
        ms_x = self.ms("dve")
        g_ms = getattr(self, "ffn_g_ms", None)
        x3 = self.xout.rearrange("(t p) d -> t p d", p=128)
        last = None
        ms_half_free = [None, None]
        for ti in range(NT):
            half = ti % 2
            msd = None
            for d in range(8):
                bk = 4 + (d % 4)
                ssl = src[:, d * TLOC + ti * 128: d * TLOC + (ti + 1) * 128]
                if cast:
                    tmp = self.misc2[:, (d % 2) * 128:(d % 2) * 128 + 128]
                    msc = self.op("dve", lambda e, tmp=tmp, ssl=ssl:
                                  e.tensor_copy(tmp, ssl),
                                  wait=[ms_x, self.bfree(bk)], inc="dve")
                    tsrc = tmp
                    twait = [msc]
                else:
                    tsrc = ssl
                    mg = g_ms.get(ti // 4) if (g_ms and not cast) else None
                    twait = [mg or ms_x, self.bfree(bk)]
                msmm = self.op("pe", lambda e, tsrc=tsrc, bk=bk:
                               e.transpose(self.pb[bk][:, 0:128], tsrc, self.ident[:]),
                               wait=twait, inc="pe")
                self.bank_written(bk, msmm)
                dst = self.fstage[:, half * 1024 + d * 128: half * 1024 + (d + 1) * 128]
                wt = [msmm]
                if d == 0 and ms_half_free[half]:
                    wt.append(ms_half_free[half])
                msd = self.op("dve", lambda e, dst=dst, bk=bk:
                              e.tensor_copy(dst, self.pb[bk][:, 0:128]),
                              wait=wt, inc="dve")
                self.bank_read(bk, msd)
            last = self.dma("sp", x3[ti, :, :],
                            self.fstage[:, half * 1024:(half + 1) * 1024], "sxo",
                            wait=[msd])
            ms_half_free[half] = last
        self._wait("sp", last[0], last[1])


_CACHE = {}


def _build(stage=None):
    key = stage
    if key not in _CACHE:
        kb = KB(stage)
        nc = kb.build()
        _CACHE[key] = nc
    return _CACHE[key]


def _consts(cg):
    ident = np.eye(128, dtype=np.float32)
    pq = np.arange(128)
    causal = (pq[:, None] <= pq[None, :]).astype(ml_dtypes.bfloat16)
    inv = (1.0 / (np.float32(10000.0) **
                  (np.arange(0, DHR, 2, dtype=np.float32) / np.float32(DHR)))).astype(np.float32)
    t = np.arange(S, dtype=np.float32)
    freqs = (t[:, None] * inv[None, :]).astype(np.float32)
    emb = np.concatenate([freqs, freqs], axis=-1)
    tab = (np.cos(emb) + np.sin(emb)).astype(np.float32)
    sl = tab[2048 * cg: 2048 * (cg + 1), :].T
    rope = np.tile(sl, (H, 1)).astype(ml_dtypes.bfloat16)
    return ident, np.ascontiguousarray(causal), np.ascontiguousarray(rope)


def kernel(**inputs):
    stage = inputs.pop("_stage", None)
    nc = _build(stage)
    x = np.asarray(inputs["L0_attn_a_hidden_states"])
    wmap = {
        "aWq": "L0_attn_a_Wq", "aWk": "L0_attn_a_Wk", "aWv": "L0_attn_a_Wv",
        "aproj": "L0_attn_a_proj", "aWo": "L0_attn_a_Wo", "abo": "L0_attn_a_bo",
        "bWDQ": "L0_attn_b_W_DQ", "bWUQ": "L0_attn_b_W_UQ", "bWDKV": "L0_attn_b_W_DKV",
        "bWUK": "L0_attn_b_W_UK", "bWUV": "L0_attn_b_W_UV", "bWQR": "L0_attn_b_W_QR",
        "bWKR": "L0_attn_b_W_KR", "bWO": "L0_attn_b_W_O",
        "fW1": "L0_ffn_W1", "fb1": "L0_ffn_b1", "fW2": "L0_ffn_W2", "fb2": "L0_ffn_b2",
    }
    in_maps = []
    for c in range(8):
        b, cg = c // 4, c % 4
        m = {"xin": np.ascontiguousarray(x[b, 2048 * cg:2048 * (cg + 1), :])}
        for k, src in wmap.items():
            m[k] = np.ascontiguousarray(np.asarray(inputs[src]))
        ident, causal, rope = _consts(cg)
        m["identi"] = ident
        m["causali"] = causal
        m["ropei"] = rope
        in_maps.append(m)
    res = run_bass_kernel_spmd(nc, in_maps, core_ids=list(range(8)))
    out = np.empty((B, S, D), dtype=np.float32)
    for c in range(8):
        b, cg = c // 4, c % 4
        out[b, 2048 * cg:2048 * (cg + 1), :] = res.results[c]["xout"]
    return out
